# version 12
# baseline (speedup 1.0000x reference)
import numpy as np

import concourse.bass as bass
import concourse.mybir as mybir
from concourse.bass_utils import run_bass_kernel_spmd

F32 = mybir.dt.float32
F32R = mybir.dt.float32r
BF16 = mybir.dt.bfloat16
AF = mybir.ActivationFunctionType
OP = mybir.AluOpType
AX = mybir.AxisListType

D = 1024
DB = 8
NST = 16
T = 2048
BC = 2
NCORE = 8
G = 128
NH = 16
HS = 512
HD = 64
JT = T // 128
NJG = T // 512
CA = 512
NCA = T // CA
CB = 256
NCB = T // CB
TH = T // 2
CC = 512
NCC = T // CC
EPS = 1e-5
DEBUG_TAPS = False
PHASE_LIMIT = 99

ENGS = ("pe", "act", "dve", "gps", "sy")
PROG = {"pe": "pe", "act": "act", "dve": "dve", "gps": "gps", "sy": "spr"}
SEMS = ["pe", "act", "dve", "gps", "spr", "qsy", "qsc", "qgp"]
ENG_OF_SEM = {"pe": "pe", "act": "act", "dve": "dve", "gps": "gps",
              "spr": "sy", "qsy": "sy", "qsc": "act", "qgp": "gps"}


class Bld:

    def __init__(self):
        self.ops = {k: [] for k in ENGS}
        self.cnt = {s: 0 for s in SEMS}
        self.sems = {}
        self.slotsems = {}
        self.sem_owner = {"qsy": "sy", "qsc": "act", "qgp": "gps"}

    def slotsem(self, slot, eng):
        if slot not in self.slotsems:
            s = f"sl{len(self.slotsems)}"
            self.slotsems[slot] = s
            self.cnt[s] = 0
            self.sem_owner[s] = eng
        return self.slotsems[slot]

    def w(self, eng, tok):
        if tok is None:
            return
        s, v = tok
        if v <= 0:
            return
        if ENG_OF_SEM.get(s) == eng and s in ("pe", "act", "dve", "gps"):
            return
        self.ops[eng].append(lambda e, _s=s, _v=v: e.wait_ge(self.sems[_s], _v))

    def ws(self, eng, toks):
        best = {}
        for t in toks:
            if t is None:
                continue
            s, v = t
            if v > best.get(s, -1):
                best[s] = v
        for s, v in best.items():
            self.w(eng, (s, v))

    def wf(self, eng, tok):
        if tok is None:
            return
        s, v = tok
        if v <= 0:
            return
        self.ops[eng].append(lambda e, _s=s, _v=v: e.wait_ge(self.sems[_s], _v))

    def i(self, eng, fn, sem=None, k=1):
        if sem is None:
            sem = PROG[eng]
        self.cnt[sem] += k
        v = self.cnt[sem]
        self.ops[eng].append(
            lambda e, _f=fn, _s=sem, _k=k: _f(e).then_inc(self.sems[_s], _k))
        return (sem, v)

    def dma(self, eng, fn, waits=(), slot=None):
        q = self.slotsem(slot, eng) if slot else             {"sy": "qsy", "act": "qsc", "gps": "qgp"}[eng]
        best = {}
        for t in waits:
            if t is None:
                continue
            s, v = t
            if v > best.get(s, -1):
                best[s] = v
        for s, v in best.items():
            self.wf(eng, (s, v))
        return self.i(eng, fn, q, 16)

    def nop_inc(self, eng):
        self.i(eng, lambda e: e.nop())

    def barrier(self):
        for eng in ENGS:
            for s, owner in self.sem_owner.items():
                if owner == eng and self.cnt.get(s, 0) > 0:
                    self.wf(eng, (s, self.cnt[s]))
            self.nop_inc(eng)
        snap = {s: self.cnt[s] for s in ("pe", "act", "dve", "gps", "spr")}
        for eng in ENGS:
            for s, v in snap.items():
                if v <= 0 or ENG_OF_SEM[s] == eng:
                    continue
                self.ops[eng].append(
                    lambda e, _s=s, _v=v: e.wait_ge(self.sems[_s], _v))


class Buf:
    __slots__ = ("w", "r")

    def __init__(self):
        self.w = None
        self.r = []


class K:
    def __init__(self):
        self.nc = bass.Bass("TRN2", target_bir_lowering=False, debug=False)
        self.E = Bld()
        self.bufs = {}
        self.psum_b = [Buf() for _ in range(8)]

    def buf(self, name):
        if name not in self.bufs:
            self.bufs[name] = Buf()
        return self.bufs[name]

    def rd(self, eng, names):
        toks = []
        for nm in names:
            toks.append(self.buf(nm).w)
        self.E.ws(eng, toks)

    def rdone(self, names, tok):
        for nm in names:
            self.buf(nm).r.append(tok)

    def wr(self, eng, names):
        toks = []
        for nm in names:
            b = self.buf(nm)
            toks.append(b.w)
            toks.extend(b.r)
        self.E.ws(eng, toks)

    def wdone(self, names, tok):
        for nm in names:
            b = self.buf(nm)
            b.w = tok
            b.r = []

    def banks_of(self, off, length):
        b0 = off // 512
        b1 = (off + length - 1) // 512
        return list(range(b0, b1 + 1))

    def ps_wr(self, eng, off, length):
        toks = []
        for b in self.banks_of(off, length):
            toks.append(self.psum_b[b].w)
            toks.extend(self.psum_b[b].r)
        self.E.ws(eng, toks)

    def ps_wdone(self, off, length, tok):
        for b in self.banks_of(off, length):
            self.psum_b[b].w = tok
            self.psum_b[b].r = []

    def ps_rd(self, eng, off, length):
        toks = [self.psum_b[b].w for b in self.banks_of(off, length)]
        self.E.ws(eng, toks)

    def ps_rdone(self, off, length, tok):
        for b in self.banks_of(off, length):
            self.psum_b[b].r.append(tok)

    def build(self):
        nc = self.nc
        E = self.E

        def din(name, shape, dt=F32):
            return nc.dram_tensor(name, shape, dt, kind="ExternalInput").ap()

        self.U = din("U", [BC, T, D])
        self.WIN = din("WIN", [D, D]); self.BIN = din("BIN", [D])
        self.WG = din("WG", [D, D]); self.BG = din("BG", [D])
        self.WOUT = din("WOUT", [D, D]); self.BOUT = din("BOUT", [D])
        self.CONVW = din("CONVW", [D, 3]); self.CONVB = din("CONVB", [D])
        self.AMAT = din("AMAT", [D, NST])
        self.WB = din("WB", [D, NST]); self.BB = din("BB", [NST])
        self.WC = din("WC", [D, NST]); self.BCC = din("BCC", [NST])
        self.WDT = din("WDT", [D, D]); self.BDT = din("BDT", [D])
        self.WRT = din("WRT", [D, G])
        self.CEN = din("CEN", [NH, G])
        self.TEMP = din("TEMP", [128, 1])
        self.IEXP = din("IEXP", [NH, D, HS])
        self.OEXP = din("OEXP", [NH, HS, HD])
        self.IDN = din("IDN", [128, 128])
        self.ONESR = din("ONESR", [32, 128])
        self.PSEL = din("PSEL", [32, 16 * 64])
        self.OSEL = din("OSEL", [32, 16 * 128])

        self.OUT = nc.dram_tensor("OUT", [BC, T, D], F32, kind="ExternalOutput").ap()

        def scr(name, shape, dt=F32):
            kind = "ExternalOutput" if DEBUG_TAPS else "Internal"
            return nc.dram_tensor(name, shape, dt, kind=kind).ap()

        self.sc_xn = scr("sc_xn", [BC, DB, 128, T], F32R)
        self.sc_x = scr("sc_x", [BC, DB, 128, T], F32R)
        self.sc_dt = scr("sc_dt", [BC, DB, 128, T], F32)
        self.sc_dtx = scr("sc_dtx", [BC, DB, 128, T], F32)
        self.sc_z = scr("sc_z", [BC, DB, 128, T], F32)
        self.sc_yz = scr("sc_yz", [BC, DB, 128, T], F32R)
        self.sc_op = scr("sc_op", [BC, T, D], F32)
        self.sc_st = scr("sc_st", [BC, 4, 128, 32], F32)
        self.sc_wt = scr("sc_wt", [BC, 32, T], BF16)
        self.sc_xnb = scr("sc_xnb", [BC, 128, DB * T], BF16)
        self.sc_h = scr("sc_h", [16, 128, T], BF16)
        self.sc_brep = scr("sc_brep", [16, 128, T], BF16)
        self.sc_crep = scr("sc_crep", [16, 128, T], BF16)
        self.sc_dbx = scr("sc_dbx", [16, 128, T], F32)
        self.sc_bm = scr("sc_bm", [BC, 32, T], BF16)
        self.sc_hid = scr("sc_hid", [4, 128, T], BF16)
        self.sc_fp = scr("sc_fp", [128, T], F32)
        self.sc_wrp = scr("sc_wrp", [128, T], F32)
        self.sc_cm = scr("sc_cm", [BC, 32, T], BF16)

        off = [16640]

        def sb(name, shape, dt, at=None):
            a = off[0] if at is None else at
            a = (a + 31) // 32 * 32
            h = nc.alloc_sbuf_tensor_at(name, shape, dt, offset=a)
            if at is None:
                off[0] = a + int(np.prod(shape[1:])) * mybir.dt.size(dt)
            return h

        self.idn_f = sb("idn_f", [128, 128], F32)
        self.idn_r = sb("idn_r", [128, 128], F32R)
        self.idn_b = sb("idn_b", [128, 128], BF16)
        self.ones1r = sb("ones1r", [32, 128], F32R)
        self.onesg = sb("onesg", [128, 1], F32R)
        self.psel_b = sb("psel_b", [32, 16 * 64], BF16)
        self.osel_b = sb("osel_b", [32, 16 * 128], BF16)
        self.negA = sb("negA", [128, 128], F32)
        self.cwcol = sb("cwcol", [128, 24], F32)
        self.bin_c = sb("bin_c", [128, 8], F32)
        self.convb_c = sb("convb_c", [128, 8], F32)
        self.bg_c = sb("bg_c", [128, 8], F32)
        self.nbdt_c = sb("nbdt_c", [128, 8], F32)
        self.bout_c = sb("bout_c", [128, 8], F32)
        self.nbb_c = sb("nbb_c", [16, 1], F32)
        self.bc_c = sb("bc_c", [16, 1], F32)
        self.scol = sb("scol", [128, 1], F32)
        self.colpad = sb("colpad", [128, 48], F32)
        self.WrT = sb("WrT", [128, DB * 128], F32R)
        self.cnT = sb("cnT", [128, 16], F32R)
        self.cen_s = sb("cen_s", [32, G], F32)
        self.zeros32 = sb("zeros32", [32, 512], BF16)
        self.cnr = sb("cnr", [32, G], F32R)
        self.wTb = sb("wTb", [32, T], BF16)
        self.Bm = sb("Bm", [32, T], BF16)
        self.Cm = sb("Cm", [32, T], BF16)
        self.woutE = sb("woutE", [128, NH * 4 * HD], BF16)
        self.tmp1 = sb("tmp1", [1, 4], F32R)
        W_END = off[0]
        assert W_END < 60000, W_END

        OV = (W_END + 255) // 256 * 256

        o = [OV]

        def ov(name, shape, dt):
            a = (o[0] + 31) // 32 * 32
            h = nc.alloc_sbuf_tensor_at(name, shape, dt, offset=a)
            o[0] = a + int(np.prod(shape[1:])) * mybir.dt.size(dt)
            return h

        def ov_reset():
            o[0] = OV

        self.a1_u = ov("a1_u", [128, 4 * 1024], F32)
        self.a1_scr = ov("a1_scr", [128, 1024], F32)
        self.a1_xtm = ov("a1_xtm", [128, 4 * 1024], F32R)
        self.a1_xfm = ov("a1_xfm", [128, 2 * 512], F32R)
        self.a1_xnb = ov("a1_xnb", [128, DB * T], BF16)
        self.a1_cols = ov("a1_cols", [128, 4 * 8], F32)
        self.a1_zr = ov("a1_zr", [128, 512], F32R)
        self.a1_zsq = ov("a1_zsq", [128, 512], F32R)
        self.a1_znorm = ov("a1_znorm", [128, 512], F32R)
        self.a1_nrm = ov("a1_nrm", [1, 512], F32)
        self.a1_inv = ov("a1_inv", [32, 512], F32R)
        self.a1_lg = ov("a1_lg", [32, 512], F32R)
        self.a1_lgt = ov("a1_lgt", [128, 128], F32)
        self.a1_wtm = ov("a1_wtm", [128, 64], F32)
        self.a1_smc = ov("a1_smc", [128, 16], F32)
        self.a1_msk = ov("a1_msk", [128, 16], F32)
        self.a1_dbg = ov("a1_dbg", [128, JT * 8], F32)
        self.a1_e2 = ov("a1_e2", [128, 16], F32)
        self.a1_wi = ov("a1_wi", [128, 2 * 8 * 512], BF16)
        self.a1_hid = ov("a1_hid", [128, 4 * 512], BF16)
        self.a1_wrep = ov("a1_wrep", [64, 2048], F32)
        self.a1_fe = ov("a1_fe", [64, 2048], F32)
        self.a1_fo = ov("a1_fo", [64, 2048], F32)
        self.a1_ops = ov("a1_ops", [128, 2 * 512], F32)
        A1_END = o[0]

        ov_reset()
        self.a2_xn = ov("a2_xn", [128, 2 * DB * CA], F32R)
        self.a2_win = ov("a2_win", [128, 8 * 1024], F32R)
        self.a2_diag = ov("a2_diag", [128, 24 * 128], F32R)
        self.a2_xpre = ov("a2_xpre", [128, DB * (CA + 2)], F32R)
        self.a2_cpre = ov("a2_cpre", [128, 2 * DB * CA], F32)
        self.a2_wg = ov("a2_wg", [128, 8 * 1024], F32R)
        self.a2_z = ov("a2_z", [128, DB * CA], F32)
        A2A_END = o[0]

        ov_reset()
        self.b2_x = ov("b2_x", [128, 2 * DB * CB], F32R)
        self.b2_sg = ov("b2_sg", [128, 2 * DB * CB], F32)
        self.b2_dtx = ov("b2_dtx", [128, 2 * DB * CB], F32)
        self.b2_z = ov("b2_z", [128, 2 * DB * CB], F32)
        self.b2_wdt = ov("b2_wdt", [128, 8 * 1024], F32R)
        self.b2_wb = ov("b2_wb", [128, 8 * 16], F32R)
        self.b2_wc = ov("b2_wc", [128, 8 * 16], F32R)
        A2B_END = o[0]

        ov_reset()
        self.s_brep = ov("s_brep", [128, 4 * T], BF16)
        self.s_crep = ov("s_crep", [128, 4 * T], BF16)
        self.s_ysb = ov("s_ysb", [128, DB * T], F32R)
        self.s_dt = ov("s_dt", [128, 2 * T], F32)
        self.s_dtx = ov("s_dtx", [128, 2 * T], F32)
        self.s_dbx = ov("s_dbx", [128, 3 * TH], F32)
        self.s_h = ov("s_h", [128, 3 * TH], BF16)
        self.s_hc = ov("s_hc", [128, 3 * TH], BF16)
        self.s_z = ov("s_z", [128, 2 * TH], F32)
        self.s_yz = ov("s_yz", [128, 2 * TH], F32R)
        B_END = o[0]

        ov_reset()
        self.c_yz = ov("c_yz", [128, 2 * DB * 512], F32R)
        self.c_wout = ov("c_wout", [128, 8 * 1024], F32R)
        self.c_attn = ov("c_attn", [128, 2 * 512], F32)
        self.c_uo = ov("c_uo", [128, 2 * 4 * 1024], F32)
        self.c_op = ov("c_op", [128, 2 * 4 * 1024], F32)
        C_END = o[0]

        assert max(A1_END, A2A_END, A2B_END, B_END, C_END) <= 229312, (
            A1_END, A2A_END, A2B_END, B_END, C_END)

        self.ps = nc.alloc_psum_tensor("ps", [128, 4096], F32)

        self.preamble()
        for b in range(BC):
            self.phase_a1(b)
            self.E.barrier()
            if PHASE_LIMIT >= 2:
                self.phase_a2a(b)
                self.E.barrier()
            if PHASE_LIMIT >= 3:
                self.phase_a2b(b)
                self.phase_a2b_taps(b)
                self.E.barrier()
            if PHASE_LIMIT >= 4:
                self.phase_b(b)
                self.E.barrier()
            if PHASE_LIMIT >= 5:
                self.phase_c(b)
                self.E.barrier()

        self.E.barrier()

        import contextlib
        stack = contextlib.ExitStack()
        for s in list(SEMS) + list(self.E.slotsems.values()):
            self.E.sems[s] = stack.enter_context(nc.semaphore(s))
        with stack:
            with nc.Block() as block:
                @block.tensor
                def _(e):
                    for f in E.ops["pe"]:
                        f(e)

                @block.scalar
                def _(e):
                    for f in E.ops["act"]:
                        f(e)

                @block.vector
                def _(e):
                    for f in E.ops["dve"]:
                        f(e)

                @block.gpsimd
                def _(e):
                    for f in E.ops["gps"]:
                        f(e)

                @block.sync
                def _(e):
                    for f in E.ops["sy"]:
                        f(e)
        return nc

    def preamble(self):
        E, nc = self.E, self.nc
        ps = self.ps.ap()

        t_idn = E.dma("sy", lambda e: e.dma_start(out=self.idn_f.ap(), in_=self.IDN), slot="c_idn")
        t_idnr = E.dma("gps", lambda e: e.dma_start(out=self.idn_r.ap(), in_=self.IDN), slot="c_idnr")
        t_idnb = E.dma("gps", lambda e: e.dma_start(out=self.idn_b.ap(), in_=self.IDN), slot="c_idnb")
        t_o1r = E.dma("gps", lambda e: e.dma_start(out=self.ones1r.ap(), in_=self.ONESR), slot="c_o1r")
        t_og = E.dma("gps", lambda e: e.dma_start(
            out=self.onesg.ap(), in_=self.ONESR[0:1, :].rearrange("o p -> p o"),
            allow_slow_non_contiguous=True), slot="c_og")
        tz0 = None
        for tns in (self.wTb, self.Bm, self.Cm):
            tz0 = E.i("dve", lambda e, _t=tns: e.memset(
                _t.ap().bitcast(F32), 0.0))
        self.buf("padzero").w = tz0
        tz1 = E.i("dve", lambda e: e.memset(self.zeros32.ap(), 0.0))
        self.buf("zeros32").w = tz1
        tz2 = E.i("dve", lambda e: e.memset(self.colpad.ap()[:, 25:26], 1e-24))
        self.buf("eps24").w = tz2
        t_ps = E.dma("gps", lambda e: e.dma_start(out=self.psel_b.ap(),
                                                  in_=self.PSEL), slot="c_ps")
        self.buf("pairs").w = t_ps
        t_os = E.dma("gps", lambda e: e.dma_start(out=self.osel_b.ap(),
                                                  in_=self.OSEL), slot="c_os")
        self.buf("osel").w = t_os

        t_a = E.dma("sy", lambda e: e.dma_start(
            out=self.a1_scr.ap()[:, 0:128].rearrange("p (db n) -> p db n", db=8),
            in_=self.AMAT.rearrange("(db p) n -> p db n", p=128)), slot="c_a")
        E.ws("dve", [t_a])
        t_na = E.i("dve", lambda e: e.tensor_scalar_mul(
            self.negA.ap(), self.a1_scr.ap()[:, 0:128], -1.0))
        self.buf("negA").w = t_na

        def loadcol(dst, src, neg=False, n16=False, nm=""):
            if n16:
                tok = E.dma("sy", lambda e, _d=dst, _s=src: e.dma_start(
                    out=_d.ap(), in_=_s.rearrange("(n o) -> n o", o=1)),
                    slot="c_" + nm)
            else:
                tok = E.dma("sy", lambda e, _d=dst, _s=src: e.dma_start(
                    out=_d.ap(), in_=_s.rearrange("(db p) -> p db", p=128),
                    allow_slow_non_contiguous=True), slot="c_" + nm)
            if neg:
                E.ws("dve", [tok])
                tok = E.i("dve", lambda e, _d=dst: e.tensor_scalar_mul(
                    _d.ap(), _d.ap(), -1.0))
            return tok

        self.buf("bin").w = loadcol(self.bin_c, self.BIN, nm="bin")
        self.buf("convb").w = loadcol(self.convb_c, self.CONVB, nm="cvb")
        self.buf("bg").w = loadcol(self.bg_c, self.BG, nm="bg")
        self.buf("nbdt").w = loadcol(self.nbdt_c, self.BDT, neg=True, nm="bdt")
        self.buf("bout").w = loadcol(self.bout_c, self.BOUT, nm="bo")
        self.buf("nbb").w = loadcol(self.nbb_c, self.BB, neg=True, n16=True, nm="bb")
        self.buf("bc").w = loadcol(self.bc_c, self.BCC, n16=True, nm="bc")

        t_cw = E.dma("sy", lambda e: e.dma_start(
            out=self.cwcol.ap()[:, 0:24].rearrange("p (db k) -> p db k", db=8),
            in_=self.CONVW.rearrange("(db p) k -> p db k", p=128)), slot="c_cw")
        self.buf("cw").w = t_cw

        t_wrt = E.dma("gps", lambda e: e.dma_start(
            out=self.WrT.ap().rearrange("p (db g) -> p db g", db=8),
            in_=self.WRT.rearrange("(db p) g -> p db g", p=128)), slot="c_wrt")
        self.buf("WrT").w = t_wrt

        t_we = E.dma("gps", lambda e: e.dma_start(
            out=self.woutE.ap().rearrange("p (n hb e) -> p n hb e", n=16, hb=4),
            in_=self.OEXP.rearrange("n (hb p) e -> p n hb e", p=128)), slot="c_we")
        self.buf("woutE").w = t_we

        E.ws("dve", [self.buf("zeros32").w])
        tcz = E.i("dve", lambda e: e.memset(self.cen_s.ap(), 0.0))
        t_ce = E.dma("sy", lambda e: e.dma_start(
            out=self.cen_s.ap()[0:16, :], in_=self.CEN), waits=[tcz],
            slot="c_cen")
        E.ws("act", [t_ce, t_na])
        t1 = E.i("act", lambda e: e.activation(
            out=self.a1_scr.ap()[0:32, 0:128], in_=self.cen_s.ap(),
            func=AF.Square))
        E.ws("dve", [t1])
        t2 = E.i("dve", lambda e: e.reduce_sum(
            self.colpad.ap()[0:32, 0:1], self.a1_scr.ap()[0:32, 0:128], AX.X))
        E.ws("act", [t2, self.buf("eps24").w])
        t3 = E.i("act", lambda e: e.activation(
            out=self.colpad.ap()[0:32, 1:2], in_=self.colpad.ap()[0:32, 0:1],
            func=AF.Ln, bias=self.colpad.ap()[0:32, 25:26]))
        t4 = E.i("act", lambda e: e.activation(
            out=self.colpad.ap()[0:32, 2:3], in_=self.colpad.ap()[0:32, 1:2],
            func=AF.Exp, scale=-0.5))
        E.ws("dve", [t4])
        E.wf("dve", t2)
        t5 = E.i("dve", lambda e: e.tensor_scalar(
            out=self.cnr.ap(),
            in0=self.cen_s.ap(), scalar1=self.colpad.ap()[0:32, 2:3],
            scalar2=None, op0=OP.mult))
        E.ws("pe", [t5, t_idnr])
        self.ps_wr("pe", 0, 32)
        tt = E.i("pe", lambda e: e.transpose(
            ps[:, 0:32].bitcast(F32R),
            self.cnr.ap(),
            self.idn_r.ap()[0:32, 0:32]))
        self.ps_wdone(0, 32, tt)
        self.ps_rd("act", 0, 32)
        tc = E.i("act", lambda e: e.activation(
            out=self.cnT.ap(), in_=ps[:, 0:16], func=AF.Identity))
        self.ps_rdone(0, 32, tc)
        self.buf("cnT").w = tc

        t_t = E.dma("sy", lambda e: e.dma_start(
            out=self.colpad.ap()[:, 8:9], in_=self.TEMP), slot="c_temp")
        E.ws("act", [t_t])
        t6 = E.i("act", lambda e: e.activation(
            out=self.colpad.ap()[:, 9:10], in_=self.colpad.ap()[:, 8:9],
            func=AF.Exp))
        E.ws("dve", [t6])
        t9 = E.i("dve", lambda e: e.tensor_scalar(
            out=self.scol.ap(), in0=self.colpad.ap()[:, 9:10],
            scalar1=float(1.0 / np.sqrt(G)), scalar2=None, op0=OP.mult))
        self.buf("scol").w = t9
        t_ep = E.i("dve", lambda e: e.memset(self.colpad.ap()[:, 24:25], EPS))
        self.buf("epscol").w = t_ep
        self.buf("idn").w = t_idn
        self.buf("idnr").w = t_idnr
        self.buf("idnb").w = t_idnb
        self.buf("onesg").w = t_og
        self.E.barrier()

    def phase_a1(self, b):
        E, nc = self.E, self.nc
        ps = self.ps.ap()
        ucols = self.a1_cols.ap()

        self.wr("dve", ["a1lg", "a1inv"])
        tzp = E.i("dve", lambda e: e.tensor_scalar(
            out=self.a1_lg.ap(), in0=self.zeros32.ap(), scalar1=1.0,
            scalar2=None, op0=OP.mult))
        tzp = E.i("dve", lambda e: e.tensor_scalar(
            out=self.a1_inv.ap(), in0=self.zeros32.ap(), scalar1=1.0,
            scalar2=None, op0=OP.mult))
        self.wdone(["a1lg", "a1inv"], tzp)

        for jg in range(NJG):
            t_sq = t_s = None
            u_toks = []
            for j4 in range(4):
                jt = jg * 4 + j4
                us = self.a1_u.ap()[:, (jt % 4) * 1024:(jt % 4) * 1024 + 1024]
                nm_u = f"a1u{jt % 4}"
                t_u = E.dma("sy", lambda e, _o=us, _b=b, _jt=jt: e.dma_start(
                    out=_o, in_=self.U[_b, _jt * 128:(_jt + 1) * 128, :]),
                    waits=[self.buf(nm_u).w] + self.buf(nm_u).r, slot=nm_u)
                self.buf(nm_u).w = t_u
                self.buf(nm_u).r = []
                u_toks.append((us, nm_u))
                E.ws("act", [t_u])
                self.wr("act", [f"a1scr"])
                t_sq = E.i("act", lambda e, _u=us, _j=j4: e.activation(
                    out=self.a1_scr.ap(), in_=_u, func=AF.Square))
                self.wdone(["a1scr"], t_sq)
                E.ws("dve", [t_u])
                t_s = E.i("dve", lambda e, _u=us, _j=j4: e.reduce_sum(
                    ucols[:, 4 + _j:5 + _j], _u, AX.X))
                self.rdone([nm_u], t_s)
                E.ws("dve", [t_sq])
                t_sq = E.i("dve", lambda e, _j=j4: e.reduce_sum(
                    ucols[:, 0 + _j:1 + _j], self.a1_scr.ap(), AX.X))
                self.rdone(["a1scr"], t_sq)
            E.wf("dve", t_s)
            t_m = E.i("dve", lambda e: e.tensor_scalar_mul(
                ucols[:, 8:12], ucols[:, 4:8], 1.0 / D))
            E.ws("dve", [t_sq])
            E.wf("dve", t_sq)
            t_m2 = E.i("dve", lambda e: e.tensor_scalar_mul(
                ucols[:, 12:16], ucols[:, 0:4], 1.0 / D))
            E.wf("dve", t_m)
            t_ms = E.i("dve", lambda e: e.tensor_mul(
                ucols[:, 16:20], ucols[:, 8:12], ucols[:, 8:12]))
            E.wf("dve", t_ms)
            t_v = E.i("dve", lambda e: e.tensor_sub(
                ucols[:, 20:24], ucols[:, 12:16], ucols[:, 16:20]))
            E.ws("act", [t_v])
            t_l = E.i("act", lambda e: e.activation(
                out=ucols[:, 24:28], in_=ucols[:, 20:24], func=AF.Ln,
                bias=self.colpad.ap()[:, 24:25]))
            t_is = E.i("act", lambda e: e.activation(
                out=ucols[:, 24:28], in_=ucols[:, 24:28], func=AF.Exp,
                scale=-0.5))
            E.ws("dve", [t_is])
            t_nm = E.i("dve", lambda e: e.tensor_mul(
                ucols[:, 28:32], ucols[:, 8:12], ucols[:, 24:28]))
            E.wf("dve", t_nm)
            t_nm2 = E.i("dve", lambda e: e.tensor_scalar_mul(
                ucols[:, 28:32], ucols[:, 28:32], -1.0))
            for j4 in range(4):
                us, nm_u = u_toks[j4]
                xs = self.a1_xtm.ap()[:, j4 * 1024:(j4 + 1) * 1024]
                E.ws("act", [t_nm2])
                self.wr("act", [f"a1xtm{j4}"])
                t_xn = E.i("act", lambda e, _u=us, _j=j4, _x=xs: e.activation(
                    out=_x, in_=_u, func=AF.Identity,
                    scale=ucols[:, 24 + _j:25 + _j],
                    bias=ucols[:, 28 + _j:29 + _j]))
                self.wdone([f"a1xtm{j4}"], t_xn)
                self.rdone([nm_u], t_xn)
            if DEBUG_TAPS:
                E.ws("dve", [t_is, t_sq])
                E.wf("dve", t_nm2)
                tdbg = E.i("dve", lambda e, _jg=jg: e.tensor_copy(
                    out=self.a1_dbg.ap()[:, _jg * 32:(_jg + 1) * 32],
                    in_=ucols[:, 0:32]))
                E.dma("sy", lambda e, _jg=jg, _b=b: e.dma_start(
                    out=self.sc_st[_b, _jg],
                    in_=self.a1_dbg.ap()[:, _jg * 32:(_jg + 1) * 32]),
                    waits=[tdbg])

            for db in range(DB):
                bank = (db % 2) * 512
                self.ps_wr("pe", bank, 512)
                E.ws("pe", [self.buf("idnr").w])
                tts = []
                for j4 in range(4):
                    self.rd("pe", [f"a1xtm{j4}"])
                    tt = E.i("pe", lambda e, _j=j4, _db=db, _bk=bank: e.transpose(
                        ps[:, _bk + _j * 128:_bk + (_j + 1) * 128].bitcast(F32R),
                        self.a1_xtm.ap()[:, _j * 1024 + _db * 128:
                                         _j * 1024 + _db * 128 + 128],
                        self.idn_r.ap()))
                    tts.append(tt)
                    self.rdone([f"a1xtm{j4}"], tt)
                self.ps_wdone(bank, 512, tts[-1])
                slot = db % 2
                xf = self.a1_xfm.ap()[:, slot * 512:slot * 512 + 512]
                self.ps_rd("act", bank, 512)
                self.wr("act", [f"a1xfm{slot}"])
                t_ev = E.i("act", lambda e, _x=xf, _bk=bank: e.activation(
                    out=_x, in_=ps[:, _bk:_bk + 512], func=AF.Identity))
                self.ps_rdone(bank, 512, t_ev)
                self.wdone([f"a1xfm{slot}"], t_ev)
                E.ws("dve", [t_ev])
                t_cb = E.i("dve", lambda e, _x=xf, _db=db, _jg=jg: e.tensor_copy(
                    out=self.a1_xnb.ap()[:, _db * T + _jg * 512:
                                         _db * T + _jg * 512 + 512], in_=_x))
                self.rdone([f"a1xfm{slot}"], t_cb)
                self.buf("xnb").w = t_cb
                t_sp = E.dma("sy", lambda e, _x=xf, _b=b, _db=db, _jg=jg:
                             e.dma_start(out=self.sc_xn[_b, _db, :,
                                                        _jg * 512:(_jg + 1) * 512],
                                         in_=_x), waits=[t_ev],
                             slot=f"a1xfmd{slot}")
                self.rdone([f"a1xfm{slot}"], t_sp)
                E.ws("pe", [t_ev, self.buf("WrT").w])
                if db == 0:
                    self.ps_wr("pe", 1024, 512)
                t_r = E.i("pe", lambda e, _x=xf, _db=db: e.matmul(
                    ps[:, 1024:1536],
                    self.WrT.ap()[:, _db * 128:(_db + 1) * 128], _x,
                    start=(_db == 0), stop=(_db == DB - 1)))
                self.rdone([f"a1xfm{slot}"], t_r)
                if db == DB - 1:
                    self.ps_wdone(1024, 512, t_r)

            self.ps_rd("act", 1024, 512)
            self.wr("act", ["a1zr"])
            t_zr = E.i("act", lambda e: e.activation(
                out=self.a1_zr.ap(), in_=ps[:, 1024:1536], func=AF.Identity))
            self.ps_rdone(1024, 512, t_zr)
            self.wdone(["a1zr"], t_zr)
            t_zq = E.i("act", lambda e: e.activation(
                out=self.a1_zsq.ap(), in_=self.a1_zr.ap(), func=AF.Square))
            E.ws("pe", [t_zq, self.buf("onesg").w])
            self.ps_wr("pe", 1536, 512)
            t_ns = E.i("pe", lambda e: e.matmul(
                ps[0:1, 1536:2048], self.onesg.ap(), self.a1_zsq.ap(),
                start=True, stop=True))
            self.ps_wdone(1536, 512, t_ns)
            self.ps_rd("act", 1536, 512)
            t_nr = E.i("act", lambda e: e.activation(
                out=self.a1_nrm.ap(), in_=ps[0:1, 1536:2048], func=AF.Ln))
            self.ps_rdone(1536, 512, t_nr)
            t_iv = E.i("act", lambda e: e.activation(
                out=self.a1_inv.ap()[0:1, :], in_=self.a1_nrm.ap(),
                func=AF.Exp, scale=-0.5))
            E.ws("pe", [t_iv, self.buf("a1inv").w])
            self.ps_wr("pe", 1536, 512)
            t_ir = E.i("pe", lambda e: e.matmul(
                ps[:, 1536:2048], self.ones1r.ap(), self.a1_inv.ap(),
                start=True, stop=True))
            self.ps_wdone(1536, 512, t_ir)
            self.ps_rd("dve", 1536, 512)
            E.ws("dve", [t_zr])
            t_zn = E.i("dve", lambda e: e.tensor_mul(
                self.a1_znorm.ap(), self.a1_zr.ap(), ps[:, 1536:2048]))
            self.ps_rdone(1536, 512, t_zn)
            E.ws("pe", [t_zn, self.buf("cnT").w])
            self.ps_wr("pe", 1024, 512)
            t_lg = E.i("pe", lambda e: e.matmul(
                ps[0:16, 1024:1536], self.cnT.ap(), self.a1_znorm.ap(),
                start=True, stop=True))
            self.ps_wdone(1024, 512, t_lg)
            self.ps_rd("act", 1024, 512)
            t_le = E.i("act", lambda e: e.activation(
                out=self.a1_lg.ap()[0:16, :], in_=ps[0:16, 1024:1536],
                func=AF.Identity))
            self.ps_rdone(1024, 512, t_le)
            E.ws("pe", [t_le, self.buf("a1lg").w])
            self.ps_wr("pe", 1024, 128)
            tl = None
            for s in range(4):
                tl = E.i("pe", lambda e, _s=s: e.transpose(
                    ps[:, 1024 + _s * 32:1024 + (_s + 1) * 32].bitcast(F32R),
                    self.a1_lg.ap()[:, _s * 128:(_s + 1) * 128],
                    self.idn_r.ap()[0:32, 0:32]))
            self.ps_wdone(1024, 128, tl)
            self.ps_rd("act", 1024, 128)
            t_lt = E.i("act", lambda e: e.activation(
                out=self.a1_lgt.ap(), in_=ps[:, 1024:1152], func=AF.Identity))
            self.ps_rdone(1024, 128, t_lt)
            for s in range(4):
                lg = self.a1_lgt.ap()[:, s * 32:s * 32 + 16]
                smc = self.a1_smc.ap()
                msk = self.a1_msk.ap()
                e2 = self.a1_e2.ap()
                wv = self.a1_wtm.ap()[:, s * 16:(s + 1) * 16]
                E.ws("dve", [t_lt])
                t_mx = E.i("dve", lambda e, _l=lg, _c=smc: e.reduce_max(
                    _c[:, 0:1], _l, AX.X))
                E.wf("dve", t_mx)
                t_bi = E.i("dve", lambda e, _c=smc: e.tensor_scalar(
                    out=_c[:, 1:2], in0=_c[:, 0:1],
                    scalar1=self.scol.ap(), scalar2=-1.0, op0=OP.mult,
                    op1=OP.mult))
                E.ws("act", [t_bi, t_lt])
                t_ex = E.i("act", lambda e, _l=lg, _c=smc: e.activation(
                    out=_l, in_=_l, func=AF.Exp, scale=self.scol.ap(),
                    bias=_c[:, 1:2]))
                E.ws("dve", [t_ex])
                t_sm = E.i("dve", lambda e, _l=lg, _c=smc: e.reduce_sum(
                    _c[:, 2:3], _l, AX.X))
                E.ws("act", [t_sm])
                t_ln = E.i("act", lambda e, _c=smc: e.activation(
                    out=_c[:, 3:4], in_=_c[:, 2:3], func=AF.Ln))
                t_rc = E.i("act", lambda e, _c=smc: e.activation(
                    out=_c[:, 3:4], in_=_c[:, 3:4], func=AF.Exp, scale=-1.0))
                t_m1 = E.i("dve", lambda e, _l=lg, _c=smc: e.reduce_max(
                    _c[:, 4:5], _l, AX.X))
                E.wf("dve", t_m1)
                t_lt2 = E.i("dve", lambda e, _l=lg, _c=smc, _m=msk:
                            e.tensor_scalar(
                    out=_m, in0=_l, scalar1=_c[:, 4:5], scalar2=None,
                    op0=OP.is_lt))
                E.wf("dve", t_lt2)
                t_p2 = E.i("dve", lambda e, _l=lg, _m=msk, _e2=e2:
                           e.tensor_mul(_e2, _l, _m))
                E.wf("dve", t_p2)
                t_m2 = E.i("dve", lambda e, _c=smc, _e2=e2: e.reduce_max(
                    _c[:, 5:6], _e2, AX.X))
                E.wf("dve", t_m2)
                t_ge = E.i("dve", lambda e, _l=lg, _c=smc, _m=msk:
                           e.tensor_scalar(
                    out=_m, in0=_l, scalar1=_c[:, 5:6], scalar2=None,
                    op0=OP.is_ge))
                E.wf("dve", t_ge)
                t_wu = E.i("dve", lambda e, _l=lg, _m=msk, _e2=e2:
                           e.tensor_mul(_e2, _l, _m))
                E.ws("dve", [t_rc])
                E.wf("dve", t_wu)
                t_w = E.i("dve", lambda e, _w=wv, _c=smc, _e2=e2:
                          e.tensor_scalar(
                    out=_w, in0=_e2, scalar1=_c[:, 3:4], scalar2=None,
                    op0=OP.mult))
                self.buf(f"a1w{s}").w = t_w
            self.ps_wr("pe", 1024, 512)
            tws = []
            for s in range(4):
                E.ws("pe", [self.buf(f"a1w{s}").w, self.buf("idn").w])
                tw = E.i("pe", lambda e, _s=s: e.transpose(
                    ps[0:16, 1024 + _s * 128:1024 + (_s + 1) * 128],
                    self.a1_wtm.ap()[:, _s * 16:(_s + 1) * 16],
                    self.idn_f.ap()))
                tws.append(tw)
            self.ps_wdone(1024, 512, tws[-1])
            self.ps_rd("act", 1024, 512)
            self.wr("act", [f"wTb{jg}"])
            t_wt = E.i("act", lambda e, _jg=jg: e.activation(
                out=self.wTb.ap()[0:16, _jg * 512:(_jg + 1) * 512],
                in_=ps[0:16, 1024:1536], func=AF.Identity))
            self.ps_rdone(1024, 512, t_wt)
            self.wdone([f"wTb{jg}"], t_wt)

        if DEBUG_TAPS:
            E.dma("sy", lambda e, _b=b: e.dma_start(
                out=self.sc_wt[_b], in_=self.wTb.ap()),
                waits=[self.buf(f"wTb{j}").w for j in range(4)])
            E.dma("sy", lambda e, _b=b: e.dma_start(
                out=self.sc_xnb[_b], in_=self.a1_xnb.ap()),
                waits=[self.buf("xnb").w])

        def wi_load(n, slot):
            return E.dma("gps", lambda e, _n=n, _s=slot: e.dma_start(
                out=self.a1_wi.ap()[:, _s * 4096:(_s + 1) * 4096].rearrange("p (k h) -> p k h", k=8),
                in_=self.IEXP[_n].rearrange("(k p) h -> p k h", p=128)),
                waits=[self.buf(f"a1wi{slot}").w] + self.buf(f"a1wi{slot}").r,
                slot=f"a1wi{slot}")

        tok = wi_load(0, 0)
        self.buf("a1wi0").w = tok
        self.buf("a1wi0").r = []
        for n in range(NH):
            slot = n % 2
            if n + 1 < NH:
                nx = wi_load(n + 1, (n + 1) % 2)
                self.buf(f"a1wi{(n + 1) % 2}").w = nx
                self.buf(f"a1wi{(n + 1) % 2}").r = []
            wi = self.a1_wi.ap()[:, slot * 4096:(slot + 1) * 4096]
            for hb in range(4):
                for tc in range(4):
                    self.ps_wr("pe", 2048 + tc * 512, 512)
                E.ws("pe", [self.buf(f"a1wi{slot}").w, self.buf("xnb").w])
                tm = None
                for k in range(8):
                    for tc in range(4):
                        tm = E.i("pe", lambda e, _k=k, _hb=hb, _tc=tc, _w=wi:
                                 e.matmul(
                            ps[:, 2048 + _tc * 512:2048 + _tc * 512 + 512],
                            _w[:, _k * 512 + _hb * 128:_k * 512 + _hb * 128 + 128],
                            self.a1_xnb.ap()[:, _k * T + _tc * 512:
                                             _k * T + _tc * 512 + 512],
                            start=(_k == 0), stop=(_k == 7)))
                self.rdone([f"a1wi{slot}"], tm)
                for tc in range(4):
                    self.ps_wdone(2048 + tc * 512, 512, tm)
                self.ps_rd("act", 2048, 2048)
                self.wr("act", [f"a1hid{tc}" for tc in range(4)])
                tg = E.i("act", lambda e: e.activation(
                    out=self.a1_hid.ap(), in_=ps[:, 2048:4096],
                    func=AF.Gelu))
                self.ps_rdone(2048, 2048, tg)
                self.wdone([f"a1hid{tc}" for tc in range(4)], tg)
                tgs = [tg]
                if DEBUG_TAPS and b == 0 and n == 0:
                    tdh = E.dma("sy", lambda e, _hb=hb: e.dma_start(
                        out=self.sc_hid[_hb], in_=self.a1_hid.ap()),
                        waits=[tg])
                    for tc in range(4):
                        self.rdone([f"a1hid{tc}"], tdh)
                E.ws("pe", [self.buf("woutE").w] + tgs)
                for tc in range(4):
                    if hb == 0:
                        self.ps_wr("pe", tc * 512, 512)
                for tc in range(4):
                    hid = self.a1_hid.ap()[:, tc * 512:(tc + 1) * 512]
                    tpo = E.i("pe", lambda e, _h=hid, _n=n, _hb=hb, _tc=tc:
                              e.matmul(
                        ps[0:64, _tc * 512:_tc * 512 + 512],
                        self.woutE.ap()[:, (_n * 4 + _hb) * 64:
                                        (_n * 4 + _hb + 1) * 64],
                        _h, start=(_hb == 0), stop=(_hb == 3)))
                    self.rdone([f"a1hid{tc}"], tpo)
                    if hb == 3:
                        self.ps_wdone(tc * 512, 512, tpo)
            fdst = self.a1_fe if n % 2 == 0 else self.a1_fo
            fnm = "a1fe" if n % 2 == 0 else "a1fo"
            for tc in range(4):
                wbank = 2048 + (tc % 2) * 512
                self.ps_wr("pe", wbank, 512)
                E.ws("pe", [self.buf("pairs").w, self.buf(f"wTb{tc}").w])
                trw = E.i("pe", lambda e, _n=n, _tc=tc, _wb=wbank: e.matmul(
                    ps[0:64, _wb:_wb + 512],
                    self.psel_b.ap()[:, _n * 64:(_n + 1) * 64],
                    self.wTb.ap()[:, _tc * 512:(_tc + 1) * 512],
                    start=True, stop=True))
                self.ps_wdone(wbank, 512, trw)
                ws = tc % 2
                wrp = self.a1_wrep.ap()[0:64, tc * 512:(tc + 1) * 512]
                self.ps_rd("act", wbank, 512)
                if tc < 2:
                    self.wr("act", ["a1wrep"])
                tew = E.i("act", lambda e, _w=wrp, _wb=wbank: e.activation(
                    out=_w, in_=ps[0:64, _wb:_wb + 512], func=AF.Identity))
                self.ps_rdone(wbank, 512, tew)
                if tc == 3:
                    self.wdone(["a1wrep"], tew)
                obank = tc * 512
                ffn = fdst.ap()[:, tc * 512:(tc + 1) * 512]
                self.ps_rd("dve", obank, 512)
                E.ws("dve", [tew])
                if tc == 0:
                    self.wr("dve", [fnm])
                tsc = E.i("dve", lambda e, _f=ffn, _w=wrp, _ob=obank:
                          e.tensor_mul(_f, ps[0:64, _ob:_ob + 512], _w))
                self.ps_rdone(obank, 512, tsc)
                self.rdone(["a1wrep"], tsc)
                if tc == 3:
                    self.wdone([fnm], tsc)
                if DEBUG_TAPS and b == 0 and n <= 1:
                    tdf = E.dma("sy", lambda e, _f=ffn, _tc=tc, _n=n:
                                e.dma_start(
                        out=self.sc_fp.rearrange(
                            "(h p) t -> h p t", h=2)[_n % 2, :,
                                                     _tc * 512:(_tc + 1) * 512],
                        in_=_f), waits=[tsc])
                    self.rdone([fnm], tdf)

            if n % 2 == 1:
                pr = n // 2
                for tc in range(4):
                    pbank = 3072 + (tc % 2) * 512
                    self.ps_wr("pe", pbank, 512)
                    self.rd("pe", ["a1fe", "a1fo"])
                    E.ws("pe", [self.buf("idn").w])
                    tp = None
                    for j4 in range(4):
                        for hh, fsrc in ((0, self.a1_fe), (1, self.a1_fo)):
                            tp = E.i("pe", lambda e, _f=fsrc, _j=j4, _pb=pbank,
                                     _tc=tc, _hh=hh: e.transpose(
                                ps[:, _pb + _j * 128 + _hh * 64:
                                   _pb + _j * 128 + _hh * 64 + 64],
                                _f.ap()[:, _tc * 512 + _j * 128:
                                        _tc * 512 + (_j + 1) * 128],
                                self.idn_f.ap()[0:64, 0:64]))
                    self.rdone(["a1fe", "a1fo"], tp)
                    self.ps_wdone(pbank, 512, tp)
                    osl = tc % 2
                    ost = self.a1_ops.ap()[:, osl * 512:(osl + 1) * 512]
                    self.ps_rd("act", pbank, 512)
                    self.wr("act", [f"a1ops{osl}"])
                    teo = E.i("act", lambda e, _o=ost, _pb=pbank: e.activation(
                        out=_o, in_=ps[:, _pb:_pb + 512], func=AF.Identity))
                    self.ps_rdone(pbank, 512, teo)
                    self.wdone([f"a1ops{osl}"], teo)
                    tdo = E.dma("sy", lambda e, _o=ost, _b=b, _pr=pr, _tc=tc:
                                e.dma_start(
                        out=self.sc_op[_b, _tc * 512:(_tc + 1) * 512,
                                       _pr * 128:(_pr + 1) * 128]
                        .rearrange("(j p) c -> p j c", p=128),
                        in_=_o.rearrange("p (j c) -> p j c", c=128)),
                        waits=[teo], slot=f"a1ops{osl}")
                    self.rdone([f"a1ops{osl}"], tdo)

    def phase_a2a(self, b):
        E = self.E
        ps = self.ps.ap()

        t_w = E.dma("gps", lambda e: e.dma_start(
            out=self.a2_win.ap().rearrange("p (k e) -> p k e", k=8),
            in_=self.WIN.rearrange("(k p) e -> p k e", p=128)),
            waits=[self.buf("a2win").w] + self.buf("a2win").r, slot="a2win")
        self.buf("a2win").w = t_w
        self.buf("a2win").r = []
        t_wg = E.dma("gps", lambda e: e.dma_start(
            out=self.a2_wg.ap().rearrange("p (k e) -> p k e", k=8),
            in_=self.WG.rearrange("(k p) e -> p k e", p=128)),
            waits=[self.buf("a2wg").w] + self.buf("a2wg").r, slot="a2wg")
        self.buf("a2wg").w = t_wg
        self.buf("a2wg").r = []
        E.ws("dve", [self.buf("cw").w, self.buf("idnr").w])
        self.wr("dve", ["a2diag"])
        td = None
        for db in range(DB):
            for tap in range(3):
                idx = db * 3 + tap
                td = E.i("dve", lambda e, _i=idx: e.tensor_scalar(
                    out=self.a2_diag.ap()[:, _i * 128:(_i + 1) * 128],
                    in0=self.idn_r.ap(), scalar1=self.cwcol.ap()[:, _i:_i + 1],
                    scalar2=None, op0=OP.mult))
        self.wdone(["a2diag"], td)
        t_hz2 = None
        for _db in range(DB):
            for _c in (0, CA + 1):
                t_hz2 = E.i("dve", lambda e, _d=_db, _cc=_c: e.tensor_scalar(
                    out=self.a2_xpre.ap()[:, _d * (CA + 2) + _cc:
                                          _d * (CA + 2) + _cc + 1],
                    in0=self.colpad.ap()[:, 8:9], scalar1=0.0, scalar2=None,
                    op0=OP.mult))
        self.buf("a2halo").w = t_hz2

        for ch in range(NCA):
            xs = ch % 2
            xn = self.a2_xn.ap()[:, xs * DB * CA:(xs + 1) * DB * CA]
            t_x = E.dma("sy", lambda e, _x=xn, _b=b, _ch=ch: e.dma_start(
                out=_x.rearrange("p (k c) -> p k c", k=DB),
                in_=self.sc_xn[_b, :, :, _ch * CA:(_ch + 1) * CA]
                .rearrange("k p c -> p k c")),
                waits=[self.buf(f"a2xn{xs}").w] + self.buf(f"a2xn{xs}").r,
                slot=f"a2xn{xs}")
            self.buf(f"a2xn{xs}").w = t_x
            self.buf(f"a2xn{xs}").r = []

            cslot = ch % 2
            cpre = self.a2_cpre.ap()[:, cslot * DB * CA:(cslot + 1) * DB * CA]

            for e_ in range(DB):
                bank = (e_ % 4) * 512
                self.ps_wr("pe", bank, 512)
                E.ws("pe", [t_x, self.buf("a2win").w])
                tm = None
                for k in range(8):
                    tm = E.i("pe", lambda e, _k=k, _e=e_, _x=xn, _bk=bank:
                             e.matmul(
                        ps[:, _bk:_bk + 512],
                        self.a2_win.ap()[:, _k * 1024 + _e * 128:
                                         _k * 1024 + _e * 128 + 128],
                        _x[:, _k * CA:(_k + 1) * CA],
                        start=(_k == 0), stop=(_k == 7)))
                self.rdone([f"a2xn{xs}"], tm)
                self.ps_wdone(bank, 512, tm)
                xpt = self.a2_xpre.ap()[:, e_ * (CA + 2):(e_ + 1) * (CA + 2)]
                xp = xpt[:, 1:CA + 1]
                self.ps_rd("act", bank, 512)
                self.wr("act", [f"a2xp{e_}"])
                E.ws("act", [self.buf("bin").w])
                te = E.i("act", lambda e, _x=xp, _bk=bank, _e=e_: e.activation(
                    out=_x, in_=ps[:, _bk:_bk + 512], func=AF.Identity,
                    bias=self.bin_c.ap()[:, _e:_e + 1]))
                self.ps_rdone(bank, 512, te)
                self.wdone([f"a2xp{e_}"], te)

                if ch > 0:
                    pslot = (ch - 1) % 2
                    cpv = self.a2_cpre.ap()[:, pslot * DB * CA + e_ * CA:
                                            pslot * DB * CA + (e_ + 1) * CA]
                    E.ws("dve", [te, self.buf(f"a2cp{pslot}_{e_}").w,
                                 self.buf("cw").w])
                    tf = E.i("dve", lambda e, _c=cpv, _x=xpt, _e=e_:
                             e.scalar_tensor_tensor(
                        out=_c[:, CA - 1:CA], in0=_x[:, 1:2],
                        scalar=self.cwcol.ap()[:, _e * 3 + 2:_e * 3 + 3],
                        in1=_c[:, CA - 1:CA], op0=OP.mult, op1=OP.add))
                    self.rdone([f"a2xp{e_}"], tf)
                    E.ws("act", [tf])
                    ts = E.i("act", lambda e, _c=cpv: e.activation(
                        out=_c.bitcast(F32R), in_=_c, func=AF.Silu))
                    self.buf(f"a2cp{pslot}_{e_}").w = ts
                    tsp = E.dma("sy", lambda e, _c=cpv, _b=b, _e=e_, _ch=ch:
                                e.dma_start(
                        out=self.sc_x[_b, _e, :, (_ch - 1) * CA:_ch * CA],
                        in_=_c.bitcast(F32R)), waits=[ts],
                        slot=f"a2cp{pslot}")
                    self.buf(f"a2cp{pslot}_{e_}").r = [tsp]

                cbank = 2048 + (e_ % 4) * 512
                self.ps_wr("pe", cbank, 512)
                E.ws("pe", [te, self.buf("a2diag").w, self.buf("a2halo").w])
                dg = self.a2_diag.ap()
                t1 = E.i("pe", lambda e, _x=xpt, _e=e_, _bk=cbank: e.matmul(
                    ps[:, _bk:_bk + 512],
                    dg[:, (_e * 3 + 0) * 128:(_e * 3 + 1) * 128],
                    _x[:, 0:CA], start=True, stop=False))
                t2 = E.i("pe", lambda e, _x=xpt, _e=e_, _bk=cbank: e.matmul(
                    ps[:, _bk:_bk + 512],
                    dg[:, (_e * 3 + 1) * 128:(_e * 3 + 2) * 128],
                    _x[:, 1:CA + 1], start=False, stop=False))
                t4 = E.i("pe", lambda e, _x=xpt, _e=e_, _bk=cbank: e.matmul(
                    ps[:, _bk:_bk + 512],
                    dg[:, (_e * 3 + 2) * 128:(_e * 3 + 3) * 128],
                    _x[:, 2:CA + 2], start=False, stop=True))
                self.ps_wdone(cbank, 512, t4)
                self.rdone([f"a2xp{e_}"], t4)
                E.ws("act", [t4])
                th = E.i("act", lambda e, _x=xpt: e.activation(
                    out=_x[:, 0:1], in_=_x[:, CA:CA + 1], func=AF.Identity))
                if e_ == DB - 1:
                    self.buf("a2halo").w = th
                cpc = self.a2_cpre.ap()[:, cslot * DB * CA + e_ * CA:
                                        cslot * DB * CA + (e_ + 1) * CA]
                self.ps_rd("act", cbank, 512)
                self.wr("act", [f"a2cp{cslot}_{e_}"])
                E.ws("act", [self.buf("convb").w])
                tev = E.i("act", lambda e, _c=cpc, _bk=cbank, _e=e_: e.activation(
                    out=_c, in_=ps[:, _bk:_bk + 512], func=AF.Identity,
                    bias=self.convb_c.ap()[:, _e:_e + 1]))
                self.ps_rdone(cbank, 512, tev)
                self.wdone([f"a2cp{cslot}_{e_}"], tev)

                if ch == NCA - 1:
                    E.ws("act", [tev])
                    ts = E.i("act", lambda e, _c=cpc: e.activation(
                        out=_c.bitcast(F32R), in_=_c, func=AF.Silu))
                    self.buf(f"a2cp{cslot}_{e_}").w = ts
                    tsp = E.dma("sy", lambda e, _c=cpc, _b=b, _e=e_, _ch=ch:
                                e.dma_start(
                        out=self.sc_x[_b, _e, :, _ch * CA:(_ch + 1) * CA],
                        in_=_c.bitcast(F32R)), waits=[ts],
                        slot=f"a2cp{cslot}")
                    self.buf(f"a2cp{cslot}_{e_}").r = [tsp]

            zsl = 0
            zv = self.a2_z.ap()
            tz = None
            for e_ in range(DB):
                bank = (e_ % 4) * 512
                self.ps_wr("pe", bank, 512)
                E.ws("pe", [t_x, self.buf("a2wg").w])
                tm = None
                for k in range(8):
                    tm = E.i("pe", lambda e, _k=k, _e=e_, _x=xn, _bk=bank:
                             e.matmul(
                        ps[:, _bk:_bk + 512],
                        self.a2_wg.ap()[:, _k * 1024 + _e * 128:
                                        _k * 1024 + _e * 128 + 128],
                        _x[:, _k * CA:(_k + 1) * CA],
                        start=(_k == 0), stop=(_k == 7)))
                self.rdone([f"a2xn{xs}"], tm)
                self.ps_wdone(bank, 512, tm)
                self.ps_rd("act", bank, 512)
                if e_ == 0:
                    self.wr("act", [f"a2z{zsl}"])
                E.ws("act", [self.buf("bg").w])
                tz = E.i("act", lambda e, _e=e_, _z=zv, _bk=bank: e.activation(
                    out=_z[:, _e * CA:(_e + 1) * CA], in_=ps[:, _bk:_bk + 512],
                    func=AF.Silu, bias=self.bg_c.ap()[:, _e:_e + 1]))
                self.ps_rdone(bank, 512, tz)
                if e_ == DB - 1:
                    self.wdone([f"a2z{zsl}"], tz)
            tzs = E.dma("sy", lambda e, _z=zv, _b=b, _ch=ch: e.dma_start(
                out=self.sc_z[_b, :, :, _ch * CA:(_ch + 1) * CA]
                .rearrange("k p c -> p k c"),
                in_=_z.rearrange("p (k c) -> p k c", k=DB)), waits=[tz],
                slot="a2zsp")
            self.rdone([f"a2z{zsl}"], tzs)

    def phase_a2b_taps(self, b):
        E = self.E
        if not DEBUG_TAPS:
            return
        tb1 = E.dma("sy", lambda e, _b=b: e.dma_start(
            out=self.sc_bm[_b], in_=self.Bm.ap()),
            waits=[self.buf("Bm").w])
        self.rdone(["Bm"], tb1)
        tb2 = E.dma("sy", lambda e, _b=b: e.dma_start(
            out=self.sc_cm[_b], in_=self.Cm.ap()),
            waits=[self.buf("Cm").w])
        self.rdone(["Cm"], tb2)

    def phase_a2b(self, b):
        E = self.E
        ps = self.ps.ap()

        t_wd = E.dma("gps", lambda e: e.dma_start(
            out=self.b2_wdt.ap().rearrange("p (k e) -> p k e", k=8),
            in_=self.WDT.rearrange("(k p) e -> p k e", p=128)),
            waits=[self.buf("b2wdt").w] + self.buf("b2wdt").r, slot="b2wdt")
        self.buf("b2wdt").w = t_wd
        t_wb = E.dma("gps", lambda e: e.dma_start(
            out=self.b2_wb.ap().rearrange("p (k n) -> p k n", k=8),
            in_=self.WB.rearrange("(k p) n -> p k n", p=128)),
            waits=[self.buf("b2wb").w] + self.buf("b2wb").r, slot="b2wb")
        self.buf("b2wb").w = t_wb
        t_wc = E.dma("gps", lambda e: e.dma_start(
            out=self.b2_wc.ap().rearrange("p (k n) -> p k n", k=8),
            in_=self.WC.rearrange("(k p) n -> p k n", p=128)),
            waits=[self.buf("b2wc").w] + self.buf("b2wc").r, slot="b2wc")
        self.buf("b2wc").w = t_wc

        for ch in range(NCB):
            xs = ch % 2
            xv = self.b2_x.ap()[:, xs * DB * CB:(xs + 1) * DB * CB]
            t_x = E.dma("sy", lambda e, _x=xv, _b=b, _ch=ch: e.dma_start(
                out=_x.rearrange("p (k c) -> p k c", k=DB),
                in_=self.sc_x[_b, :, :, _ch * CB:(_ch + 1) * CB]
                .rearrange("k p c -> p k c")),
                waits=[self.buf(f"b2x{xs}").w] + self.buf(f"b2x{xs}").r,
                slot=f"b2x{xs}")
            self.buf(f"b2x{xs}").w = t_x
            self.buf(f"b2x{xs}").r = []

            sg = self.b2_sg.ap()[:, xs * DB * CB:(xs + 1) * DB * CB]
            for e_ in range(DB):
                bank = (e_ % 2) * 512
                self.ps_wr("pe", bank, 512)
                E.ws("pe", [t_x, self.buf("b2wdt").w])
                tm = None
                for k in range(8):
                    tm = E.i("pe", lambda e, _k=k, _e=e_, _x=xv, _bk=bank:
                             e.matmul(
                        ps[:, _bk:_bk + CB],
                        self.b2_wdt.ap()[:, _k * 1024 + _e * 128:
                                         _k * 1024 + _e * 128 + 128],
                        _x[:, _k * CB:(_k + 1) * CB],
                        start=(_k == 0), stop=(_k == 7)))
                self.rdone([f"b2x{xs}"], tm)
                self.ps_wdone(bank, 512, tm)
                self.ps_rd("act", bank, 512)
                self.wr("act", [f"b2sg{xs}"]) if e_ == 0 else None
                E.ws("act", [self.buf("nbdt").w])
                te = E.i("act", lambda e, _e=e_, _s=sg, _bk=bank: e.activation(
                    out=_s[:, _e * CB:(_e + 1) * CB], in_=ps[:, _bk:_bk + CB],
                    func=AF.Sigmoid, scale=-1.0,
                    bias=self.nbdt_c.ap()[:, _e:_e + 1]))
                self.ps_rdone(bank, 512, te)
                if e_ == DB - 1:
                    self.wdone([f"b2sg{xs}"], te)
            tl = E.i("act", lambda e, _s=sg: e.activation(
                out=_s, in_=_s, func=AF.Ln))
            self.buf(f"b2sg{xs}").w = tl
            tdt = E.dma("sy", lambda e, _s=sg, _b=b, _ch=ch: e.dma_start(
                out=self.sc_dt[_b, :, :, _ch * CB:(_ch + 1) * CB]
                .rearrange("k p c -> p k c"),
                in_=_s.rearrange("p (k c) -> p k c", k=DB)), waits=[tl],
                slot=f"b2sg{xs}d")
            self.rdone([f"b2sg{xs}"], tdt)
            dx = self.b2_dtx.ap()[:, xs * DB * CB:(xs + 1) * DB * CB]
            E.ws("dve", [tl, t_x])
            self.wr("dve", [f"b2dtx{xs}"])
            tx = E.i("dve", lambda e, _d=dx, _s=sg, _x=xv: e.tensor_mul(
                _d, _s, _x.bitcast(F32)))
            self.rdone([f"b2x{xs}"], tx)
            self.rdone([f"b2sg{xs}"], tx)
            self.wdone([f"b2dtx{xs}"], tx)
            tdx = E.dma("sy", lambda e, _d=dx, _b=b, _ch=ch: e.dma_start(
                out=self.sc_dtx[_b, :, :, _ch * CB:(_ch + 1) * CB]
                .rearrange("k p c -> p k c"),
                in_=_d.rearrange("p (k c) -> p k c", k=DB)), waits=[tx],
                slot=f"b2dtx{xs}d")
            self.rdone([f"b2dtx{xs}"], tdx)

            for (wsb, dst, biascol, scl, nm) in (
                    (self.b2_wb, self.Bm, self.nbb_c, -1.0, "Bm"),
                    (self.b2_wc, self.Cm, self.bc_c, 1.0, "Cm")):
                bank = 1024
                self.ps_wr("pe", bank, 512)
                E.ws("pe", [t_x, self.buf("b2wb").w, self.buf("b2wc").w])
                tm = None
                for k in range(8):
                    tm = E.i("pe", lambda e, _k=k, _w=wsb, _x=xv, _bk=bank:
                             e.matmul(
                        ps[0:16, _bk:_bk + CB],
                        _w.ap()[:, _k * 16:(_k + 1) * 16],
                        _x[:, _k * CB:(_k + 1) * CB],
                        start=(_k == 0), stop=(_k == 7)))
                self.rdone([f"b2x{xs}"], tm)
                self.ps_wdone(bank, 512, tm)
                self.ps_rd("act", bank, 512)
                self.wr("act", [nm])
                E.ws("act", [self.buf("nbb").w, self.buf("bc").w])
                tb = E.i("act", lambda e, _d=dst, _c=biascol, _s=scl, _b2=b,
                         _ch=ch, _bk=bank: e.activation(
                    out=_d.ap()[0:16, _ch * CB:(_ch + 1) * CB],
                    in_=ps[0:16, _bk:_bk + CB], func=AF.Identity, scale=_s,
                    bias=_c.ap()))
                self.ps_rdone(bank, 512, tb)
                self.buf(nm).w = tb


    def phase_b(self, b):
        E = self.E
        ps = self.ps.ap()

        for q in range(4):
            ns = list(range(4 * q, 4 * q + 4))
            for i, n in enumerate(ns):
                for (src, dstbuf, dst) in ((self.Bm, "brep", self.s_brep),
                                           (self.Cm, "crep", self.s_crep)):
                    for tc in range(4):
                        bank = (tc % 2) * 512
                        self.ps_wr("pe", bank, 512)
                        E.ws("pe", [self.buf("Bm").w, self.buf("Cm").w,
                                    self.buf("osel").w])
                        tr = E.i("pe", lambda e, _s=src, _n=n, _tc=tc, _bk=bank:
                                 e.matmul(
                            ps[:, _bk:_bk + 512],
                            self.osel_b.ap()[:, _n * 128:(_n + 1) * 128],
                            _s.ap()[:, _tc * 512:(_tc + 1) * 512],
                            start=True, stop=True))
                        self.ps_wdone(bank, 512, tr)
                        self.ps_rd("act", bank, 512)
                        if tc == 0:
                            self.wr("act", [f"{dstbuf}{i}"])
                        tv = E.i("act", lambda e, _d=dst, _i=i, _tc=tc, _bk=bank:
                                 e.activation(
                            out=_d.ap()[:, _i * T + _tc * 512:
                                        _i * T + (_tc + 1) * 512],
                            in_=ps[:, _bk:_bk + 512], func=AF.Identity))
                        self.ps_rdone(bank, 512, tv)
                        if tc == 3:
                            self.wdone([f"{dstbuf}{i}"], tv)
                            if DEBUG_TAPS and b == 0:
                                scd = (self.sc_brep if dstbuf == "brep"
                                       else self.sc_crep)
                                tdr = E.dma("sy", lambda e, _d=dst, _i=i, _n=n,
                                            _s=scd: e.dma_start(
                                    out=_s[_n],
                                    in_=_d.ap()[:, _i * T:(_i + 1) * T]),
                                    waits=[tv])
                                self.rdone([f"{dstbuf}{i}"], tdr)

            def load_db(db):
                sl = db % 2
                tdt = E.dma("act", lambda e, _b=b, _db=db, _sl=sl: e.dma_start(
                    out=self.s_dt.ap()[:, _sl * T:(_sl + 1) * T],
                    in_=self.sc_dt[_b, _db, :, :]),
                    waits=[self.buf(f"sdt{sl}").w] + self.buf(f"sdt{sl}").r,
                    slot=f"sdt{sl}")
                self.buf(f"sdt{sl}").w = tdt
                self.buf(f"sdt{sl}").r = []
                tdx = E.dma("act", lambda e, _b=b, _db=db, _sl=sl: e.dma_start(
                    out=self.s_dtx.ap()[:, _sl * T:(_sl + 1) * T],
                    in_=self.sc_dtx[_b, _db, :, :]),
                    waits=[self.buf(f"sdx{sl}").w] + self.buf(f"sdx{sl}").r,
                    slot=f"sdx{sl}")
                self.buf(f"sdx{sl}").w = tdx
                self.buf(f"sdx{sl}").r = []

            load_db(0)
            for db in range(DB):
                if db + 1 < DB:
                    load_db(db + 1)
                sl = db % 2
                dtv = self.s_dt.ap()[:, sl * T:(sl + 1) * T]
                dxv = self.s_dtx.ap()[:, sl * T:(sl + 1) * T]

                if q > 0:
                    E.ws("pe", [self.buf(f"ysb{db}").w, self.buf("idnr").w])
                    for tc in range(4):
                        self.ps_wr("pe", tc * 512, 512)
                    tj = None
                    for tc in range(4):
                        tj = E.i("pe", lambda e, _db=db, _tc=tc: e.matmul(
                            ps[:, _tc * 512:(_tc + 1) * 512], self.idn_r.ap(),
                            self.s_ysb.ap()[:, _db * T + _tc * 512:
                                            _db * T + (_tc + 1) * 512],
                            start=True, stop=False))
                    self.rdone([f"ysb{db}"], tj)
                    for tc in range(4):
                        self.ps_wdone(tc * 512, 512, tj)
                else:
                    for tc in range(4):
                        self.ps_wr("pe", tc * 512, 512)

                for i, n in enumerate(ns):
                    pr = {}
                    for h in (0, 1):
                        step = i * 2 + h
                        ping = 2048 + (step % 2) * 1024
                        self.ps_wr("act", ping, 1024)
                        E.ws("act", [self.buf(f"sdt{sl}").w,
                                     self.buf("negA").w])
                        ta = E.i("act", lambda e, _d=dtv, _h=h, _db=db, _n=n,
                                 _pg=ping: e.activation(
                            out=ps[:, _pg:_pg + 1024],
                            in_=_d[:, _h * TH:(_h + 1) * TH], func=AF.Exp,
                            scale=self.negA.ap()[:, _db * 16 + _n:
                                                 _db * 16 + _n + 1]))
                        self.ps_wdone(ping, 1024, ta)
                        self.rdone([f"sdt{sl}"], ta)
                        ds = step % 3
                        dbx = self.s_dbx.ap()[:, ds * TH:(ds + 1) * TH]
                        self.wr("gps", [f"dbx{ds}"])
                        E.ws("gps", [self.buf(f"sdx{sl}").w,
                                     self.buf(f"brep{i}").w])
                        tb = E.i("gps", lambda e, _x=dxv, _o=dbx, _i=i, _h=h:
                                 e.tensor_mul(
                            _o, _x[:, _h * TH:(_h + 1) * TH],
                            self.s_brep.ap()[:, _i * T + _h * TH:
                                             _i * T + (_h + 1) * TH]))
                        self.wdone([f"dbx{ds}"], tb)
                        self.rdone([f"sdx{sl}"], tb)
                        self.rdone([f"brep{i}"], tb)
                        pr[h] = (ta, tb, ping, ds, dbx)
                    for h in (0, 1):
                        step = i * 2 + h
                        ta, tb, ping, ds, dbx = pr[h]
                        hs = step % 3
                        hv = self.s_h.ap()[:, hs * TH:(hs + 1) * TH]
                        if h == 0:
                            init = 0.0
                        else:
                            hp = (step - 1) % 3
                            init = self.s_h.ap()[:, hp * TH + TH - 1:
                                                 hp * TH + TH]
                        self.ps_rd("dve", ping, 1024)
                        self.wr("dve", [f"sh{hs}"])
                        E.ws("dve", [ta, tb])
                        if h == 1:
                            E.wf("dve", tts_prev)
                        tts = E.i("dve", lambda e, _o=hv, _pg=ping, _d=dbx,
                                  _in=init: e.tensor_tensor_scan(
                            out=_o, data0=ps[:, _pg:_pg + 1024], data1=_d,
                            initial=_in, op0=OP.mult, op1=OP.add))
                        if DEBUG_TAPS and b == 0 and db == 0:
                            tdh = E.dma("sy", lambda e, _n=n, _h=h, _o=hv:
                                        e.dma_start(
                                out=self.sc_h[_n, :, _h * TH:(_h + 1) * TH],
                                in_=_o), waits=[tts])
                            self.rdone([f"sh{hs}"], tdh)
                            tdb = E.dma("sy", lambda e, _n=n, _h=h, _d=dbx:
                                        e.dma_start(
                                out=self.sc_dbx[_n, :, _h * TH:(_h + 1) * TH],
                                in_=_d), waits=[tb])
                            self.rdone([f"dbx{ds}"], tdb)
                        self.ps_rdone(ping, 1024, tts)
                        self.rdone([f"dbx{ds}"], tts)
                        self.wdone([f"sh{hs}"], tts)
                        tts_prev = tts
                        cs = step % 3
                        heng = "gps" if h == 0 else "dve"
                        hcv = self.s_hc.ap()[:, cs * TH:(cs + 1) * TH]
                        self.wr(heng, [f"shc{cs}"])
                        E.ws(heng, [self.buf(f"crep{i}").w, tts])
                        thc = E.i(heng, lambda e, _o=hcv, _h=hv, _i=i, _hh=h:
                                  e.tensor_mul(
                            _o, _h,
                            self.s_crep.ap()[:, _i * T + _hh * TH:
                                             _i * T + (_hh + 1) * TH]))
                        self.rdone([f"sh{hs}"], thc)
                        self.rdone([f"crep{i}"], thc)
                        self.wdone([f"shc{cs}"], thc)
                        E.ws("pe", [thc, self.buf("idnb").w])
                        tp = None
                        for t2 in range(2):
                            tc = h * 2 + t2
                            tp = E.i("pe", lambda e, _hc=hcv, _t2=t2, _tc=tc,
                                     _st=(q == 0 and i == 0),
                                     _sp=(i == 3): e.matmul(
                                ps[:, _tc * 512:(_tc + 1) * 512],
                                self.idn_b.ap(),
                                _hc[:, _t2 * 512:(_t2 + 1) * 512],
                                start=_st, stop=_sp))
                        self.rdone([f"shc{cs}"], tp)
                        self.ps_wdone(h * 2 * 512, 1024, tp)

                if q < 3:
                    self.ps_rd("act", 0, 2048)
                    self.wr("act", [f"ysb{db}"])
                    ty = E.i("act", lambda e, _db=db: e.activation(
                        out=self.s_ysb.ap()[:, _db * T:(_db + 1) * T],
                        in_=ps[:, 0:2048], func=AF.Identity))
                    self.ps_rdone(0, 2048, ty)
                    self.wdone([f"ysb{db}"], ty)
                else:
                    for h in (0, 1):
                        zs = h % 2
                        zv = self.s_z.ap()[:, zs * TH:(zs + 1) * TH]
                        tz = E.dma("act", lambda e, _b=b, _db=db, _h=h, _z=zv:
                                   e.dma_start(
                            out=_z,
                            in_=self.sc_z[_b, _db, :, _h * TH:(_h + 1) * TH]),
                            waits=[self.buf(f"sz{zs}").w] + self.buf(f"sz{zs}").r,
                            slot=f"sz{zs}")
                        self.buf(f"sz{zs}").w = tz
                        self.buf(f"sz{zs}").r = []
                        ys = h % 2
                        yzv = self.s_yz.ap()[:, ys * TH:(ys + 1) * TH]
                        self.ps_rd("dve", h * 1024, 1024)
                        self.wr("dve", [f"syz{ys}"])
                        E.ws("dve", [tz])
                        tyz = E.i("dve", lambda e, _o=yzv, _z=zv, _h=h:
                                  e.tensor_mul(
                            _o, ps[:, _h * 1024:(_h + 1) * 1024], _z))
                        self.ps_rdone(h * 1024, 1024, tyz)
                        self.rdone([f"sz{zs}"], tyz)
                        self.wdone([f"syz{ys}"], tyz)
                        tsp = E.dma("sy", lambda e, _o=yzv, _b=b, _db=db, _h=h:
                                    e.dma_start(
                            out=self.sc_yz[_b, _db, :, _h * TH:(_h + 1) * TH],
                            in_=_o), waits=[tyz], slot=f"syz{ys}")
                        self.rdone([f"syz{ys}"], tsp)

    def phase_c(self, b):
        E = self.E
        ps = self.ps.ap()

        t_w = E.dma("gps", lambda e: e.dma_start(
            out=self.c_wout.ap().rearrange("p (k e) -> p k e", k=8),
            in_=self.WOUT.rearrange("(k p) e -> p k e", p=128)),
            waits=[self.buf("cwout").w] + self.buf("cwout").r, slot="cwout")
        self.buf("cwout").w = t_w
        self.buf("cwout").r = []

        def load_ch(ch):
            sl = ch % 2
            tyz = E.dma("act", lambda e, _b=b, _ch=ch, _sl=sl: e.dma_start(
                out=self.c_yz.ap()[:, _sl * DB * 512:(_sl + 1) * DB * 512]
                .rearrange("p (k c) -> p k c", k=DB),
                in_=self.sc_yz[_b, :, :, _ch * CC:(_ch + 1) * CC]
                .rearrange("k p c -> p k c")),
                waits=[self.buf(f"cyz{sl}").w] + self.buf(f"cyz{sl}").r,
                slot=f"cyz{sl}")
            self.buf(f"cyz{sl}").w = tyz
            self.buf(f"cyz{sl}").r = []
            tuo = E.dma("sy", lambda e, _b=b, _ch=ch, _sl=sl: e.dma_start(
                out=self.c_uo.ap()[:, _sl * 4096:(_sl + 1) * 4096]
                .rearrange("p (j c) -> p j c", j=4),
                in_=self.U[_b, _ch * CC:(_ch + 1) * CC, :]
                .rearrange("(j p) c -> p j c", p=128)),
                waits=[self.buf(f"cuo{sl}").w] + self.buf(f"cuo{sl}").r,
                slot=f"cuo{sl}")
            self.buf(f"cuo{sl}").w = tuo
            self.buf(f"cuo{sl}").r = []
            top = E.dma("sy", lambda e, _b=b, _ch=ch, _sl=sl: e.dma_start(
                out=self.c_op.ap()[:, _sl * 4096:(_sl + 1) * 4096]
                .rearrange("p (j c) -> p j c", j=4),
                in_=self.sc_op[_b, _ch * CC:(_ch + 1) * CC, :]
                .rearrange("(j p) c -> p j c", p=128)),
                waits=[self.buf(f"cop{sl}").w] + self.buf(f"cop{sl}").r,
                slot=f"cop{sl}")
            self.buf(f"cop{sl}").w = top
            self.buf(f"cop{sl}").r = []

        load_ch(0)
        for ch in range(NCC):
            if ch + 1 < NCC:
                load_ch(ch + 1)
            sl = ch % 2
            yz = self.c_yz.ap()[:, sl * DB * 512:(sl + 1) * DB * 512]
            uo = self.c_uo.ap()[:, sl * 4096:(sl + 1) * 4096]
            op = self.c_op.ap()[:, sl * 4096:(sl + 1) * 4096]
            E.ws("dve", [self.buf(f"cuo{sl}").w, self.buf(f"cop{sl}").w])
            tadd = E.i("dve", lambda e, _u=uo, _o=op: e.tensor_add(_u, _u, _o))
            self.rdone([f"cop{sl}"], tadd)
            self.buf(f"cuo{sl}").w = tadd
            E.wf("dve", tadd)

            for e_ in range(DB):
                bank = (e_ % 2) * 512
                self.ps_wr("pe", bank, 512)
                E.ws("pe", [self.buf(f"cyz{sl}").w, self.buf("cwout").w])
                tm = None
                for k in range(8):
                    tm = E.i("pe", lambda e, _k=k, _e=e_, _y=yz, _bk=bank:
                             e.matmul(
                        ps[:, _bk:_bk + 512],
                        self.c_wout.ap()[:, _k * 1024 + _e * 128:
                                         _k * 1024 + _e * 128 + 128],
                        _y[:, _k * 512:(_k + 1) * 512],
                        start=(_k == 0), stop=(_k == 7)))
                self.rdone([f"cyz{sl}"], tm)
                self.ps_wdone(bank, 512, tm)
                asl = e_ % 2
                att = self.c_attn.ap()[:, asl * 512:(asl + 1) * 512]
                self.ps_rd("act", bank, 512)
                self.wr("act", [f"cat{asl}"])
                E.ws("act", [self.buf("bout").w])
                te = E.i("act", lambda e, _a=att, _bk=bank, _e=e_: e.activation(
                    out=_a, in_=ps[:, _bk:_bk + 512], func=AF.Identity,
                    bias=self.bout_c.ap()[:, _e:_e + 1]))
                self.ps_rdone(bank, 512, te)
                self.wdone([f"cat{asl}"], te)
                pbank = 1024 + (e_ % 2) * 512
                self.ps_wr("pe", pbank, 512)
                E.ws("pe", [te, self.buf("idn").w])
                tp = None
                for j4 in range(4):
                    tp = E.i("pe", lambda e, _a=att, _j=j4, _pb=pbank:
                             e.transpose(
                        ps[:, _pb + _j * 128:_pb + (_j + 1) * 128],
                        _a[:, _j * 128:(_j + 1) * 128], self.idn_f.ap()))
                self.rdone([f"cat{asl}"], tp)
                self.ps_wdone(pbank, 512, tp)
                self.ps_rd("dve", pbank, 512)
                E.ws("dve", [self.buf(f"cuo{sl}").w])
                tv = E.i("dve", lambda e, _u=uo, _e=e_, _pb=pbank: e.tensor_add(
                    _u.rearrange("p (j c) -> p j c", j=4)[:, :,
                                                          _e * 128:(_e + 1) * 128],
                    _u.rearrange("p (j c) -> p j c", j=4)[:, :,
                                                          _e * 128:(_e + 1) * 128],
                    ps[:, _pb:_pb + 512].rearrange("p (j c) -> p j c", c=128)))
                self.ps_rdone(pbank, 512, tv)
                self.buf(f"cuo{sl}").w = tv
            tso = E.dma("sy", lambda e, _u=uo, _b=b, _ch=ch: e.dma_start(
                out=self.OUT[_b, _ch * CC:(_ch + 1) * CC, :]
                .rearrange("(j p) c -> p j c", p=128),
                in_=_u.rearrange("p (j c) -> p j c", j=4)),
                waits=[self.buf(f"cuo{sl}").w], slot=f"cuo{sl}")
            self.rdone([f"cuo{sl}"], tso)


def _ctx():
    import contextlib
    return contextlib.nullcontext()


_NC_CACHE = {}


def build_program():
    if "nc" not in _NC_CACHE:
        _NC_CACHE["nc"] = K().build()
    return _NC_CACHE["nc"]


def kernel(**inputs):
    u = np.ascontiguousarray(np.asarray(inputs["L0_attn_u"]))
    wrt = np.ascontiguousarray(np.asarray(inputs["L0_ffn_w_router"]).T)
    idn = np.eye(128, dtype=np.float32)
    ones = np.zeros((32, 128), dtype=np.float32)
    ones[0, :] = 1.0
    psel = np.zeros((32, 16 * 64), dtype=np.float32)
    for n in range(16):
        psel[n, n * 64:(n + 1) * 64] = 1.0
    osel = np.zeros((32, 16 * 128), dtype=np.float32)
    for n in range(16):
        osel[n, n * 128:(n + 1) * 128] = 1.0

    common = {
        "WIN": np.ascontiguousarray(inputs["L0_attn_Win"]),
        "BIN": np.ascontiguousarray(inputs["L0_attn_b_in"]),
        "WG": np.ascontiguousarray(inputs["L0_attn_Wg"]),
        "BG": np.ascontiguousarray(inputs["L0_attn_bg"]),
        "WOUT": np.ascontiguousarray(inputs["L0_attn_Wout"]),
        "BOUT": np.ascontiguousarray(inputs["L0_attn_bout"]),
        "CONVW": np.ascontiguousarray(inputs["L0_attn_convw"]),
        "CONVB": np.ascontiguousarray(inputs["L0_attn_convb"]),
        "AMAT": np.ascontiguousarray(inputs["L0_attn_A"]),
        "WB": np.ascontiguousarray(inputs["L0_attn_Wb"]),
        "BB": np.ascontiguousarray(inputs["L0_attn_bb"]),
        "WC": np.ascontiguousarray(inputs["L0_attn_Wc"]),
        "BCC": np.ascontiguousarray(inputs["L0_attn_bc"]),
        "WDT": np.ascontiguousarray(inputs["L0_attn_Wdt"]),
        "BDT": np.ascontiguousarray(inputs["L0_attn_bdt"]),
        "WRT": wrt,
        "CEN": np.ascontiguousarray(inputs["L0_ffn_head_centroids"]),
        "TEMP": np.ascontiguousarray(np.repeat(np.asarray(inputs["L0_ffn_temperature"]).reshape(1, 1), 128, axis=0)),
        "IEXP": np.ascontiguousarray(inputs["L0_ffn_input_experts"]),
        "OEXP": np.ascontiguousarray(inputs["L0_ffn_output_experts"]),
        "IDN": idn,
        "ONESR": ones,
        "PSEL": psel,
        "OSEL": osel,
    }
    in_maps = []
    for c in range(NCORE):
        m = dict(common)
        m["U"] = np.ascontiguousarray(u[2 * c:2 * c + 2])
        in_maps.append(m)

    nc = build_program()
    res = run_bass_kernel_spmd(nc, in_maps, core_ids=list(range(NCORE)))
    out = np.concatenate([r["OUT"] for r in res.results], axis=0)
    return out.astype(np.float32)


if __name__ == "__main__":
    nc = build_program()
    print("built ok")


# version 13
# speedup vs baseline: 1.0605x; 1.0605x over previous
import numpy as np

import concourse.bass as bass
import concourse.mybir as mybir
from concourse.bass_utils import run_bass_kernel_spmd

F32 = mybir.dt.float32
F32R = mybir.dt.float32r
BF16 = mybir.dt.bfloat16
AF = mybir.ActivationFunctionType
OP = mybir.AluOpType
AX = mybir.AxisListType

D = 1024
DB = 8
NST = 16
T = 2048
BC = 2
NCORE = 8
G = 128
NH = 16
HS = 512
HD = 64
JT = T // 128
NJG = T // 512
CA = 512
NCA = T // CA
CB = 256
NCB = T // CB
TH = T // 2
CC = 512
NCC = T // CC
EPS = 1e-5
DEBUG_TAPS = False
PHASE_LIMIT = 99

ENGS = ("pe", "act", "dve", "gps", "sy")
PROG = {"pe": "pe", "act": "act", "dve": "dve", "gps": "gps", "sy": "spr"}
SEMS = ["pe", "act", "dve", "gps", "spr", "qsy", "qsc", "qgp"]
ENG_OF_SEM = {"pe": "pe", "act": "act", "dve": "dve", "gps": "gps",
              "spr": "sy", "qsy": "sy", "qsc": "act", "qgp": "gps"}


class Bld:

    def __init__(self):
        self.ops = {k: [] for k in ENGS}
        self.cnt = {s: 0 for s in SEMS}
        self.sems = {}
        self.slotsems = {}
        self.sem_owner = {"qsy": "sy", "qsc": "act", "qgp": "gps"}

    def slotsem(self, slot, eng):
        if slot not in self.slotsems:
            s = f"sl{len(self.slotsems)}"
            self.slotsems[slot] = s
            self.cnt[s] = 0
            self.sem_owner[s] = eng
        return self.slotsems[slot]

    def w(self, eng, tok):
        if tok is None:
            return
        s, v = tok
        if v <= 0:
            return
        if ENG_OF_SEM.get(s) == eng and s in ("pe", "act", "dve", "gps"):
            return
        self.ops[eng].append(lambda e, _s=s, _v=v: e.wait_ge(self.sems[_s], _v))

    def ws(self, eng, toks):
        best = {}
        for t in toks:
            if t is None:
                continue
            s, v = t
            if v > best.get(s, -1):
                best[s] = v
        for s, v in best.items():
            self.w(eng, (s, v))

    def wf(self, eng, tok):
        if tok is None:
            return
        s, v = tok
        if v <= 0:
            return
        self.ops[eng].append(lambda e, _s=s, _v=v: e.wait_ge(self.sems[_s], _v))

    def i(self, eng, fn, sem=None, k=1):
        if sem is None:
            sem = PROG[eng]
        self.cnt[sem] += k
        v = self.cnt[sem]
        self.ops[eng].append(
            lambda e, _f=fn, _s=sem, _k=k: _f(e).then_inc(self.sems[_s], _k))
        return (sem, v)

    def dma(self, eng, fn, waits=(), slot=None):
        q = self.slotsem(slot, eng) if slot else             {"sy": "qsy", "act": "qsc", "gps": "qgp"}[eng]
        best = {}
        for t in waits:
            if t is None:
                continue
            s, v = t
            if v > best.get(s, -1):
                best[s] = v
        for s, v in best.items():
            self.wf(eng, (s, v))
        return self.i(eng, fn, q, 16)

    def nop_inc(self, eng):
        self.i(eng, lambda e: e.nop())

    def barrier(self):
        for eng in ENGS:
            for s, owner in self.sem_owner.items():
                if owner == eng and self.cnt.get(s, 0) > 0:
                    self.wf(eng, (s, self.cnt[s]))
            self.nop_inc(eng)
        snap = {s: self.cnt[s] for s in ("pe", "act", "dve", "gps", "spr")}
        for eng in ENGS:
            for s, v in snap.items():
                if v <= 0 or ENG_OF_SEM[s] == eng:
                    continue
                self.ops[eng].append(
                    lambda e, _s=s, _v=v: e.wait_ge(self.sems[_s], _v))


class Buf:
    __slots__ = ("w", "r")

    def __init__(self):
        self.w = None
        self.r = []


class K:
    def __init__(self):
        self.nc = bass.Bass("TRN2", target_bir_lowering=False, debug=False)
        self.E = Bld()
        self.bufs = {}
        self.psum_b = [Buf() for _ in range(8)]

    def buf(self, name):
        if name not in self.bufs:
            self.bufs[name] = Buf()
        return self.bufs[name]

    def rd(self, eng, names):
        toks = []
        for nm in names:
            toks.append(self.buf(nm).w)
        self.E.ws(eng, toks)

    def rdone(self, names, tok):
        for nm in names:
            self.buf(nm).r.append(tok)

    def wr(self, eng, names):
        toks = []
        for nm in names:
            b = self.buf(nm)
            toks.append(b.w)
            toks.extend(b.r)
        self.E.ws(eng, toks)

    def wdone(self, names, tok):
        for nm in names:
            b = self.buf(nm)
            b.w = tok
            b.r = []

    def banks_of(self, off, length):
        b0 = off // 512
        b1 = (off + length - 1) // 512
        return list(range(b0, b1 + 1))

    def ps_wr(self, eng, off, length):
        toks = []
        for b in self.banks_of(off, length):
            toks.append(self.psum_b[b].w)
            toks.extend(self.psum_b[b].r)
        self.E.ws(eng, toks)

    def ps_wdone(self, off, length, tok):
        for b in self.banks_of(off, length):
            self.psum_b[b].w = tok
            self.psum_b[b].r = []

    def ps_rd(self, eng, off, length):
        toks = [self.psum_b[b].w for b in self.banks_of(off, length)]
        self.E.ws(eng, toks)

    def ps_rdone(self, off, length, tok):
        for b in self.banks_of(off, length):
            self.psum_b[b].r.append(tok)

    def build(self):
        nc = self.nc
        E = self.E

        def din(name, shape, dt=F32):
            return nc.dram_tensor(name, shape, dt, kind="ExternalInput").ap()

        self.U = din("U", [BC, T, D])
        self.WIN = din("WIN", [D, D]); self.BIN = din("BIN", [D])
        self.WG = din("WG", [D, D]); self.BG = din("BG", [D])
        self.WOUT = din("WOUT", [D, D]); self.BOUT = din("BOUT", [D])
        self.CONVW = din("CONVW", [D, 3]); self.CONVB = din("CONVB", [D])
        self.AMAT = din("AMAT", [D, NST])
        self.WB = din("WB", [D, NST]); self.BB = din("BB", [NST])
        self.WC = din("WC", [D, NST]); self.BCC = din("BCC", [NST])
        self.WDT = din("WDT", [D, D]); self.BDT = din("BDT", [D])
        self.WRT = din("WRT", [D, G])
        self.CEN = din("CEN", [NH, G])
        self.TEMP = din("TEMP", [128, 1])
        self.IEXP = din("IEXP", [NH, D, HS])
        self.OEXP = din("OEXP", [NH, HS, HD])
        self.IDN = din("IDN", [128, 128])
        self.ONESR = din("ONESR", [32, 128])
        self.PSEL = din("PSEL", [32, 16 * 64])
        self.OSEL = din("OSEL", [32, 16 * 128])

        self.OUT = nc.dram_tensor("OUT", [BC, T, D], F32, kind="ExternalOutput").ap()

        def scr(name, shape, dt=F32):
            kind = "ExternalOutput" if DEBUG_TAPS else "Internal"
            return nc.dram_tensor(name, shape, dt, kind=kind).ap()

        self.sc_xn = scr("sc_xn", [BC, DB, 128, T], F32R)
        self.sc_x = scr("sc_x", [BC, DB, 128, T], F32R)
        self.sc_dt = scr("sc_dt", [BC, DB, 128, T], F32)
        self.sc_dtx = scr("sc_dtx", [BC, DB, 128, T], F32)
        self.sc_z = scr("sc_z", [BC, DB, 128, T], F32)
        self.sc_yz = scr("sc_yz", [BC, DB, 128, T], F32R)
        self.sc_op = scr("sc_op", [BC, T, D], F32)
        self.sc_st = scr("sc_st", [BC, 4, 128, 32], F32)
        self.sc_wt = scr("sc_wt", [BC, 32, T], BF16)
        self.sc_xnb = scr("sc_xnb", [BC, 128, DB * T], BF16)
        self.sc_h = scr("sc_h", [16, 128, T], BF16)
        self.sc_brep = scr("sc_brep", [16, 128, T], BF16)
        self.sc_crep = scr("sc_crep", [16, 128, T], BF16)
        self.sc_dbx = scr("sc_dbx", [16, 128, T], F32)
        self.sc_bm = scr("sc_bm", [BC, 32, T], BF16)
        self.sc_hid = scr("sc_hid", [4, 128, T], BF16)
        self.sc_fp = scr("sc_fp", [128, T], F32)
        self.sc_wrp = scr("sc_wrp", [128, T], F32)
        self.sc_cm = scr("sc_cm", [BC, 32, T], BF16)

        off = [16640]

        def sb(name, shape, dt, at=None):
            a = off[0] if at is None else at
            a = (a + 31) // 32 * 32
            h = nc.alloc_sbuf_tensor_at(name, shape, dt, offset=a)
            if at is None:
                off[0] = a + int(np.prod(shape[1:])) * mybir.dt.size(dt)
            return h

        self.idn_f = sb("idn_f", [128, 128], F32)
        self.idn_r = sb("idn_r", [128, 128], F32R)
        self.idn_b = sb("idn_b", [128, 128], BF16)
        self.ones1r = sb("ones1r", [32, 128], F32R)
        self.onesg = sb("onesg", [128, 1], F32R)
        self.psel_b = sb("psel_b", [32, 16 * 64], BF16)
        self.osel_b = sb("osel_b", [32, 16 * 128], BF16)
        self.negA = sb("negA", [128, 128], F32)
        self.cwcol = sb("cwcol", [128, 24], F32)
        self.bin_c = sb("bin_c", [128, 8], F32)
        self.convb_c = sb("convb_c", [128, 8], F32)
        self.bg_c = sb("bg_c", [128, 8], F32)
        self.nbdt_c = sb("nbdt_c", [128, 8], F32)
        self.bout_c = sb("bout_c", [128, 8], F32)
        self.nbb_c = sb("nbb_c", [16, 1], F32)
        self.bc_c = sb("bc_c", [16, 1], F32)
        self.scol = sb("scol", [128, 1], F32)
        self.colpad = sb("colpad", [128, 48], F32)
        self.WrT = sb("WrT", [128, DB * 128], F32R)
        self.cnT = sb("cnT", [128, 16], F32R)
        self.cen_s = sb("cen_s", [32, G], F32)
        self.zeros32 = sb("zeros32", [32, 512], BF16)
        self.cnr = sb("cnr", [32, G], F32R)
        self.wTb = sb("wTb", [32, T], BF16)
        self.Bm = sb("Bm", [32, T], BF16)
        self.Cm = sb("Cm", [32, T], BF16)
        self.woutE = sb("woutE", [128, NH * 4 * HD], BF16)
        self.tmp1 = sb("tmp1", [1, 4], F32R)
        W_END = off[0]
        assert W_END < 60000, W_END

        OV = (W_END + 255) // 256 * 256

        o = [OV]

        def ov(name, shape, dt):
            a = (o[0] + 31) // 32 * 32
            h = nc.alloc_sbuf_tensor_at(name, shape, dt, offset=a)
            o[0] = a + int(np.prod(shape[1:])) * mybir.dt.size(dt)
            return h

        def ov_reset():
            o[0] = OV

        self.a1_u = ov("a1_u", [128, 4 * 1024], F32)
        self.a1_scr = ov("a1_scr", [128, 1024], F32)
        self.a1_xtm = ov("a1_xtm", [128, 4 * 1024], F32R)
        self.a1_xfm = ov("a1_xfm", [128, 2 * 512], F32R)
        self.a1_xnb = ov("a1_xnb", [128, DB * T], BF16)
        self.a1_cols = ov("a1_cols", [128, 4 * 8], F32)
        self.a1_zr = ov("a1_zr", [128, 512], F32R)
        self.a1_zsq = ov("a1_zsq", [128, 512], F32R)
        self.a1_znorm = ov("a1_znorm", [128, 512], F32R)
        self.a1_nrm = ov("a1_nrm", [1, 512], F32)
        self.a1_inv = ov("a1_inv", [32, 512], F32R)
        self.a1_lg = ov("a1_lg", [32, 512], F32R)
        self.a1_lgt = ov("a1_lgt", [128, 128], F32)
        self.a1_wtm = ov("a1_wtm", [128, 64], F32)
        self.a1_smc = ov("a1_smc", [128, 16], F32)
        self.a1_msk = ov("a1_msk", [128, 16], F32)
        self.a1_dbg = ov("a1_dbg", [128, JT * 8], F32)
        self.a1_e2 = ov("a1_e2", [128, 16], F32)
        self.a1_wi = ov("a1_wi", [128, 2 * 8 * 512], BF16)
        self.a1_hid = ov("a1_hid", [128, 4 * 512], BF16)
        self.a1_wrep = ov("a1_wrep", [64, 2048], F32)
        self.a1_fe = ov("a1_fe", [64, 2048], F32)
        self.a1_fo = ov("a1_fo", [64, 2048], F32)
        self.a1_ops = ov("a1_ops", [128, 2 * 512], F32)
        A1_END = o[0]

        ov_reset()
        self.a2_xn = ov("a2_xn", [128, 2 * DB * CA], F32R)
        self.a2_win = ov("a2_win", [128, 8 * 1024], F32R)
        self.a2_diag = ov("a2_diag", [128, 24 * 128], F32R)
        self.a2_xpre = ov("a2_xpre", [128, DB * (CA + 2)], F32R)
        self.a2_cpre = ov("a2_cpre", [128, 2 * DB * CA], F32)
        self.a2_wg = ov("a2_wg", [128, 8 * 1024], F32R)
        self.a2_z = ov("a2_z", [128, DB * CA], F32)
        A2A_END = o[0]

        ov_reset()
        self.b2_x = ov("b2_x", [128, 2 * DB * CB], F32R)
        self.b2_sg = ov("b2_sg", [128, 2 * DB * CB], F32)
        self.b2_dtx = ov("b2_dtx", [128, 2 * DB * CB], F32)
        self.b2_z = ov("b2_z", [128, 2 * DB * CB], F32)
        self.b2_wdt = ov("b2_wdt", [128, 8 * 1024], F32R)
        self.b2_wb = ov("b2_wb", [128, 8 * 16], F32R)
        self.b2_wc = ov("b2_wc", [128, 8 * 16], F32R)
        A2B_END = o[0]

        ov_reset()
        self.s_brep = ov("s_brep", [128, 4 * T], BF16)
        self.s_crep = ov("s_crep", [128, 4 * T], BF16)
        self.s_ysb = ov("s_ysb", [128, DB * T], F32R)
        self.s_dt = ov("s_dt", [128, 2 * T], F32)
        self.s_dtx = ov("s_dtx", [128, 2 * T], F32)
        self.s_dbx = ov("s_dbx", [128, 3 * TH], F32)
        self.s_h = ov("s_h", [128, 3 * TH], BF16)
        self.s_hc = ov("s_hc", [128, 3 * TH], BF16)
        self.s_z = ov("s_z", [128, 2 * TH], F32)
        self.s_yz = ov("s_yz", [128, 2 * TH], F32R)
        B_END = o[0]

        ov_reset()
        self.c_yz = ov("c_yz", [128, 2 * DB * 512], F32R)
        self.c_wout = ov("c_wout", [128, 8 * 1024], F32R)
        self.c_attn = ov("c_attn", [128, 2 * 512], F32)
        self.c_uo = ov("c_uo", [128, 2 * 4 * 1024], F32)
        self.c_op = ov("c_op", [128, 2 * 4 * 1024], F32)
        C_END = o[0]

        assert max(A1_END, A2A_END, A2B_END, B_END, C_END) <= 229312, (
            A1_END, A2A_END, A2B_END, B_END, C_END)

        self.ps = nc.alloc_psum_tensor("ps", [128, 4096], F32)

        self.preamble()
        for b in range(BC):
            self.phase_a1(b)
            self.E.barrier()
            if PHASE_LIMIT >= 2:
                self.phase_a2a(b)
                self.E.barrier()
            if PHASE_LIMIT >= 3:
                self.phase_a2b(b)
                self.phase_a2b_taps(b)
                self.E.barrier()
            if PHASE_LIMIT >= 4:
                self.phase_b(b)
                self.E.barrier()
            if PHASE_LIMIT >= 5:
                self.phase_c(b)
                self.E.barrier()

        self.E.barrier()

        import contextlib
        stack = contextlib.ExitStack()
        for s in list(SEMS) + list(self.E.slotsems.values()):
            self.E.sems[s] = stack.enter_context(nc.semaphore(s))
        with stack:
            with nc.Block() as block:
                @block.tensor
                def _(e):
                    for f in E.ops["pe"]:
                        f(e)

                @block.scalar
                def _(e):
                    for f in E.ops["act"]:
                        f(e)

                @block.vector
                def _(e):
                    for f in E.ops["dve"]:
                        f(e)

                @block.gpsimd
                def _(e):
                    for f in E.ops["gps"]:
                        f(e)

                @block.sync
                def _(e):
                    for f in E.ops["sy"]:
                        f(e)
        return nc

    def preamble(self):
        E, nc = self.E, self.nc
        ps = self.ps.ap()

        t_idn = E.dma("sy", lambda e: e.dma_start(out=self.idn_f.ap(), in_=self.IDN), slot="c_idn")
        t_idnr = E.dma("gps", lambda e: e.dma_start(out=self.idn_r.ap(), in_=self.IDN), slot="c_idnr")
        t_idnb = E.dma("gps", lambda e: e.dma_start(out=self.idn_b.ap(), in_=self.IDN), slot="c_idnb")
        t_o1r = E.dma("gps", lambda e: e.dma_start(out=self.ones1r.ap(), in_=self.ONESR), slot="c_o1r")
        t_og = E.dma("gps", lambda e: e.dma_start(
            out=self.onesg.ap(), in_=self.ONESR[0:1, :].rearrange("o p -> p o"),
            allow_slow_non_contiguous=True), slot="c_og")
        tz0 = None
        for tns in (self.wTb, self.Bm, self.Cm):
            tz0 = E.i("dve", lambda e, _t=tns: e.memset(
                _t.ap().bitcast(F32), 0.0))
        self.buf("padzero").w = tz0
        tz1 = E.i("dve", lambda e: e.memset(self.zeros32.ap(), 0.0))
        self.buf("zeros32").w = tz1
        tz2 = E.i("dve", lambda e: e.memset(self.colpad.ap()[:, 25:26], 1e-24))
        self.buf("eps24").w = tz2
        t_ps = E.dma("gps", lambda e: e.dma_start(out=self.psel_b.ap(),
                                                  in_=self.PSEL), slot="c_ps")
        self.buf("pairs").w = t_ps
        t_os = E.dma("gps", lambda e: e.dma_start(out=self.osel_b.ap(),
                                                  in_=self.OSEL), slot="c_os")
        self.buf("osel").w = t_os

        t_a = E.dma("sy", lambda e: e.dma_start(
            out=self.a1_scr.ap()[:, 0:128].rearrange("p (db n) -> p db n", db=8),
            in_=self.AMAT.rearrange("(db p) n -> p db n", p=128)), slot="c_a")
        E.ws("dve", [t_a])
        t_na = E.i("dve", lambda e: e.tensor_scalar_mul(
            self.negA.ap(), self.a1_scr.ap()[:, 0:128], -1.0))
        self.buf("negA").w = t_na

        def loadcol(dst, src, neg=False, n16=False, nm=""):
            if n16:
                tok = E.dma("sy", lambda e, _d=dst, _s=src: e.dma_start(
                    out=_d.ap(), in_=_s.rearrange("(n o) -> n o", o=1)),
                    slot="c_" + nm)
            else:
                tok = E.dma("sy", lambda e, _d=dst, _s=src: e.dma_start(
                    out=_d.ap(), in_=_s.rearrange("(db p) -> p db", p=128),
                    allow_slow_non_contiguous=True), slot="c_" + nm)
            if neg:
                E.ws("dve", [tok])
                tok = E.i("dve", lambda e, _d=dst: e.tensor_scalar_mul(
                    _d.ap(), _d.ap(), -1.0))
            return tok

        self.buf("bin").w = loadcol(self.bin_c, self.BIN, nm="bin")
        self.buf("convb").w = loadcol(self.convb_c, self.CONVB, nm="cvb")
        self.buf("bg").w = loadcol(self.bg_c, self.BG, nm="bg")
        self.buf("nbdt").w = loadcol(self.nbdt_c, self.BDT, neg=True, nm="bdt")
        self.buf("bout").w = loadcol(self.bout_c, self.BOUT, nm="bo")
        self.buf("nbb").w = loadcol(self.nbb_c, self.BB, neg=True, n16=True, nm="bb")
        self.buf("bc").w = loadcol(self.bc_c, self.BCC, n16=True, nm="bc")

        t_cw = E.dma("sy", lambda e: e.dma_start(
            out=self.cwcol.ap()[:, 0:24].rearrange("p (db k) -> p db k", db=8),
            in_=self.CONVW.rearrange("(db p) k -> p db k", p=128)), slot="c_cw")
        self.buf("cw").w = t_cw

        t_wrt = E.dma("gps", lambda e: e.dma_start(
            out=self.WrT.ap().rearrange("p (db g) -> p db g", db=8),
            in_=self.WRT.rearrange("(db p) g -> p db g", p=128)), slot="c_wrt")
        self.buf("WrT").w = t_wrt

        t_we = E.dma("gps", lambda e: e.dma_start(
            out=self.woutE.ap().rearrange("p (n hb e) -> p n hb e", n=16, hb=4),
            in_=self.OEXP.rearrange("n (hb p) e -> p n hb e", p=128)), slot="c_we")
        self.buf("woutE").w = t_we

        E.ws("dve", [self.buf("zeros32").w])
        tcz = E.i("dve", lambda e: e.memset(self.cen_s.ap(), 0.0))
        t_ce = E.dma("sy", lambda e: e.dma_start(
            out=self.cen_s.ap()[0:16, :], in_=self.CEN), waits=[tcz],
            slot="c_cen")
        E.ws("act", [t_ce, t_na])
        t1 = E.i("act", lambda e: e.activation(
            out=self.a1_scr.ap()[0:32, 0:128], in_=self.cen_s.ap(),
            func=AF.Square))
        E.ws("dve", [t1])
        t2 = E.i("dve", lambda e: e.reduce_sum(
            self.colpad.ap()[0:32, 0:1], self.a1_scr.ap()[0:32, 0:128], AX.X))
        E.ws("act", [t2, self.buf("eps24").w])
        t3 = E.i("act", lambda e: e.activation(
            out=self.colpad.ap()[0:32, 1:2], in_=self.colpad.ap()[0:32, 0:1],
            func=AF.Ln, bias=self.colpad.ap()[0:32, 25:26]))
        t4 = E.i("act", lambda e: e.activation(
            out=self.colpad.ap()[0:32, 2:3], in_=self.colpad.ap()[0:32, 1:2],
            func=AF.Exp, scale=-0.5))
        E.ws("dve", [t4])
        E.wf("dve", t2)
        t5 = E.i("dve", lambda e: e.tensor_scalar(
            out=self.cnr.ap(),
            in0=self.cen_s.ap(), scalar1=self.colpad.ap()[0:32, 2:3],
            scalar2=None, op0=OP.mult))
        E.ws("pe", [t5, t_idnr])
        self.ps_wr("pe", 0, 32)
        tt = E.i("pe", lambda e: e.transpose(
            ps[:, 0:32].bitcast(F32R),
            self.cnr.ap(),
            self.idn_r.ap()[0:32, 0:32]))
        self.ps_wdone(0, 32, tt)
        self.ps_rd("act", 0, 32)
        tc = E.i("act", lambda e: e.activation(
            out=self.cnT.ap(), in_=ps[:, 0:16], func=AF.Identity))
        self.ps_rdone(0, 32, tc)
        self.buf("cnT").w = tc

        t_t = E.dma("sy", lambda e: e.dma_start(
            out=self.colpad.ap()[:, 8:9], in_=self.TEMP), slot="c_temp")
        E.ws("act", [t_t])
        t6 = E.i("act", lambda e: e.activation(
            out=self.colpad.ap()[:, 9:10], in_=self.colpad.ap()[:, 8:9],
            func=AF.Exp))
        E.ws("dve", [t6])
        t9 = E.i("dve", lambda e: e.tensor_scalar(
            out=self.scol.ap(), in0=self.colpad.ap()[:, 9:10],
            scalar1=float(1.0 / np.sqrt(G)), scalar2=None, op0=OP.mult))
        self.buf("scol").w = t9
        t_ep = E.i("dve", lambda e: e.memset(self.colpad.ap()[:, 24:25], EPS))
        self.buf("epscol").w = t_ep
        self.buf("idn").w = t_idn
        self.buf("idnr").w = t_idnr
        self.buf("idnb").w = t_idnb
        self.buf("onesg").w = t_og
        self.E.barrier()

    def phase_a1(self, b):
        E, nc = self.E, self.nc
        ps = self.ps.ap()
        ucols = self.a1_cols.ap()

        self.wr("dve", ["a1lg", "a1inv"])
        tzp = E.i("dve", lambda e: e.tensor_scalar(
            out=self.a1_lg.ap(), in0=self.zeros32.ap(), scalar1=1.0,
            scalar2=None, op0=OP.mult))
        tzp = E.i("dve", lambda e: e.tensor_scalar(
            out=self.a1_inv.ap(), in0=self.zeros32.ap(), scalar1=1.0,
            scalar2=None, op0=OP.mult))
        self.wdone(["a1lg", "a1inv"], tzp)

        for jg in range(NJG):
            t_sq = t_s = None
            u_toks = []
            for j4 in range(4):
                jt = jg * 4 + j4
                us = self.a1_u.ap()[:, (jt % 4) * 1024:(jt % 4) * 1024 + 1024]
                nm_u = f"a1u{jt % 4}"
                t_u = E.dma("sy", lambda e, _o=us, _b=b, _jt=jt: e.dma_start(
                    out=_o, in_=self.U[_b, _jt * 128:(_jt + 1) * 128, :]),
                    waits=[self.buf(nm_u).w] + self.buf(nm_u).r, slot=nm_u)
                self.buf(nm_u).w = t_u
                self.buf(nm_u).r = []
                u_toks.append((us, nm_u))
                E.ws("act", [t_u])
                self.wr("act", [f"a1scr"])
                t_sq = E.i("act", lambda e, _u=us, _j=j4: e.activation(
                    out=self.a1_scr.ap(), in_=_u, func=AF.Square))
                self.wdone(["a1scr"], t_sq)
                E.ws("dve", [t_u])
                t_s = E.i("dve", lambda e, _u=us, _j=j4: e.reduce_sum(
                    ucols[:, 4 + _j:5 + _j], _u, AX.X))
                self.rdone([nm_u], t_s)
                E.ws("dve", [t_sq])
                t_sq = E.i("dve", lambda e, _j=j4: e.reduce_sum(
                    ucols[:, 0 + _j:1 + _j], self.a1_scr.ap(), AX.X))
                self.rdone(["a1scr"], t_sq)
            E.wf("dve", t_s)
            t_m = E.i("dve", lambda e: e.tensor_scalar_mul(
                ucols[:, 8:12], ucols[:, 4:8], 1.0 / D))
            E.ws("dve", [t_sq])
            E.wf("dve", t_sq)
            t_m2 = E.i("dve", lambda e: e.tensor_scalar_mul(
                ucols[:, 12:16], ucols[:, 0:4], 1.0 / D))
            E.wf("dve", t_m)
            t_ms = E.i("dve", lambda e: e.tensor_mul(
                ucols[:, 16:20], ucols[:, 8:12], ucols[:, 8:12]))
            E.wf("dve", t_ms)
            t_v = E.i("dve", lambda e: e.tensor_sub(
                ucols[:, 20:24], ucols[:, 12:16], ucols[:, 16:20]))
            E.ws("act", [t_v])
            t_l = E.i("act", lambda e: e.activation(
                out=ucols[:, 24:28], in_=ucols[:, 20:24], func=AF.Ln,
                bias=self.colpad.ap()[:, 24:25]))
            t_is = E.i("act", lambda e: e.activation(
                out=ucols[:, 24:28], in_=ucols[:, 24:28], func=AF.Exp,
                scale=-0.5))
            E.ws("dve", [t_is])
            t_nm = E.i("dve", lambda e: e.tensor_mul(
                ucols[:, 28:32], ucols[:, 8:12], ucols[:, 24:28]))
            E.wf("dve", t_nm)
            t_nm2 = E.i("dve", lambda e: e.tensor_scalar_mul(
                ucols[:, 28:32], ucols[:, 28:32], -1.0))
            for j4 in range(4):
                us, nm_u = u_toks[j4]
                xs = self.a1_xtm.ap()[:, j4 * 1024:(j4 + 1) * 1024]
                E.ws("act", [t_nm2])
                self.wr("act", [f"a1xtm{j4}"])
                t_xn = E.i("act", lambda e, _u=us, _j=j4, _x=xs: e.activation(
                    out=_x, in_=_u, func=AF.Identity,
                    scale=ucols[:, 24 + _j:25 + _j],
                    bias=ucols[:, 28 + _j:29 + _j]))
                self.wdone([f"a1xtm{j4}"], t_xn)
                self.rdone([nm_u], t_xn)
            if DEBUG_TAPS:
                E.ws("dve", [t_is, t_sq])
                E.wf("dve", t_nm2)
                tdbg = E.i("dve", lambda e, _jg=jg: e.tensor_copy(
                    out=self.a1_dbg.ap()[:, _jg * 32:(_jg + 1) * 32],
                    in_=ucols[:, 0:32]))
                E.dma("sy", lambda e, _jg=jg, _b=b: e.dma_start(
                    out=self.sc_st[_b, _jg],
                    in_=self.a1_dbg.ap()[:, _jg * 32:(_jg + 1) * 32]),
                    waits=[tdbg])

            for db in range(DB):
                bank = (db % 2) * 512
                self.ps_wr("pe", bank, 512)
                E.ws("pe", [self.buf("idnr").w])
                tts = []
                for j4 in range(4):
                    self.rd("pe", [f"a1xtm{j4}"])
                    tt = E.i("pe", lambda e, _j=j4, _db=db, _bk=bank: e.transpose(
                        ps[:, _bk + _j * 128:_bk + (_j + 1) * 128].bitcast(F32R),
                        self.a1_xtm.ap()[:, _j * 1024 + _db * 128:
                                         _j * 1024 + _db * 128 + 128],
                        self.idn_r.ap()))
                    tts.append(tt)
                    self.rdone([f"a1xtm{j4}"], tt)
                self.ps_wdone(bank, 512, tts[-1])
                slot = db % 2
                xf = self.a1_xfm.ap()[:, slot * 512:slot * 512 + 512]
                self.ps_rd("act", bank, 512)
                self.wr("act", [f"a1xfm{slot}"])
                t_ev = E.i("act", lambda e, _x=xf, _bk=bank: e.activation(
                    out=_x, in_=ps[:, _bk:_bk + 512], func=AF.Identity))
                self.ps_rdone(bank, 512, t_ev)
                self.wdone([f"a1xfm{slot}"], t_ev)
                E.ws("dve", [t_ev])
                t_cb = E.i("dve", lambda e, _x=xf, _db=db, _jg=jg: e.tensor_copy(
                    out=self.a1_xnb.ap()[:, _db * T + _jg * 512:
                                         _db * T + _jg * 512 + 512], in_=_x))
                self.rdone([f"a1xfm{slot}"], t_cb)
                self.buf("xnb").w = t_cb
                t_sp = E.dma("sy", lambda e, _x=xf, _b=b, _db=db, _jg=jg:
                             e.dma_start(out=self.sc_xn[_b, _db, :,
                                                        _jg * 512:(_jg + 1) * 512],
                                         in_=_x), waits=[t_ev],
                             slot=f"a1xfmd{slot}")
                self.rdone([f"a1xfm{slot}"], t_sp)
                E.ws("pe", [t_ev, self.buf("WrT").w])
                if db == 0:
                    self.ps_wr("pe", 1024, 512)
                t_r = E.i("pe", lambda e, _x=xf, _db=db: e.matmul(
                    ps[:, 1024:1536],
                    self.WrT.ap()[:, _db * 128:(_db + 1) * 128], _x,
                    start=(_db == 0), stop=(_db == DB - 1)))
                self.rdone([f"a1xfm{slot}"], t_r)
                if db == DB - 1:
                    self.ps_wdone(1024, 512, t_r)

            self.ps_rd("act", 1024, 512)
            self.wr("act", ["a1zr"])
            t_zr = E.i("act", lambda e: e.activation(
                out=self.a1_zr.ap(), in_=ps[:, 1024:1536], func=AF.Identity))
            self.ps_rdone(1024, 512, t_zr)
            self.wdone(["a1zr"], t_zr)
            t_zq = E.i("act", lambda e: e.activation(
                out=self.a1_zsq.ap(), in_=self.a1_zr.ap(), func=AF.Square))
            E.ws("pe", [t_zq, self.buf("onesg").w])
            self.ps_wr("pe", 1536, 512)
            t_ns = E.i("pe", lambda e: e.matmul(
                ps[0:1, 1536:2048], self.onesg.ap(), self.a1_zsq.ap(),
                start=True, stop=True))
            self.ps_wdone(1536, 512, t_ns)
            self.ps_rd("act", 1536, 512)
            t_nr = E.i("act", lambda e: e.activation(
                out=self.a1_nrm.ap(), in_=ps[0:1, 1536:2048], func=AF.Ln))
            self.ps_rdone(1536, 512, t_nr)
            t_iv = E.i("act", lambda e: e.activation(
                out=self.a1_inv.ap()[0:1, :], in_=self.a1_nrm.ap(),
                func=AF.Exp, scale=-0.5))
            E.ws("pe", [t_iv, self.buf("a1inv").w])
            self.ps_wr("pe", 1536, 512)
            t_ir = E.i("pe", lambda e: e.matmul(
                ps[:, 1536:2048], self.ones1r.ap(), self.a1_inv.ap(),
                start=True, stop=True))
            self.ps_wdone(1536, 512, t_ir)
            self.ps_rd("dve", 1536, 512)
            E.ws("dve", [t_zr])
            t_zn = E.i("dve", lambda e: e.tensor_mul(
                self.a1_znorm.ap(), self.a1_zr.ap(), ps[:, 1536:2048]))
            self.ps_rdone(1536, 512, t_zn)
            E.ws("pe", [t_zn, self.buf("cnT").w])
            self.ps_wr("pe", 1024, 512)
            t_lg = E.i("pe", lambda e: e.matmul(
                ps[0:16, 1024:1536], self.cnT.ap(), self.a1_znorm.ap(),
                start=True, stop=True))
            self.ps_wdone(1024, 512, t_lg)
            self.ps_rd("act", 1024, 512)
            t_le = E.i("act", lambda e: e.activation(
                out=self.a1_lg.ap()[0:16, :], in_=ps[0:16, 1024:1536],
                func=AF.Identity))
            self.ps_rdone(1024, 512, t_le)
            E.ws("pe", [t_le, self.buf("a1lg").w])
            self.ps_wr("pe", 1024, 128)
            tl = None
            for s in range(4):
                tl = E.i("pe", lambda e, _s=s: e.transpose(
                    ps[:, 1024 + _s * 32:1024 + (_s + 1) * 32].bitcast(F32R),
                    self.a1_lg.ap()[:, _s * 128:(_s + 1) * 128],
                    self.idn_r.ap()[0:32, 0:32]))
            self.ps_wdone(1024, 128, tl)
            self.ps_rd("act", 1024, 128)
            t_lt = E.i("act", lambda e: e.activation(
                out=self.a1_lgt.ap(), in_=ps[:, 1024:1152], func=AF.Identity))
            self.ps_rdone(1024, 128, t_lt)
            for s in range(4):
                lg = self.a1_lgt.ap()[:, s * 32:s * 32 + 16]
                smc = self.a1_smc.ap()
                msk = self.a1_msk.ap()
                e2 = self.a1_e2.ap()
                wv = self.a1_wtm.ap()[:, s * 16:(s + 1) * 16]
                E.ws("dve", [t_lt])
                t_mx = E.i("dve", lambda e, _l=lg, _c=smc: e.reduce_max(
                    _c[:, 0:1], _l, AX.X))
                E.wf("dve", t_mx)
                t_bi = E.i("dve", lambda e, _c=smc: e.tensor_scalar(
                    out=_c[:, 1:2], in0=_c[:, 0:1],
                    scalar1=self.scol.ap(), scalar2=-1.0, op0=OP.mult,
                    op1=OP.mult))
                E.ws("act", [t_bi, t_lt])
                t_ex = E.i("act", lambda e, _l=lg, _c=smc: e.activation(
                    out=_l, in_=_l, func=AF.Exp, scale=self.scol.ap(),
                    bias=_c[:, 1:2]))
                E.ws("dve", [t_ex])
                t_sm = E.i("dve", lambda e, _l=lg, _c=smc: e.reduce_sum(
                    _c[:, 2:3], _l, AX.X))
                E.ws("act", [t_sm])
                t_ln = E.i("act", lambda e, _c=smc: e.activation(
                    out=_c[:, 3:4], in_=_c[:, 2:3], func=AF.Ln))
                t_rc = E.i("act", lambda e, _c=smc: e.activation(
                    out=_c[:, 3:4], in_=_c[:, 3:4], func=AF.Exp, scale=-1.0))
                t_m1 = E.i("dve", lambda e, _l=lg, _c=smc: e.reduce_max(
                    _c[:, 4:5], _l, AX.X))
                E.wf("dve", t_m1)
                t_lt2 = E.i("dve", lambda e, _l=lg, _c=smc, _m=msk:
                            e.tensor_scalar(
                    out=_m, in0=_l, scalar1=_c[:, 4:5], scalar2=None,
                    op0=OP.is_lt))
                E.wf("dve", t_lt2)
                t_p2 = E.i("dve", lambda e, _l=lg, _m=msk, _e2=e2:
                           e.tensor_mul(_e2, _l, _m))
                E.wf("dve", t_p2)
                t_m2 = E.i("dve", lambda e, _c=smc, _e2=e2: e.reduce_max(
                    _c[:, 5:6], _e2, AX.X))
                E.wf("dve", t_m2)
                t_ge = E.i("dve", lambda e, _l=lg, _c=smc, _m=msk:
                           e.tensor_scalar(
                    out=_m, in0=_l, scalar1=_c[:, 5:6], scalar2=None,
                    op0=OP.is_ge))
                E.wf("dve", t_ge)
                t_wu = E.i("dve", lambda e, _l=lg, _m=msk, _e2=e2:
                           e.tensor_mul(_e2, _l, _m))
                E.ws("dve", [t_rc])
                E.wf("dve", t_wu)
                t_w = E.i("dve", lambda e, _w=wv, _c=smc, _e2=e2:
                          e.tensor_scalar(
                    out=_w, in0=_e2, scalar1=_c[:, 3:4], scalar2=None,
                    op0=OP.mult))
                self.buf(f"a1w{s}").w = t_w
            self.ps_wr("pe", 1024, 512)
            tws = []
            for s in range(4):
                E.ws("pe", [self.buf(f"a1w{s}").w, self.buf("idn").w])
                tw = E.i("pe", lambda e, _s=s: e.transpose(
                    ps[0:16, 1024 + _s * 128:1024 + (_s + 1) * 128],
                    self.a1_wtm.ap()[:, _s * 16:(_s + 1) * 16],
                    self.idn_f.ap()))
                tws.append(tw)
            self.ps_wdone(1024, 512, tws[-1])
            self.ps_rd("act", 1024, 512)
            self.wr("act", [f"wTb{jg}"])
            t_wt = E.i("act", lambda e, _jg=jg: e.activation(
                out=self.wTb.ap()[0:16, _jg * 512:(_jg + 1) * 512],
                in_=ps[0:16, 1024:1536], func=AF.Identity))
            self.ps_rdone(1024, 512, t_wt)
            self.wdone([f"wTb{jg}"], t_wt)

        if DEBUG_TAPS:
            E.dma("sy", lambda e, _b=b: e.dma_start(
                out=self.sc_wt[_b], in_=self.wTb.ap()),
                waits=[self.buf(f"wTb{j}").w for j in range(4)])
            E.dma("sy", lambda e, _b=b: e.dma_start(
                out=self.sc_xnb[_b], in_=self.a1_xnb.ap()),
                waits=[self.buf("xnb").w])

        def wi_load(n, slot):
            return E.dma("gps", lambda e, _n=n, _s=slot: e.dma_start(
                out=self.a1_wi.ap()[:, _s * 4096:(_s + 1) * 4096].rearrange("p (k h) -> p k h", k=8),
                in_=self.IEXP[_n].rearrange("(k p) h -> p k h", p=128)),
                waits=[self.buf(f"a1wi{slot}").w] + self.buf(f"a1wi{slot}").r,
                slot=f"a1wi{slot}")

        tok = wi_load(0, 0)
        self.buf("a1wi0").w = tok
        self.buf("a1wi0").r = []
        for n in range(NH):
            slot = n % 2
            if n + 1 < NH:
                nx = wi_load(n + 1, (n + 1) % 2)
                self.buf(f"a1wi{(n + 1) % 2}").w = nx
                self.buf(f"a1wi{(n + 1) % 2}").r = []
            wi = self.a1_wi.ap()[:, slot * 4096:(slot + 1) * 4096]
            for hb in range(4):
                for tc in range(4):
                    self.ps_wr("pe", 2048 + tc * 512, 512)
                E.ws("pe", [self.buf(f"a1wi{slot}").w, self.buf("xnb").w])
                tm = None
                for k in range(8):
                    for tc in range(4):
                        tm = E.i("pe", lambda e, _k=k, _hb=hb, _tc=tc, _w=wi:
                                 e.matmul(
                            ps[:, 2048 + _tc * 512:2048 + _tc * 512 + 512],
                            _w[:, _k * 512 + _hb * 128:_k * 512 + _hb * 128 + 128],
                            self.a1_xnb.ap()[:, _k * T + _tc * 512:
                                             _k * T + _tc * 512 + 512],
                            start=(_k == 0), stop=(_k == 7)))
                self.rdone([f"a1wi{slot}"], tm)
                for tc in range(4):
                    self.ps_wdone(2048 + tc * 512, 512, tm)
                self.ps_rd("act", 2048, 2048)
                self.wr("act", [f"a1hid{tc}" for tc in range(4)])
                tg = E.i("act", lambda e: e.activation(
                    out=self.a1_hid.ap(), in_=ps[:, 2048:4096],
                    func=AF.Gelu))
                self.ps_rdone(2048, 2048, tg)
                self.wdone([f"a1hid{tc}" for tc in range(4)], tg)
                tgs = [tg]
                if DEBUG_TAPS and b == 0 and n == 0:
                    tdh = E.dma("sy", lambda e, _hb=hb: e.dma_start(
                        out=self.sc_hid[_hb], in_=self.a1_hid.ap()),
                        waits=[tg])
                    for tc in range(4):
                        self.rdone([f"a1hid{tc}"], tdh)
                E.ws("pe", [self.buf("woutE").w] + tgs)
                for tc in range(4):
                    if hb == 0:
                        self.ps_wr("pe", tc * 512, 512)
                for tc in range(4):
                    hid = self.a1_hid.ap()[:, tc * 512:(tc + 1) * 512]
                    tpo = E.i("pe", lambda e, _h=hid, _n=n, _hb=hb, _tc=tc:
                              e.matmul(
                        ps[0:64, _tc * 512:_tc * 512 + 512],
                        self.woutE.ap()[:, (_n * 4 + _hb) * 64:
                                        (_n * 4 + _hb + 1) * 64],
                        _h, start=(_hb == 0), stop=(_hb == 3)))
                    self.rdone([f"a1hid{tc}"], tpo)
                    if hb == 3:
                        self.ps_wdone(tc * 512, 512, tpo)
            fdst = self.a1_fe if n % 2 == 0 else self.a1_fo
            fnm = "a1fe" if n % 2 == 0 else "a1fo"
            for tc in range(4):
                wbank = 2048 + (tc % 2) * 512
                self.ps_wr("pe", wbank, 512)
                E.ws("pe", [self.buf("pairs").w, self.buf(f"wTb{tc}").w])
                trw = E.i("pe", lambda e, _n=n, _tc=tc, _wb=wbank: e.matmul(
                    ps[0:64, _wb:_wb + 512],
                    self.psel_b.ap()[:, _n * 64:(_n + 1) * 64],
                    self.wTb.ap()[:, _tc * 512:(_tc + 1) * 512],
                    start=True, stop=True))
                self.ps_wdone(wbank, 512, trw)
                ws = tc % 2
                wrp = self.a1_wrep.ap()[0:64, tc * 512:(tc + 1) * 512]
                self.ps_rd("act", wbank, 512)
                if tc < 2:
                    self.wr("act", ["a1wrep"])
                tew = E.i("act", lambda e, _w=wrp, _wb=wbank: e.activation(
                    out=_w, in_=ps[0:64, _wb:_wb + 512], func=AF.Identity))
                self.ps_rdone(wbank, 512, tew)
                if tc == 3:
                    self.wdone(["a1wrep"], tew)
                obank = tc * 512
                ffn = fdst.ap()[:, tc * 512:(tc + 1) * 512]
                self.ps_rd("dve", obank, 512)
                E.ws("dve", [tew])
                if tc == 0:
                    self.wr("dve", [fnm])
                tsc = E.i("dve", lambda e, _f=ffn, _w=wrp, _ob=obank:
                          e.tensor_mul(_f, ps[0:64, _ob:_ob + 512], _w))
                self.ps_rdone(obank, 512, tsc)
                self.rdone(["a1wrep"], tsc)
                if tc == 3:
                    self.wdone([fnm], tsc)
                if DEBUG_TAPS and b == 0 and n <= 1:
                    tdf = E.dma("sy", lambda e, _f=ffn, _tc=tc, _n=n:
                                e.dma_start(
                        out=self.sc_fp.rearrange(
                            "(h p) t -> h p t", h=2)[_n % 2, :,
                                                     _tc * 512:(_tc + 1) * 512],
                        in_=_f), waits=[tsc])
                    self.rdone([fnm], tdf)

            if n % 2 == 1:
                pr = n // 2
                for tc in range(4):
                    pbank = 3072 + (tc % 2) * 512
                    self.ps_wr("pe", pbank, 512)
                    self.rd("pe", ["a1fe", "a1fo"])
                    E.ws("pe", [self.buf("idn").w])
                    tp = None
                    for j4 in range(4):
                        for hh, fsrc in ((0, self.a1_fe), (1, self.a1_fo)):
                            tp = E.i("pe", lambda e, _f=fsrc, _j=j4, _pb=pbank,
                                     _tc=tc, _hh=hh: e.transpose(
                                ps[:, _pb + _j * 128 + _hh * 64:
                                   _pb + _j * 128 + _hh * 64 + 64],
                                _f.ap()[:, _tc * 512 + _j * 128:
                                        _tc * 512 + (_j + 1) * 128],
                                self.idn_f.ap()[0:64, 0:64]))
                    self.rdone(["a1fe", "a1fo"], tp)
                    self.ps_wdone(pbank, 512, tp)
                    osl = tc % 2
                    ost = self.a1_ops.ap()[:, osl * 512:(osl + 1) * 512]
                    self.ps_rd("act", pbank, 512)
                    self.wr("act", [f"a1ops{osl}"])
                    teo = E.i("act", lambda e, _o=ost, _pb=pbank: e.activation(
                        out=_o, in_=ps[:, _pb:_pb + 512], func=AF.Identity))
                    self.ps_rdone(pbank, 512, teo)
                    self.wdone([f"a1ops{osl}"], teo)
                    tdo = E.dma("sy", lambda e, _o=ost, _b=b, _pr=pr, _tc=tc:
                                e.dma_start(
                        out=self.sc_op[_b, _tc * 512:(_tc + 1) * 512,
                                       _pr * 128:(_pr + 1) * 128]
                        .rearrange("(j p) c -> p j c", p=128),
                        in_=_o.rearrange("p (j c) -> p j c", c=128)),
                        waits=[teo], slot=f"a1ops{osl}")
                    self.rdone([f"a1ops{osl}"], tdo)

    def phase_a2a(self, b):
        E = self.E
        ps = self.ps.ap()

        t_w = E.dma("gps", lambda e: e.dma_start(
            out=self.a2_win.ap().rearrange("p (k e) -> p k e", k=8),
            in_=self.WIN.rearrange("(k p) e -> p k e", p=128)),
            waits=[self.buf("a2win").w] + self.buf("a2win").r, slot="a2win")
        self.buf("a2win").w = t_w
        self.buf("a2win").r = []
        t_wg = E.dma("gps", lambda e: e.dma_start(
            out=self.a2_wg.ap().rearrange("p (k e) -> p k e", k=8),
            in_=self.WG.rearrange("(k p) e -> p k e", p=128)),
            waits=[self.buf("a2wg").w] + self.buf("a2wg").r, slot="a2wg")
        self.buf("a2wg").w = t_wg
        self.buf("a2wg").r = []
        E.ws("dve", [self.buf("cw").w, self.buf("idnr").w])
        self.wr("dve", ["a2diag"])
        td = None
        for db in range(DB):
            for tap in range(3):
                idx = db * 3 + tap
                td = E.i("dve", lambda e, _i=idx: e.tensor_scalar(
                    out=self.a2_diag.ap()[:, _i * 128:(_i + 1) * 128],
                    in0=self.idn_r.ap(), scalar1=self.cwcol.ap()[:, _i:_i + 1],
                    scalar2=None, op0=OP.mult))
        self.wdone(["a2diag"], td)
        t_hz2 = None
        for _db in range(DB):
            for _c in (0, CA + 1):
                t_hz2 = E.i("dve", lambda e, _d=_db, _cc=_c: e.tensor_scalar(
                    out=self.a2_xpre.ap()[:, _d * (CA + 2) + _cc:
                                          _d * (CA + 2) + _cc + 1],
                    in0=self.colpad.ap()[:, 8:9], scalar1=0.0, scalar2=None,
                    op0=OP.mult))
        self.buf("a2halo").w = t_hz2

        for ch in range(NCA):
            xs = ch % 2
            xn = self.a2_xn.ap()[:, xs * DB * CA:(xs + 1) * DB * CA]
            t_x = E.dma("sy", lambda e, _x=xn, _b=b, _ch=ch: e.dma_start(
                out=_x.rearrange("p (k c) -> p k c", k=DB),
                in_=self.sc_xn[_b, :, :, _ch * CA:(_ch + 1) * CA]
                .rearrange("k p c -> p k c")),
                waits=[self.buf(f"a2xn{xs}").w] + self.buf(f"a2xn{xs}").r,
                slot=f"a2xn{xs}")
            self.buf(f"a2xn{xs}").w = t_x
            self.buf(f"a2xn{xs}").r = []

            cslot = ch % 2
            cpre = self.a2_cpre.ap()[:, cslot * DB * CA:(cslot + 1) * DB * CA]

            for e_ in range(DB):
                bank = (e_ % 4) * 512
                self.ps_wr("pe", bank, 512)
                E.ws("pe", [t_x, self.buf("a2win").w])
                tm = None
                for k in range(8):
                    tm = E.i("pe", lambda e, _k=k, _e=e_, _x=xn, _bk=bank:
                             e.matmul(
                        ps[:, _bk:_bk + 512],
                        self.a2_win.ap()[:, _k * 1024 + _e * 128:
                                         _k * 1024 + _e * 128 + 128],
                        _x[:, _k * CA:(_k + 1) * CA],
                        start=(_k == 0), stop=(_k == 7)))
                self.rdone([f"a2xn{xs}"], tm)
                self.ps_wdone(bank, 512, tm)
                xpt = self.a2_xpre.ap()[:, e_ * (CA + 2):(e_ + 1) * (CA + 2)]
                xp = xpt[:, 1:CA + 1]
                self.ps_rd("act", bank, 512)
                self.wr("act", [f"a2xp{e_}"])
                E.ws("act", [self.buf("bin").w])
                te = E.i("act", lambda e, _x=xp, _bk=bank, _e=e_: e.activation(
                    out=_x, in_=ps[:, _bk:_bk + 512], func=AF.Identity,
                    bias=self.bin_c.ap()[:, _e:_e + 1]))
                self.ps_rdone(bank, 512, te)
                self.wdone([f"a2xp{e_}"], te)

                if ch > 0:
                    pslot = (ch - 1) % 2
                    cpv = self.a2_cpre.ap()[:, pslot * DB * CA + e_ * CA:
                                            pslot * DB * CA + (e_ + 1) * CA]
                    E.ws("dve", [te, self.buf(f"a2cp{pslot}_{e_}").w,
                                 self.buf("cw").w])
                    tf = E.i("dve", lambda e, _c=cpv, _x=xpt, _e=e_:
                             e.scalar_tensor_tensor(
                        out=_c[:, CA - 1:CA], in0=_x[:, 1:2],
                        scalar=self.cwcol.ap()[:, _e * 3 + 2:_e * 3 + 3],
                        in1=_c[:, CA - 1:CA], op0=OP.mult, op1=OP.add))
                    self.rdone([f"a2xp{e_}"], tf)
                    E.ws("act", [tf])
                    ts = E.i("act", lambda e, _c=cpv: e.activation(
                        out=_c.bitcast(F32R), in_=_c, func=AF.Silu))
                    self.buf(f"a2cp{pslot}_{e_}").w = ts
                    tsp = E.dma("sy", lambda e, _c=cpv, _b=b, _e=e_, _ch=ch:
                                e.dma_start(
                        out=self.sc_x[_b, _e, :, (_ch - 1) * CA:_ch * CA],
                        in_=_c.bitcast(F32R)), waits=[ts],
                        slot=f"a2cp{pslot}")
                    self.buf(f"a2cp{pslot}_{e_}").r = [tsp]

                cbank = 2048 + (e_ % 4) * 512
                self.ps_wr("pe", cbank, 512)
                E.ws("pe", [te, self.buf("a2diag").w, self.buf("a2halo").w])
                dg = self.a2_diag.ap()
                t1 = E.i("pe", lambda e, _x=xpt, _e=e_, _bk=cbank: e.matmul(
                    ps[:, _bk:_bk + 512],
                    dg[:, (_e * 3 + 0) * 128:(_e * 3 + 1) * 128],
                    _x[:, 0:CA], start=True, stop=False))
                t2 = E.i("pe", lambda e, _x=xpt, _e=e_, _bk=cbank: e.matmul(
                    ps[:, _bk:_bk + 512],
                    dg[:, (_e * 3 + 1) * 128:(_e * 3 + 2) * 128],
                    _x[:, 1:CA + 1], start=False, stop=False))
                t4 = E.i("pe", lambda e, _x=xpt, _e=e_, _bk=cbank: e.matmul(
                    ps[:, _bk:_bk + 512],
                    dg[:, (_e * 3 + 2) * 128:(_e * 3 + 3) * 128],
                    _x[:, 2:CA + 2], start=False, stop=True))
                self.ps_wdone(cbank, 512, t4)
                self.rdone([f"a2xp{e_}"], t4)
                E.ws("act", [t4])
                th = E.i("act", lambda e, _x=xpt: e.activation(
                    out=_x[:, 0:1], in_=_x[:, CA:CA + 1], func=AF.Identity))
                if e_ == DB - 1:
                    self.buf("a2halo").w = th
                cpc = self.a2_cpre.ap()[:, cslot * DB * CA + e_ * CA:
                                        cslot * DB * CA + (e_ + 1) * CA]
                self.ps_rd("act", cbank, 512)
                self.wr("act", [f"a2cp{cslot}_{e_}"])
                E.ws("act", [self.buf("convb").w])
                tev = E.i("act", lambda e, _c=cpc, _bk=cbank, _e=e_: e.activation(
                    out=_c, in_=ps[:, _bk:_bk + 512], func=AF.Identity,
                    bias=self.convb_c.ap()[:, _e:_e + 1]))
                self.ps_rdone(cbank, 512, tev)
                self.wdone([f"a2cp{cslot}_{e_}"], tev)

                if ch == NCA - 1:
                    E.ws("act", [tev])
                    ts = E.i("act", lambda e, _c=cpc: e.activation(
                        out=_c.bitcast(F32R), in_=_c, func=AF.Silu))
                    self.buf(f"a2cp{cslot}_{e_}").w = ts
                    tsp = E.dma("sy", lambda e, _c=cpc, _b=b, _e=e_, _ch=ch:
                                e.dma_start(
                        out=self.sc_x[_b, _e, :, _ch * CA:(_ch + 1) * CA],
                        in_=_c.bitcast(F32R)), waits=[ts],
                        slot=f"a2cp{cslot}")
                    self.buf(f"a2cp{cslot}_{e_}").r = [tsp]

            zsl = 0
            zv = self.a2_z.ap()
            tz = None
            for e_ in range(DB):
                bank = (e_ % 4) * 512
                self.ps_wr("pe", bank, 512)
                E.ws("pe", [t_x, self.buf("a2wg").w])
                tm = None
                for k in range(8):
                    tm = E.i("pe", lambda e, _k=k, _e=e_, _x=xn, _bk=bank:
                             e.matmul(
                        ps[:, _bk:_bk + 512],
                        self.a2_wg.ap()[:, _k * 1024 + _e * 128:
                                        _k * 1024 + _e * 128 + 128],
                        _x[:, _k * CA:(_k + 1) * CA],
                        start=(_k == 0), stop=(_k == 7)))
                self.rdone([f"a2xn{xs}"], tm)
                self.ps_wdone(bank, 512, tm)
                self.ps_rd("act", bank, 512)
                if e_ == 0:
                    self.wr("act", [f"a2z{zsl}"])
                E.ws("act", [self.buf("bg").w])
                tz = E.i("act", lambda e, _e=e_, _z=zv, _bk=bank: e.activation(
                    out=_z[:, _e * CA:(_e + 1) * CA], in_=ps[:, _bk:_bk + 512],
                    func=AF.Silu, bias=self.bg_c.ap()[:, _e:_e + 1]))
                self.ps_rdone(bank, 512, tz)
                if e_ == DB - 1:
                    self.wdone([f"a2z{zsl}"], tz)
            tzs = E.dma("sy", lambda e, _z=zv, _b=b, _ch=ch: e.dma_start(
                out=self.sc_z[_b, :, :, _ch * CA:(_ch + 1) * CA]
                .rearrange("k p c -> p k c"),
                in_=_z.rearrange("p (k c) -> p k c", k=DB)), waits=[tz],
                slot="a2zsp")
            self.rdone([f"a2z{zsl}"], tzs)

    def phase_a2b_taps(self, b):
        E = self.E
        if not DEBUG_TAPS:
            return
        tb1 = E.dma("sy", lambda e, _b=b: e.dma_start(
            out=self.sc_bm[_b], in_=self.Bm.ap()),
            waits=[self.buf("Bm").w])
        self.rdone(["Bm"], tb1)
        tb2 = E.dma("sy", lambda e, _b=b: e.dma_start(
            out=self.sc_cm[_b], in_=self.Cm.ap()),
            waits=[self.buf("Cm").w])
        self.rdone(["Cm"], tb2)

    def phase_a2b(self, b):
        E = self.E
        ps = self.ps.ap()

        t_wd = E.dma("gps", lambda e: e.dma_start(
            out=self.b2_wdt.ap().rearrange("p (k e) -> p k e", k=8),
            in_=self.WDT.rearrange("(k p) e -> p k e", p=128)),
            waits=[self.buf("b2wdt").w] + self.buf("b2wdt").r, slot="b2wdt")
        self.buf("b2wdt").w = t_wd
        t_wb = E.dma("gps", lambda e: e.dma_start(
            out=self.b2_wb.ap().rearrange("p (k n) -> p k n", k=8),
            in_=self.WB.rearrange("(k p) n -> p k n", p=128)),
            waits=[self.buf("b2wb").w] + self.buf("b2wb").r, slot="b2wb")
        self.buf("b2wb").w = t_wb
        t_wc = E.dma("gps", lambda e: e.dma_start(
            out=self.b2_wc.ap().rearrange("p (k n) -> p k n", k=8),
            in_=self.WC.rearrange("(k p) n -> p k n", p=128)),
            waits=[self.buf("b2wc").w] + self.buf("b2wc").r, slot="b2wc")
        self.buf("b2wc").w = t_wc

        for ch in range(NCB):
            xs = ch % 2
            xv = self.b2_x.ap()[:, xs * DB * CB:(xs + 1) * DB * CB]
            t_x = E.dma("sy", lambda e, _x=xv, _b=b, _ch=ch: e.dma_start(
                out=_x.rearrange("p (k c) -> p k c", k=DB),
                in_=self.sc_x[_b, :, :, _ch * CB:(_ch + 1) * CB]
                .rearrange("k p c -> p k c")),
                waits=[self.buf(f"b2x{xs}").w] + self.buf(f"b2x{xs}").r,
                slot=f"b2x{xs}")
            self.buf(f"b2x{xs}").w = t_x
            self.buf(f"b2x{xs}").r = []

            sg = self.b2_sg.ap()[:, xs * DB * CB:(xs + 1) * DB * CB]
            for e_ in range(DB):
                bank = (e_ % 2) * 512
                self.ps_wr("pe", bank, 512)
                E.ws("pe", [t_x, self.buf("b2wdt").w])
                tm = None
                for k in range(8):
                    tm = E.i("pe", lambda e, _k=k, _e=e_, _x=xv, _bk=bank:
                             e.matmul(
                        ps[:, _bk:_bk + CB],
                        self.b2_wdt.ap()[:, _k * 1024 + _e * 128:
                                         _k * 1024 + _e * 128 + 128],
                        _x[:, _k * CB:(_k + 1) * CB],
                        start=(_k == 0), stop=(_k == 7)))
                self.rdone([f"b2x{xs}"], tm)
                self.ps_wdone(bank, 512, tm)
                self.ps_rd("act", bank, 512)
                self.wr("act", [f"b2sg{xs}"]) if e_ == 0 else None
                E.ws("act", [self.buf("nbdt").w])
                te = E.i("act", lambda e, _e=e_, _s=sg, _bk=bank: e.activation(
                    out=_s[:, _e * CB:(_e + 1) * CB], in_=ps[:, _bk:_bk + CB],
                    func=AF.Sigmoid, scale=-1.0,
                    bias=self.nbdt_c.ap()[:, _e:_e + 1]))
                self.ps_rdone(bank, 512, te)
                if e_ == DB - 1:
                    self.wdone([f"b2sg{xs}"], te)
            tl = E.i("act", lambda e, _s=sg: e.activation(
                out=_s, in_=_s, func=AF.Ln))
            self.buf(f"b2sg{xs}").w = tl
            tdt = E.dma("sy", lambda e, _s=sg, _b=b, _ch=ch: e.dma_start(
                out=self.sc_dt[_b, :, :, _ch * CB:(_ch + 1) * CB]
                .rearrange("k p c -> p k c"),
                in_=_s.rearrange("p (k c) -> p k c", k=DB)), waits=[tl],
                slot=f"b2sg{xs}d")
            self.rdone([f"b2sg{xs}"], tdt)
            dx = self.b2_dtx.ap()[:, xs * DB * CB:(xs + 1) * DB * CB]
            E.ws("dve", [tl, t_x])
            self.wr("dve", [f"b2dtx{xs}"])
            tx = E.i("dve", lambda e, _d=dx, _s=sg, _x=xv: e.tensor_mul(
                _d, _s, _x.bitcast(F32)))
            self.rdone([f"b2x{xs}"], tx)
            self.rdone([f"b2sg{xs}"], tx)
            self.wdone([f"b2dtx{xs}"], tx)
            tdx = E.dma("sy", lambda e, _d=dx, _b=b, _ch=ch: e.dma_start(
                out=self.sc_dtx[_b, :, :, _ch * CB:(_ch + 1) * CB]
                .rearrange("k p c -> p k c"),
                in_=_d.rearrange("p (k c) -> p k c", k=DB)), waits=[tx],
                slot=f"b2dtx{xs}d")
            self.rdone([f"b2dtx{xs}"], tdx)

            for (wsb, dst, biascol, scl, nm) in (
                    (self.b2_wb, self.Bm, self.nbb_c, -1.0, "Bm"),
                    (self.b2_wc, self.Cm, self.bc_c, 1.0, "Cm")):
                bank = 1024
                self.ps_wr("pe", bank, 512)
                E.ws("pe", [t_x, self.buf("b2wb").w, self.buf("b2wc").w])
                tm = None
                for k in range(8):
                    tm = E.i("pe", lambda e, _k=k, _w=wsb, _x=xv, _bk=bank:
                             e.matmul(
                        ps[0:16, _bk:_bk + CB],
                        _w.ap()[:, _k * 16:(_k + 1) * 16],
                        _x[:, _k * CB:(_k + 1) * CB],
                        start=(_k == 0), stop=(_k == 7)))
                self.rdone([f"b2x{xs}"], tm)
                self.ps_wdone(bank, 512, tm)
                self.ps_rd("act", bank, 512)
                self.wr("act", [nm])
                E.ws("act", [self.buf("nbb").w, self.buf("bc").w])
                tb = E.i("act", lambda e, _d=dst, _c=biascol, _s=scl, _b2=b,
                         _ch=ch, _bk=bank: e.activation(
                    out=_d.ap()[0:16, _ch * CB:(_ch + 1) * CB],
                    in_=ps[0:16, _bk:_bk + CB], func=AF.Identity, scale=_s,
                    bias=_c.ap()))
                self.ps_rdone(bank, 512, tb)
                self.buf(nm).w = tb


    def phase_b(self, b):
        E = self.E
        ps = self.ps.ap()

        for q in range(4):
            ns = list(range(4 * q, 4 * q + 4))
            for i, n in enumerate(ns):
                for (src, dstbuf, dst) in ((self.Bm, "brep", self.s_brep),
                                           (self.Cm, "crep", self.s_crep)):
                    for tc in range(4):
                        bank = (tc % 2) * 512
                        self.ps_wr("pe", bank, 512)
                        E.ws("pe", [self.buf("Bm").w, self.buf("Cm").w,
                                    self.buf("osel").w])
                        tr = E.i("pe", lambda e, _s=src, _n=n, _tc=tc, _bk=bank:
                                 e.matmul(
                            ps[:, _bk:_bk + 512],
                            self.osel_b.ap()[:, _n * 128:(_n + 1) * 128],
                            _s.ap()[:, _tc * 512:(_tc + 1) * 512],
                            start=True, stop=True))
                        self.ps_wdone(bank, 512, tr)
                        self.ps_rd("act", bank, 512)
                        if tc == 0:
                            self.wr("act", [f"{dstbuf}{i}"])
                        tv = E.i("act", lambda e, _d=dst, _i=i, _tc=tc, _bk=bank:
                                 e.activation(
                            out=_d.ap()[:, _i * T + _tc * 512:
                                        _i * T + (_tc + 1) * 512],
                            in_=ps[:, _bk:_bk + 512], func=AF.Identity))
                        self.ps_rdone(bank, 512, tv)
                        if tc == 3:
                            self.wdone([f"{dstbuf}{i}"], tv)
                            if DEBUG_TAPS and b == 0:
                                scd = (self.sc_brep if dstbuf == "brep"
                                       else self.sc_crep)
                                tdr = E.dma("sy", lambda e, _d=dst, _i=i, _n=n,
                                            _s=scd: e.dma_start(
                                    out=_s[_n],
                                    in_=_d.ap()[:, _i * T:(_i + 1) * T]),
                                    waits=[tv])
                                self.rdone([f"{dstbuf}{i}"], tdr)

            def load_db(db):
                sl = db % 2
                tdt = E.dma("act", lambda e, _b=b, _db=db, _sl=sl: e.dma_start(
                    out=self.s_dt.ap()[:, _sl * T:(_sl + 1) * T],
                    in_=self.sc_dt[_b, _db, :, :]),
                    waits=[self.buf(f"sdt{sl}").w] + self.buf(f"sdt{sl}").r,
                    slot=f"sdt{sl}")
                self.buf(f"sdt{sl}").w = tdt
                self.buf(f"sdt{sl}").r = []
                tdx = E.dma("act", lambda e, _b=b, _db=db, _sl=sl: e.dma_start(
                    out=self.s_dtx.ap()[:, _sl * T:(_sl + 1) * T],
                    in_=self.sc_dtx[_b, _db, :, :]),
                    waits=[self.buf(f"sdx{sl}").w] + self.buf(f"sdx{sl}").r,
                    slot=f"sdx{sl}")
                self.buf(f"sdx{sl}").w = tdx
                self.buf(f"sdx{sl}").r = []

            load_db(0)
            for db in range(DB):
                if db + 1 < DB:
                    load_db(db + 1)
                sl = db % 2
                dtv = self.s_dt.ap()[:, sl * T:(sl + 1) * T]
                dxv = self.s_dtx.ap()[:, sl * T:(sl + 1) * T]

                if q > 0:
                    E.ws("pe", [self.buf(f"ysb{db}").w, self.buf("idnr").w])
                    for tc in range(4):
                        self.ps_wr("pe", tc * 512, 512)
                    tj = None
                    for tc in range(4):
                        tj = E.i("pe", lambda e, _db=db, _tc=tc: e.matmul(
                            ps[:, _tc * 512:(_tc + 1) * 512], self.idn_r.ap(),
                            self.s_ysb.ap()[:, _db * T + _tc * 512:
                                            _db * T + (_tc + 1) * 512],
                            start=True, stop=False))
                    self.rdone([f"ysb{db}"], tj)
                    for tc in range(4):
                        self.ps_wdone(tc * 512, 512, tj)
                else:
                    for tc in range(4):
                        self.ps_wr("pe", tc * 512, 512)

                for i, n in enumerate(ns):
                    for h in (0, 1):
                        step = i * 2 + h
                        ping = 2048 + (step % 2) * 1024
                        self.ps_wr("act", ping, 1024)
                        E.ws("act", [self.buf(f"sdt{sl}").w,
                                     self.buf("negA").w])
                        ta = E.i("act", lambda e, _d=dtv, _h=h, _db=db, _n=n,
                                 _pg=ping: e.activation(
                            out=ps[:, _pg:_pg + 1024],
                            in_=_d[:, _h * TH:(_h + 1) * TH], func=AF.Exp,
                            scale=self.negA.ap()[:, _db * 16 + _n:
                                                 _db * 16 + _n + 1]))
                        self.ps_wdone(ping, 1024, ta)
                        self.rdone([f"sdt{sl}"], ta)
                        ds = step % 3
                        dbx = self.s_dbx.ap()[:, ds * TH:(ds + 1) * TH]
                        self.wr("gps", [f"dbx{ds}"])
                        E.ws("gps", [self.buf(f"sdx{sl}").w,
                                     self.buf(f"brep{i}").w])
                        tb = E.i("gps", lambda e, _x=dxv, _o=dbx, _i=i, _h=h:
                                 e.tensor_mul(
                            _o, _x[:, _h * TH:(_h + 1) * TH],
                            self.s_brep.ap()[:, _i * T + _h * TH:
                                             _i * T + (_h + 1) * TH]))
                        self.wdone([f"dbx{ds}"], tb)
                        self.rdone([f"sdx{sl}"], tb)
                        self.rdone([f"brep{i}"], tb)
                        hs = step % 3
                        hv = self.s_h.ap()[:, hs * TH:(hs + 1) * TH]
                        if h == 0:
                            init = 0.0
                        else:
                            hp = (step - 1) % 3
                            init = self.s_h.ap()[:, hp * TH + TH - 1:
                                                 hp * TH + TH]
                        self.ps_rd("dve", ping, 1024)
                        self.wr("dve", [f"sh{hs}"])
                        E.ws("dve", [ta, tb])
                        tts = E.i("dve", lambda e, _o=hv, _pg=ping, _d=dbx,
                                  _in=init: e.tensor_tensor_scan(
                            out=_o, data0=ps[:, _pg:_pg + 1024], data1=_d,
                            initial=_in, op0=OP.mult, op1=OP.add))
                        if DEBUG_TAPS and b == 0 and db == 0:
                            tdh = E.dma("sy", lambda e, _n=n, _h=h, _o=hv:
                                        e.dma_start(
                                out=self.sc_h[_n, :, _h * TH:(_h + 1) * TH],
                                in_=_o), waits=[tts])
                            self.rdone([f"sh{hs}"], tdh)
                            tdb = E.dma("sy", lambda e, _n=n, _h=h, _d=dbx:
                                        e.dma_start(
                                out=self.sc_dbx[_n, :, _h * TH:(_h + 1) * TH],
                                in_=_d), waits=[tb])
                            self.rdone([f"dbx{ds}"], tdb)
                        self.ps_rdone(ping, 1024, tts)
                        self.rdone([f"dbx{ds}"], tts)
                        self.wdone([f"sh{hs}"], tts)
                        cs = step % 3
                        hcv = self.s_hc.ap()[:, cs * TH:(cs + 1) * TH]
                        self.wr("dve", [f"shc{cs}"])
                        E.ws("dve", [self.buf(f"crep{i}").w])
                        thc = E.i("dve", lambda e, _o=hcv, _h=hv, _i=i, _hh=h:
                                  e.tensor_mul(
                            _o, _h,
                            self.s_crep.ap()[:, _i * T + _hh * TH:
                                             _i * T + (_hh + 1) * TH]))
                        self.rdone([f"sh{hs}"], thc)
                        self.rdone([f"crep{i}"], thc)
                        self.wdone([f"shc{cs}"], thc)
                        E.ws("pe", [thc, self.buf("idnb").w])
                        tp = None
                        for t2 in range(2):
                            tc = h * 2 + t2
                            tp = E.i("pe", lambda e, _hc=hcv, _t2=t2, _tc=tc,
                                     _st=(q == 0 and i == 0),
                                     _sp=(i == 3): e.matmul(
                                ps[:, _tc * 512:(_tc + 1) * 512],
                                self.idn_b.ap(),
                                _hc[:, _t2 * 512:(_t2 + 1) * 512],
                                start=_st, stop=_sp))
                        self.rdone([f"shc{cs}"], tp)
                        self.ps_wdone(h * 2 * 512, 1024, tp)

                if q < 3:
                    self.ps_rd("act", 0, 2048)
                    self.wr("act", [f"ysb{db}"])
                    ty = E.i("act", lambda e, _db=db: e.activation(
                        out=self.s_ysb.ap()[:, _db * T:(_db + 1) * T],
                        in_=ps[:, 0:2048], func=AF.Identity))
                    self.ps_rdone(0, 2048, ty)
                    self.wdone([f"ysb{db}"], ty)
                else:
                    for h in (0, 1):
                        zs = h % 2
                        zv = self.s_z.ap()[:, zs * TH:(zs + 1) * TH]
                        tz = E.dma("act", lambda e, _b=b, _db=db, _h=h, _z=zv:
                                   e.dma_start(
                            out=_z,
                            in_=self.sc_z[_b, _db, :, _h * TH:(_h + 1) * TH]),
                            waits=[self.buf(f"sz{zs}").w] + self.buf(f"sz{zs}").r,
                            slot=f"sz{zs}")
                        self.buf(f"sz{zs}").w = tz
                        self.buf(f"sz{zs}").r = []
                        ys = h % 2
                        yzv = self.s_yz.ap()[:, ys * TH:(ys + 1) * TH]
                        self.ps_rd("dve", h * 1024, 1024)
                        self.wr("dve", [f"syz{ys}"])
                        E.ws("dve", [tz])
                        tyz = E.i("dve", lambda e, _o=yzv, _z=zv, _h=h:
                                  e.tensor_mul(
                            _o, ps[:, _h * 1024:(_h + 1) * 1024], _z))
                        self.ps_rdone(h * 1024, 1024, tyz)
                        self.rdone([f"sz{zs}"], tyz)
                        self.wdone([f"syz{ys}"], tyz)
                        tsp = E.dma("sy", lambda e, _o=yzv, _b=b, _db=db, _h=h:
                                    e.dma_start(
                            out=self.sc_yz[_b, _db, :, _h * TH:(_h + 1) * TH],
                            in_=_o), waits=[tyz], slot=f"syz{ys}")
                        self.rdone([f"syz{ys}"], tsp)

    def phase_c(self, b):
        E = self.E
        ps = self.ps.ap()

        t_w = E.dma("gps", lambda e: e.dma_start(
            out=self.c_wout.ap().rearrange("p (k e) -> p k e", k=8),
            in_=self.WOUT.rearrange("(k p) e -> p k e", p=128)),
            waits=[self.buf("cwout").w] + self.buf("cwout").r, slot="cwout")
        self.buf("cwout").w = t_w
        self.buf("cwout").r = []

        def load_ch(ch):
            sl = ch % 2
            tyz = E.dma("act", lambda e, _b=b, _ch=ch, _sl=sl: e.dma_start(
                out=self.c_yz.ap()[:, _sl * DB * 512:(_sl + 1) * DB * 512]
                .rearrange("p (k c) -> p k c", k=DB),
                in_=self.sc_yz[_b, :, :, _ch * CC:(_ch + 1) * CC]
                .rearrange("k p c -> p k c")),
                waits=[self.buf(f"cyz{sl}").w] + self.buf(f"cyz{sl}").r,
                slot=f"cyz{sl}")
            self.buf(f"cyz{sl}").w = tyz
            self.buf(f"cyz{sl}").r = []
            tuo = E.dma("sy", lambda e, _b=b, _ch=ch, _sl=sl: e.dma_start(
                out=self.c_uo.ap()[:, _sl * 4096:(_sl + 1) * 4096]
                .rearrange("p (j c) -> p j c", j=4),
                in_=self.U[_b, _ch * CC:(_ch + 1) * CC, :]
                .rearrange("(j p) c -> p j c", p=128)),
                waits=[self.buf(f"cuo{sl}").w] + self.buf(f"cuo{sl}").r,
                slot=f"cuo{sl}")
            self.buf(f"cuo{sl}").w = tuo
            self.buf(f"cuo{sl}").r = []
            top = E.dma("sy", lambda e, _b=b, _ch=ch, _sl=sl: e.dma_start(
                out=self.c_op.ap()[:, _sl * 4096:(_sl + 1) * 4096]
                .rearrange("p (j c) -> p j c", j=4),
                in_=self.sc_op[_b, _ch * CC:(_ch + 1) * CC, :]
                .rearrange("(j p) c -> p j c", p=128)),
                waits=[self.buf(f"cop{sl}").w] + self.buf(f"cop{sl}").r,
                slot=f"cop{sl}")
            self.buf(f"cop{sl}").w = top
            self.buf(f"cop{sl}").r = []

        load_ch(0)
        for ch in range(NCC):
            if ch + 1 < NCC:
                load_ch(ch + 1)
            sl = ch % 2
            yz = self.c_yz.ap()[:, sl * DB * 512:(sl + 1) * DB * 512]
            uo = self.c_uo.ap()[:, sl * 4096:(sl + 1) * 4096]
            op = self.c_op.ap()[:, sl * 4096:(sl + 1) * 4096]
            E.ws("dve", [self.buf(f"cuo{sl}").w, self.buf(f"cop{sl}").w])
            tadd = E.i("dve", lambda e, _u=uo, _o=op: e.tensor_add(_u, _u, _o))
            self.rdone([f"cop{sl}"], tadd)
            self.buf(f"cuo{sl}").w = tadd
            E.wf("dve", tadd)

            for e_ in range(DB):
                bank = (e_ % 2) * 512
                self.ps_wr("pe", bank, 512)
                E.ws("pe", [self.buf(f"cyz{sl}").w, self.buf("cwout").w])
                tm = None
                for k in range(8):
                    tm = E.i("pe", lambda e, _k=k, _e=e_, _y=yz, _bk=bank:
                             e.matmul(
                        ps[:, _bk:_bk + 512],
                        self.c_wout.ap()[:, _k * 1024 + _e * 128:
                                         _k * 1024 + _e * 128 + 128],
                        _y[:, _k * 512:(_k + 1) * 512],
                        start=(_k == 0), stop=(_k == 7)))
                self.rdone([f"cyz{sl}"], tm)
                self.ps_wdone(bank, 512, tm)
                asl = e_ % 2
                att = self.c_attn.ap()[:, asl * 512:(asl + 1) * 512]
                self.ps_rd("act", bank, 512)
                self.wr("act", [f"cat{asl}"])
                E.ws("act", [self.buf("bout").w])
                te = E.i("act", lambda e, _a=att, _bk=bank, _e=e_: e.activation(
                    out=_a, in_=ps[:, _bk:_bk + 512], func=AF.Identity,
                    bias=self.bout_c.ap()[:, _e:_e + 1]))
                self.ps_rdone(bank, 512, te)
                self.wdone([f"cat{asl}"], te)
                pbank = 1024 + (e_ % 2) * 512
                self.ps_wr("pe", pbank, 512)
                E.ws("pe", [te, self.buf("idn").w])
                tp = None
                for j4 in range(4):
                    tp = E.i("pe", lambda e, _a=att, _j=j4, _pb=pbank:
                             e.transpose(
                        ps[:, _pb + _j * 128:_pb + (_j + 1) * 128],
                        _a[:, _j * 128:(_j + 1) * 128], self.idn_f.ap()))
                self.rdone([f"cat{asl}"], tp)
                self.ps_wdone(pbank, 512, tp)
                self.ps_rd("dve", pbank, 512)
                E.ws("dve", [self.buf(f"cuo{sl}").w])
                tv = E.i("dve", lambda e, _u=uo, _e=e_, _pb=pbank: e.tensor_add(
                    _u.rearrange("p (j c) -> p j c", j=4)[:, :,
                                                          _e * 128:(_e + 1) * 128],
                    _u.rearrange("p (j c) -> p j c", j=4)[:, :,
                                                          _e * 128:(_e + 1) * 128],
                    ps[:, _pb:_pb + 512].rearrange("p (j c) -> p j c", c=128)))
                self.ps_rdone(pbank, 512, tv)
                self.buf(f"cuo{sl}").w = tv
            tso = E.dma("sy", lambda e, _u=uo, _b=b, _ch=ch: e.dma_start(
                out=self.OUT[_b, _ch * CC:(_ch + 1) * CC, :]
                .rearrange("(j p) c -> p j c", p=128),
                in_=_u.rearrange("p (j c) -> p j c", j=4)),
                waits=[self.buf(f"cuo{sl}").w], slot=f"cuo{sl}")
            self.rdone([f"cuo{sl}"], tso)


def _ctx():
    import contextlib
    return contextlib.nullcontext()


_NC_CACHE = {}


def build_program():
    if "nc" not in _NC_CACHE:
        _NC_CACHE["nc"] = K().build()
    return _NC_CACHE["nc"]


def kernel(**inputs):
    u = np.ascontiguousarray(np.asarray(inputs["L0_attn_u"]))
    wrt = np.ascontiguousarray(np.asarray(inputs["L0_ffn_w_router"]).T)
    idn = np.eye(128, dtype=np.float32)
    ones = np.zeros((32, 128), dtype=np.float32)
    ones[0, :] = 1.0
    psel = np.zeros((32, 16 * 64), dtype=np.float32)
    for n in range(16):
        psel[n, n * 64:(n + 1) * 64] = 1.0
    osel = np.zeros((32, 16 * 128), dtype=np.float32)
    for n in range(16):
        osel[n, n * 128:(n + 1) * 128] = 1.0

    common = {
        "WIN": np.ascontiguousarray(inputs["L0_attn_Win"]),
        "BIN": np.ascontiguousarray(inputs["L0_attn_b_in"]),
        "WG": np.ascontiguousarray(inputs["L0_attn_Wg"]),
        "BG": np.ascontiguousarray(inputs["L0_attn_bg"]),
        "WOUT": np.ascontiguousarray(inputs["L0_attn_Wout"]),
        "BOUT": np.ascontiguousarray(inputs["L0_attn_bout"]),
        "CONVW": np.ascontiguousarray(inputs["L0_attn_convw"]),
        "CONVB": np.ascontiguousarray(inputs["L0_attn_convb"]),
        "AMAT": np.ascontiguousarray(inputs["L0_attn_A"]),
        "WB": np.ascontiguousarray(inputs["L0_attn_Wb"]),
        "BB": np.ascontiguousarray(inputs["L0_attn_bb"]),
        "WC": np.ascontiguousarray(inputs["L0_attn_Wc"]),
        "BCC": np.ascontiguousarray(inputs["L0_attn_bc"]),
        "WDT": np.ascontiguousarray(inputs["L0_attn_Wdt"]),
        "BDT": np.ascontiguousarray(inputs["L0_attn_bdt"]),
        "WRT": wrt,
        "CEN": np.ascontiguousarray(inputs["L0_ffn_head_centroids"]),
        "TEMP": np.ascontiguousarray(np.repeat(np.asarray(inputs["L0_ffn_temperature"]).reshape(1, 1), 128, axis=0)),
        "IEXP": np.ascontiguousarray(inputs["L0_ffn_input_experts"]),
        "OEXP": np.ascontiguousarray(inputs["L0_ffn_output_experts"]),
        "IDN": idn,
        "ONESR": ones,
        "PSEL": psel,
        "OSEL": osel,
    }
    in_maps = []
    for c in range(NCORE):
        m = dict(common)
        m["U"] = np.ascontiguousarray(u[2 * c:2 * c + 2])
        in_maps.append(m)

    nc = build_program()
    res = run_bass_kernel_spmd(nc, in_maps, core_ids=list(range(NCORE)))
    out = np.concatenate([r["OUT"] for r in res.results], axis=0)
    return out.astype(np.float32)


if __name__ == "__main__":
    nc = build_program()
    print("built ok")


# version 14
# speedup vs baseline: 1.1376x; 1.0726x over previous
import numpy as np

import concourse.bass as bass
import concourse.mybir as mybir
from concourse.bass_utils import run_bass_kernel_spmd

F32 = mybir.dt.float32
F32R = mybir.dt.float32r
BF16 = mybir.dt.bfloat16
AF = mybir.ActivationFunctionType
OP = mybir.AluOpType
AX = mybir.AxisListType

D = 1024
DB = 8
NST = 16
T = 2048
BC = 2
NCORE = 8
G = 128
NH = 16
HS = 512
HD = 64
JT = T // 128
NJG = T // 512
CA = 512
NCA = T // CA
CB = 256
NCB = T // CB
TH = T // 2
CC = 512
NCC = T // CC
EPS = 1e-5
DEBUG_TAPS = False
PHASE_LIMIT = 99

ENGS = ("pe", "act", "dve", "gps", "sy")
PROG = {"pe": "pe", "act": "act", "dve": "dve", "gps": "gps", "sy": "spr"}
SEMS = ["pe", "act", "dve", "gps", "spr", "qsy", "qsc", "qgp"]
ENG_OF_SEM = {"pe": "pe", "act": "act", "dve": "dve", "gps": "gps",
              "spr": "sy", "qsy": "sy", "qsc": "act", "qgp": "gps"}


class Bld:

    def __init__(self):
        self.ops = {k: [] for k in ENGS}
        self.cnt = {s: 0 for s in SEMS}
        self.sems = {}
        self.slotsems = {}
        self.sem_owner = {"qsy": "sy", "qsc": "act", "qgp": "gps"}

    def slotsem(self, slot, eng):
        if slot not in self.slotsems:
            s = f"sl{len(self.slotsems)}"
            self.slotsems[slot] = s
            self.cnt[s] = 0
            self.sem_owner[s] = eng
        return self.slotsems[slot]

    def w(self, eng, tok):
        if tok is None:
            return
        s, v = tok
        if v <= 0:
            return
        if ENG_OF_SEM.get(s) == eng and s in ("pe", "act", "dve", "gps"):
            return
        self.ops[eng].append(lambda e, _s=s, _v=v: e.wait_ge(self.sems[_s], _v))

    def ws(self, eng, toks):
        best = {}
        for t in toks:
            if t is None:
                continue
            s, v = t
            if v > best.get(s, -1):
                best[s] = v
        for s, v in best.items():
            self.w(eng, (s, v))

    def wf(self, eng, tok):
        if tok is None:
            return
        s, v = tok
        if v <= 0:
            return
        self.ops[eng].append(lambda e, _s=s, _v=v: e.wait_ge(self.sems[_s], _v))

    def i(self, eng, fn, sem=None, k=1):
        if sem is None:
            sem = PROG[eng]
        self.cnt[sem] += k
        v = self.cnt[sem]
        self.ops[eng].append(
            lambda e, _f=fn, _s=sem, _k=k: _f(e).then_inc(self.sems[_s], _k))
        return (sem, v)

    def dma(self, eng, fn, waits=(), slot=None):
        q = self.slotsem(slot, eng) if slot else             {"sy": "qsy", "act": "qsc", "gps": "qgp"}[eng]
        best = {}
        for t in waits:
            if t is None:
                continue
            s, v = t
            if v > best.get(s, -1):
                best[s] = v
        for s, v in best.items():
            self.wf(eng, (s, v))
        return self.i(eng, fn, q, 16)

    def nop_inc(self, eng):
        self.i(eng, lambda e: e.nop())

    def barrier(self):
        for eng in ENGS:
            for s, owner in self.sem_owner.items():
                if owner == eng and self.cnt.get(s, 0) > 0:
                    self.wf(eng, (s, self.cnt[s]))
            self.nop_inc(eng)
        snap = {s: self.cnt[s] for s in ("pe", "act", "dve", "gps", "spr")}
        for eng in ENGS:
            for s, v in snap.items():
                if v <= 0 or ENG_OF_SEM[s] == eng:
                    continue
                self.ops[eng].append(
                    lambda e, _s=s, _v=v: e.wait_ge(self.sems[_s], _v))


class Buf:
    __slots__ = ("w", "r")

    def __init__(self):
        self.w = None
        self.r = []


class K:
    def __init__(self):
        self.nc = bass.Bass("TRN2", target_bir_lowering=False, debug=False)
        self.E = Bld()
        self.bufs = {}
        self.psum_b = [Buf() for _ in range(8)]

    def buf(self, name):
        if name not in self.bufs:
            self.bufs[name] = Buf()
        return self.bufs[name]

    def rd(self, eng, names):
        toks = []
        for nm in names:
            toks.append(self.buf(nm).w)
        self.E.ws(eng, toks)

    def rdone(self, names, tok):
        for nm in names:
            self.buf(nm).r.append(tok)

    def wr(self, eng, names):
        toks = []
        for nm in names:
            b = self.buf(nm)
            toks.append(b.w)
            toks.extend(b.r)
        self.E.ws(eng, toks)

    def wdone(self, names, tok):
        for nm in names:
            b = self.buf(nm)
            b.w = tok
            b.r = []

    def banks_of(self, off, length):
        b0 = off // 512
        b1 = (off + length - 1) // 512
        return list(range(b0, b1 + 1))

    def ps_wr(self, eng, off, length):
        toks = []
        for b in self.banks_of(off, length):
            toks.append(self.psum_b[b].w)
            toks.extend(self.psum_b[b].r)
        self.E.ws(eng, toks)

    def ps_wdone(self, off, length, tok):
        for b in self.banks_of(off, length):
            self.psum_b[b].w = tok
            self.psum_b[b].r = []

    def ps_rd(self, eng, off, length):
        toks = [self.psum_b[b].w for b in self.banks_of(off, length)]
        self.E.ws(eng, toks)

    def ps_rdone(self, off, length, tok):
        for b in self.banks_of(off, length):
            self.psum_b[b].r.append(tok)

    def build(self):
        nc = self.nc
        E = self.E

        def din(name, shape, dt=F32):
            return nc.dram_tensor(name, shape, dt, kind="ExternalInput").ap()

        self.U = din("U", [BC, T, D])
        self.WIN = din("WIN", [D, D]); self.BIN = din("BIN", [D])
        self.WG = din("WG", [D, D]); self.BG = din("BG", [D])
        self.WOUT = din("WOUT", [D, D]); self.BOUT = din("BOUT", [D])
        self.CONVW = din("CONVW", [D, 3]); self.CONVB = din("CONVB", [D])
        self.AMAT = din("AMAT", [D, NST])
        self.WB = din("WB", [D, NST]); self.BB = din("BB", [NST])
        self.WC = din("WC", [D, NST]); self.BCC = din("BCC", [NST])
        self.WDT = din("WDT", [D, D]); self.BDT = din("BDT", [D])
        self.WRT = din("WRT", [D, G])
        self.CEN = din("CEN", [NH, G])
        self.TEMP = din("TEMP", [128, 1])
        self.IEXP = din("IEXP", [NH, D, HS])
        self.OEXP = din("OEXP", [NH, HS, HD])
        self.IDN = din("IDN", [128, 128])
        self.ONESR = din("ONESR", [32, 128])
        self.PSEL = din("PSEL", [32, 16 * 64])
        self.OSEL = din("OSEL", [32, 16 * 128])

        self.OUT = nc.dram_tensor("OUT", [BC, T, D], F32, kind="ExternalOutput").ap()

        def scr(name, shape, dt=F32):
            kind = "ExternalOutput" if DEBUG_TAPS else "Internal"
            return nc.dram_tensor(name, shape, dt, kind=kind).ap()

        self.sc_xn = scr("sc_xn", [BC, DB, 128, T], F32R)
        self.sc_x = scr("sc_x", [BC, DB, 128, T], F32R)
        self.sc_dt = scr("sc_dt", [BC, DB, 128, T], F32)
        self.sc_dtx = scr("sc_dtx", [BC, DB, 128, T], F32)
        self.sc_z = scr("sc_z", [BC, DB, 128, T], F32)
        self.sc_yz = scr("sc_yz", [BC, DB, 128, T], F32R)
        self.sc_op = scr("sc_op", [BC, T, D], F32)
        self.sc_st = scr("sc_st", [BC, 4, 128, 32], F32)
        self.sc_wt = scr("sc_wt", [BC, 32, T], BF16)
        self.sc_xnb = scr("sc_xnb", [BC, 128, DB * T], BF16)
        self.sc_h = scr("sc_h", [16, 128, T], BF16)
        self.sc_brep = scr("sc_brep", [16, 128, T], BF16)
        self.sc_crep = scr("sc_crep", [16, 128, T], BF16)
        self.sc_dbx = scr("sc_dbx", [16, 128, T], F32)
        self.sc_bm = scr("sc_bm", [BC, 32, T], BF16)
        self.sc_hid = scr("sc_hid", [4, 128, T], BF16)
        self.sc_fp = scr("sc_fp", [128, T], F32)
        self.sc_wrp = scr("sc_wrp", [128, T], F32)
        self.sc_cm = scr("sc_cm", [BC, 32, T], BF16)

        off = [16640]

        def sb(name, shape, dt, at=None):
            a = off[0] if at is None else at
            a = (a + 31) // 32 * 32
            h = nc.alloc_sbuf_tensor_at(name, shape, dt, offset=a)
            if at is None:
                off[0] = a + int(np.prod(shape[1:])) * mybir.dt.size(dt)
            return h

        self.idn_f = sb("idn_f", [128, 128], F32)
        self.idn_r = sb("idn_r", [128, 128], F32R)
        self.idn_b = sb("idn_b", [128, 128], BF16)
        self.ones1r = sb("ones1r", [32, 128], F32R)
        self.onesg = sb("onesg", [128, 1], F32R)
        self.psel_b = sb("psel_b", [32, 16 * 64], BF16)
        self.osel_b = sb("osel_b", [32, 16 * 128], BF16)
        self.negA = sb("negA", [128, 128], F32)
        self.cwcol = sb("cwcol", [128, 24], F32)
        self.bin_c = sb("bin_c", [128, 8], F32)
        self.convb_c = sb("convb_c", [128, 8], F32)
        self.bg_c = sb("bg_c", [128, 8], F32)
        self.nbdt_c = sb("nbdt_c", [128, 8], F32)
        self.bout_c = sb("bout_c", [128, 8], F32)
        self.nbb_c = sb("nbb_c", [16, 1], F32)
        self.bc_c = sb("bc_c", [16, 1], F32)
        self.scol = sb("scol", [128, 1], F32)
        self.colpad = sb("colpad", [128, 48], F32)
        self.WrT = sb("WrT", [128, DB * 128], F32R)
        self.cnT = sb("cnT", [128, 16], F32R)
        self.cen_s = sb("cen_s", [32, G], F32)
        self.zeros32 = sb("zeros32", [32, 512], BF16)
        self.cnr = sb("cnr", [32, G], F32R)
        self.wTb = sb("wTb", [32, T], BF16)
        self.Bm = sb("Bm", [32, T], BF16)
        self.Cm = sb("Cm", [32, T], BF16)
        self.woutE = sb("woutE", [128, NH * 4 * HD], BF16)
        self.tmp1 = sb("tmp1", [1, 4], F32R)
        W_END = off[0]
        assert W_END < 60000, W_END

        OV = (W_END + 255) // 256 * 256

        o = [OV]

        def ov(name, shape, dt):
            a = (o[0] + 31) // 32 * 32
            h = nc.alloc_sbuf_tensor_at(name, shape, dt, offset=a)
            o[0] = a + int(np.prod(shape[1:])) * mybir.dt.size(dt)
            return h

        def ov_reset():
            o[0] = OV

        self.a1_u = ov("a1_u", [128, 4 * 1024], F32)
        self.a1_scr = ov("a1_scr", [128, 1024], F32)
        self.a1_xtm = ov("a1_xtm", [128, 4 * 1024], F32R)
        self.a1_xfm = ov("a1_xfm", [128, 2 * 512], F32R)
        self.a1_xnb = ov("a1_xnb", [128, DB * T], BF16)
        self.a1_cols = ov("a1_cols", [128, 4 * 8], F32)
        self.a1_zr = ov("a1_zr", [128, 512], F32R)
        self.a1_zsq = ov("a1_zsq", [128, 512], F32R)
        self.a1_znorm = ov("a1_znorm", [128, 512], F32R)
        self.a1_nrm = ov("a1_nrm", [1, 512], F32)
        self.a1_inv = ov("a1_inv", [32, 512], F32R)
        self.a1_lg = ov("a1_lg", [32, 512], F32R)
        self.a1_lgt = ov("a1_lgt", [128, 128], F32)
        self.a1_wtm = ov("a1_wtm", [128, 64], F32)
        self.a1_smc = ov("a1_smc", [128, 16], F32)
        self.a1_msk = ov("a1_msk", [128, 16], F32)
        self.a1_dbg = ov("a1_dbg", [128, JT * 8], F32)
        self.a1_e2 = ov("a1_e2", [128, 16], F32)
        self.a1_wi = ov("a1_wi", [128, 2 * 8 * 512], BF16)
        self.a1_hid = ov("a1_hid", [128, 4 * 512], BF16)
        self.a1_wrep = ov("a1_wrep", [64, 2048], F32)
        self.a1_fe = ov("a1_fe", [64, 2048], F32)
        self.a1_fo = ov("a1_fo", [64, 2048], F32)
        self.a1_ops = ov("a1_ops", [128, 2 * 512], F32)
        A1_END = o[0]

        ov_reset()
        self.a2_xn = ov("a2_xn", [128, 2 * DB * CA], F32R)
        self.a2_win = ov("a2_win", [128, 8 * 1024], F32R)
        self.a2_diag = ov("a2_diag", [128, 24 * 128], F32R)
        self.a2_xpre = ov("a2_xpre", [128, DB * (CA + 2)], F32R)
        self.a2_cpre = ov("a2_cpre", [128, 2 * DB * CA], F32)
        self.a2_wg = ov("a2_wg", [128, 8 * 1024], F32R)
        self.a2_z = ov("a2_z", [128, DB * CA], F32)
        A2A_END = o[0]

        ov_reset()
        self.b2_x = ov("b2_x", [128, 2 * DB * CB], F32R)
        self.b2_sg = ov("b2_sg", [128, 2 * DB * CB], F32)
        self.b2_dtx = ov("b2_dtx", [128, 2 * DB * CB], F32)
        self.b2_z = ov("b2_z", [128, 2 * DB * CB], F32)
        self.b2_wdt = ov("b2_wdt", [128, 8 * 1024], F32R)
        self.b2_wb = ov("b2_wb", [128, 8 * 16], F32R)
        self.b2_wc = ov("b2_wc", [128, 8 * 16], F32R)
        A2B_END = o[0]

        ov_reset()
        self.s_brep = ov("s_brep", [128, 4 * T], BF16)
        self.s_crep = ov("s_crep", [128, 4 * T], BF16)
        self.s_ysb = ov("s_ysb", [128, DB * T], F32R)
        self.s_dt = ov("s_dt", [128, 2 * T], F32)
        self.s_dtx = ov("s_dtx", [128, 2 * T], F32)
        self.s_dbx = ov("s_dbx", [128, 3 * TH], F32)
        self.s_h = ov("s_h", [128, 3 * TH], BF16)
        self.s_hc = ov("s_hc", [128, 3 * TH], BF16)
        self.s_z = ov("s_z", [128, 2 * TH], F32)
        self.s_yz = ov("s_yz", [128, 2 * TH], F32R)
        B_END = o[0]

        ov_reset()
        self.c_yz = ov("c_yz", [128, 2 * DB * 512], F32R)
        self.c_wout = ov("c_wout", [128, 8 * 1024], F32R)
        self.c_attn = ov("c_attn", [128, 2 * 512], F32)
        self.c_uo = ov("c_uo", [128, 2 * 4 * 1024], F32)
        self.c_op = ov("c_op", [128, 2 * 4 * 1024], F32)
        C_END = o[0]

        assert max(A1_END, A2A_END, A2B_END, B_END, C_END) <= 229312, (
            A1_END, A2A_END, A2B_END, B_END, C_END)

        self.ps = nc.alloc_psum_tensor("ps", [128, 4096], F32)

        self.preamble()
        for b in range(BC):
            self.phase_a1(b)
            self.E.barrier()
            if PHASE_LIMIT >= 2:
                self.phase_a2a(b)
                self.E.barrier()
            if PHASE_LIMIT >= 3:
                self.phase_a2b(b)
                self.phase_a2b_taps(b)
                self.E.barrier()
            if PHASE_LIMIT >= 4:
                self.phase_b(b)
                self.E.barrier()
            if PHASE_LIMIT >= 5:
                self.phase_c(b)
                self.E.barrier()

        self.E.barrier()

        import contextlib
        stack = contextlib.ExitStack()
        for s in list(SEMS) + list(self.E.slotsems.values()):
            self.E.sems[s] = stack.enter_context(nc.semaphore(s))
        with stack:
            with nc.Block() as block:
                @block.tensor
                def _(e):
                    for f in E.ops["pe"]:
                        f(e)

                @block.scalar
                def _(e):
                    for f in E.ops["act"]:
                        f(e)

                @block.vector
                def _(e):
                    for f in E.ops["dve"]:
                        f(e)

                @block.gpsimd
                def _(e):
                    for f in E.ops["gps"]:
                        f(e)

                @block.sync
                def _(e):
                    for f in E.ops["sy"]:
                        f(e)
        return nc

    def preamble(self):
        E, nc = self.E, self.nc
        ps = self.ps.ap()

        t_idn = E.dma("sy", lambda e: e.dma_start(out=self.idn_f.ap(), in_=self.IDN), slot="c_idn")
        t_idnr = E.dma("gps", lambda e: e.dma_start(out=self.idn_r.ap(), in_=self.IDN), slot="c_idnr")
        t_idnb = E.dma("gps", lambda e: e.dma_start(out=self.idn_b.ap(), in_=self.IDN), slot="c_idnb")
        t_o1r = E.dma("gps", lambda e: e.dma_start(out=self.ones1r.ap(), in_=self.ONESR), slot="c_o1r")
        t_og = E.dma("gps", lambda e: e.dma_start(
            out=self.onesg.ap(), in_=self.ONESR[0:1, :].rearrange("o p -> p o"),
            allow_slow_non_contiguous=True), slot="c_og")
        tz0 = None
        for tns in (self.wTb, self.Bm, self.Cm):
            tz0 = E.i("dve", lambda e, _t=tns: e.memset(
                _t.ap().bitcast(F32), 0.0))
        self.buf("padzero").w = tz0
        tz1 = E.i("dve", lambda e: e.memset(self.zeros32.ap(), 0.0))
        self.buf("zeros32").w = tz1
        tz2 = E.i("dve", lambda e: e.memset(self.colpad.ap()[:, 25:26], 1e-24))
        self.buf("eps24").w = tz2
        t_ps = E.dma("gps", lambda e: e.dma_start(out=self.psel_b.ap(),
                                                  in_=self.PSEL), slot="c_ps")
        self.buf("pairs").w = t_ps
        t_os = E.dma("gps", lambda e: e.dma_start(out=self.osel_b.ap(),
                                                  in_=self.OSEL), slot="c_os")
        self.buf("osel").w = t_os

        t_a = E.dma("sy", lambda e: e.dma_start(
            out=self.a1_scr.ap()[:, 0:128].rearrange("p (db n) -> p db n", db=8),
            in_=self.AMAT.rearrange("(db p) n -> p db n", p=128)), slot="c_a")
        E.ws("dve", [t_a])
        t_na = E.i("dve", lambda e: e.tensor_scalar_mul(
            self.negA.ap(), self.a1_scr.ap()[:, 0:128], -1.0))
        self.buf("negA").w = t_na

        def loadcol(dst, src, neg=False, n16=False, nm=""):
            if n16:
                tok = E.dma("sy", lambda e, _d=dst, _s=src: e.dma_start(
                    out=_d.ap(), in_=_s.rearrange("(n o) -> n o", o=1)),
                    slot="c_" + nm)
            else:
                tok = E.dma("sy", lambda e, _d=dst, _s=src: e.dma_start(
                    out=_d.ap(), in_=_s.rearrange("(db p) -> p db", p=128),
                    allow_slow_non_contiguous=True), slot="c_" + nm)
            if neg:
                E.ws("dve", [tok])
                tok = E.i("dve", lambda e, _d=dst: e.tensor_scalar_mul(
                    _d.ap(), _d.ap(), -1.0))
            return tok

        self.buf("bin").w = loadcol(self.bin_c, self.BIN, nm="bin")
        self.buf("convb").w = loadcol(self.convb_c, self.CONVB, nm="cvb")
        self.buf("bg").w = loadcol(self.bg_c, self.BG, nm="bg")
        self.buf("nbdt").w = loadcol(self.nbdt_c, self.BDT, neg=True, nm="bdt")
        self.buf("bout").w = loadcol(self.bout_c, self.BOUT, nm="bo")
        self.buf("nbb").w = loadcol(self.nbb_c, self.BB, neg=True, n16=True, nm="bb")
        self.buf("bc").w = loadcol(self.bc_c, self.BCC, n16=True, nm="bc")

        t_cw = E.dma("sy", lambda e: e.dma_start(
            out=self.cwcol.ap()[:, 0:24].rearrange("p (db k) -> p db k", db=8),
            in_=self.CONVW.rearrange("(db p) k -> p db k", p=128)), slot="c_cw")
        self.buf("cw").w = t_cw

        t_wrt = E.dma("gps", lambda e: e.dma_start(
            out=self.WrT.ap().rearrange("p (db g) -> p db g", db=8),
            in_=self.WRT.rearrange("(db p) g -> p db g", p=128)), slot="c_wrt")
        self.buf("WrT").w = t_wrt

        t_we = E.dma("gps", lambda e: e.dma_start(
            out=self.woutE.ap().rearrange("p (n hb e) -> p n hb e", n=16, hb=4),
            in_=self.OEXP.rearrange("n (hb p) e -> p n hb e", p=128)), slot="c_we")
        self.buf("woutE").w = t_we

        E.ws("dve", [self.buf("zeros32").w])
        tcz = E.i("dve", lambda e: e.memset(self.cen_s.ap(), 0.0))
        t_ce = E.dma("sy", lambda e: e.dma_start(
            out=self.cen_s.ap()[0:16, :], in_=self.CEN), waits=[tcz],
            slot="c_cen")
        E.ws("act", [t_ce, t_na])
        t1 = E.i("act", lambda e: e.activation(
            out=self.a1_scr.ap()[0:32, 0:128], in_=self.cen_s.ap(),
            func=AF.Square))
        E.ws("dve", [t1])
        t2 = E.i("dve", lambda e: e.reduce_sum(
            self.colpad.ap()[0:32, 0:1], self.a1_scr.ap()[0:32, 0:128], AX.X))
        E.ws("act", [t2, self.buf("eps24").w])
        t3 = E.i("act", lambda e: e.activation(
            out=self.colpad.ap()[0:32, 1:2], in_=self.colpad.ap()[0:32, 0:1],
            func=AF.Ln, bias=self.colpad.ap()[0:32, 25:26]))
        t4 = E.i("act", lambda e: e.activation(
            out=self.colpad.ap()[0:32, 2:3], in_=self.colpad.ap()[0:32, 1:2],
            func=AF.Exp, scale=-0.5))
        E.ws("dve", [t4])
        E.wf("dve", t2)
        t5 = E.i("dve", lambda e: e.tensor_scalar(
            out=self.cnr.ap(),
            in0=self.cen_s.ap(), scalar1=self.colpad.ap()[0:32, 2:3],
            scalar2=None, op0=OP.mult))
        E.ws("pe", [t5, t_idnr])
        self.ps_wr("pe", 0, 32)
        tt = E.i("pe", lambda e: e.transpose(
            ps[:, 0:32].bitcast(F32R),
            self.cnr.ap(),
            self.idn_r.ap()[0:32, 0:32]))
        self.ps_wdone(0, 32, tt)
        self.ps_rd("act", 0, 32)
        tc = E.i("act", lambda e: e.activation(
            out=self.cnT.ap(), in_=ps[:, 0:16], func=AF.Identity))
        self.ps_rdone(0, 32, tc)
        self.buf("cnT").w = tc

        t_t = E.dma("sy", lambda e: e.dma_start(
            out=self.colpad.ap()[:, 8:9], in_=self.TEMP), slot="c_temp")
        E.ws("act", [t_t])
        t6 = E.i("act", lambda e: e.activation(
            out=self.colpad.ap()[:, 9:10], in_=self.colpad.ap()[:, 8:9],
            func=AF.Exp))
        E.ws("dve", [t6])
        t9 = E.i("dve", lambda e: e.tensor_scalar(
            out=self.scol.ap(), in0=self.colpad.ap()[:, 9:10],
            scalar1=float(1.0 / np.sqrt(G)), scalar2=None, op0=OP.mult))
        self.buf("scol").w = t9
        t_ep = E.i("dve", lambda e: e.memset(self.colpad.ap()[:, 24:25], EPS))
        self.buf("epscol").w = t_ep
        self.buf("idn").w = t_idn
        self.buf("idnr").w = t_idnr
        self.buf("idnb").w = t_idnb
        self.buf("onesg").w = t_og
        self.E.barrier()

    def phase_a1(self, b):
        E, nc = self.E, self.nc
        ps = self.ps.ap()
        ucols = self.a1_cols.ap()

        self.wr("dve", ["a1lg", "a1inv"])
        tzp = E.i("dve", lambda e: e.tensor_scalar(
            out=self.a1_lg.ap(), in0=self.zeros32.ap(), scalar1=1.0,
            scalar2=None, op0=OP.mult))
        tzp = E.i("dve", lambda e: e.tensor_scalar(
            out=self.a1_inv.ap(), in0=self.zeros32.ap(), scalar1=1.0,
            scalar2=None, op0=OP.mult))
        self.wdone(["a1lg", "a1inv"], tzp)

        for jg in range(NJG):
            t_sq = t_s = None
            u_toks = []
            for j4 in range(4):
                jt = jg * 4 + j4
                us = self.a1_u.ap()[:, (jt % 4) * 1024:(jt % 4) * 1024 + 1024]
                nm_u = f"a1u{jt % 4}"
                t_u = E.dma("sy", lambda e, _o=us, _b=b, _jt=jt: e.dma_start(
                    out=_o, in_=self.U[_b, _jt * 128:(_jt + 1) * 128, :]),
                    waits=[self.buf(nm_u).w] + self.buf(nm_u).r, slot=nm_u)
                self.buf(nm_u).w = t_u
                self.buf(nm_u).r = []
                u_toks.append((us, nm_u))
                E.ws("act", [t_u])
                self.wr("act", [f"a1scr"])
                t_sq = E.i("act", lambda e, _u=us, _j=j4: e.activation(
                    out=self.a1_scr.ap(), in_=_u, func=AF.Square))
                self.wdone(["a1scr"], t_sq)
                E.ws("dve", [t_u])
                t_s = E.i("dve", lambda e, _u=us, _j=j4: e.reduce_sum(
                    ucols[:, 4 + _j:5 + _j], _u, AX.X))
                self.rdone([nm_u], t_s)
                E.ws("dve", [t_sq])
                t_sq = E.i("dve", lambda e, _j=j4: e.reduce_sum(
                    ucols[:, 0 + _j:1 + _j], self.a1_scr.ap(), AX.X))
                self.rdone(["a1scr"], t_sq)
            E.wf("dve", t_s)
            t_m = E.i("dve", lambda e: e.tensor_scalar_mul(
                ucols[:, 8:12], ucols[:, 4:8], 1.0 / D))
            E.ws("dve", [t_sq])
            E.wf("dve", t_sq)
            t_m2 = E.i("dve", lambda e: e.tensor_scalar_mul(
                ucols[:, 12:16], ucols[:, 0:4], 1.0 / D))
            E.wf("dve", t_m)
            t_ms = E.i("dve", lambda e: e.tensor_mul(
                ucols[:, 16:20], ucols[:, 8:12], ucols[:, 8:12]))
            E.wf("dve", t_ms)
            t_v = E.i("dve", lambda e: e.tensor_sub(
                ucols[:, 20:24], ucols[:, 12:16], ucols[:, 16:20]))
            E.ws("act", [t_v])
            t_l = E.i("act", lambda e: e.activation(
                out=ucols[:, 24:28], in_=ucols[:, 20:24], func=AF.Ln,
                bias=self.colpad.ap()[:, 24:25]))
            t_is = E.i("act", lambda e: e.activation(
                out=ucols[:, 24:28], in_=ucols[:, 24:28], func=AF.Exp,
                scale=-0.5))
            E.ws("dve", [t_is])
            t_nm = E.i("dve", lambda e: e.tensor_mul(
                ucols[:, 28:32], ucols[:, 8:12], ucols[:, 24:28]))
            E.wf("dve", t_nm)
            t_nm2 = E.i("dve", lambda e: e.tensor_scalar_mul(
                ucols[:, 28:32], ucols[:, 28:32], -1.0))
            for j4 in range(4):
                us, nm_u = u_toks[j4]
                xs = self.a1_xtm.ap()[:, j4 * 1024:(j4 + 1) * 1024]
                E.ws("act", [t_nm2])
                self.wr("act", [f"a1xtm{j4}"])
                t_xn = E.i("act", lambda e, _u=us, _j=j4, _x=xs: e.activation(
                    out=_x, in_=_u, func=AF.Identity,
                    scale=ucols[:, 24 + _j:25 + _j],
                    bias=ucols[:, 28 + _j:29 + _j]))
                self.wdone([f"a1xtm{j4}"], t_xn)
                self.rdone([nm_u], t_xn)
            if DEBUG_TAPS:
                E.ws("dve", [t_is, t_sq])
                E.wf("dve", t_nm2)
                tdbg = E.i("dve", lambda e, _jg=jg: e.tensor_copy(
                    out=self.a1_dbg.ap()[:, _jg * 32:(_jg + 1) * 32],
                    in_=ucols[:, 0:32]))
                E.dma("sy", lambda e, _jg=jg, _b=b: e.dma_start(
                    out=self.sc_st[_b, _jg],
                    in_=self.a1_dbg.ap()[:, _jg * 32:(_jg + 1) * 32]),
                    waits=[tdbg])

            for db in range(DB):
                bank = (db % 2) * 512
                self.ps_wr("pe", bank, 512)
                E.ws("pe", [self.buf("idnr").w])
                tts = []
                for j4 in range(4):
                    self.rd("pe", [f"a1xtm{j4}"])
                    tt = E.i("pe", lambda e, _j=j4, _db=db, _bk=bank: e.transpose(
                        ps[:, _bk + _j * 128:_bk + (_j + 1) * 128].bitcast(F32R),
                        self.a1_xtm.ap()[:, _j * 1024 + _db * 128:
                                         _j * 1024 + _db * 128 + 128],
                        self.idn_r.ap()))
                    tts.append(tt)
                    self.rdone([f"a1xtm{j4}"], tt)
                self.ps_wdone(bank, 512, tts[-1])
                slot = db % 2
                xf = self.a1_xfm.ap()[:, slot * 512:slot * 512 + 512]
                self.ps_rd("act", bank, 512)
                self.wr("act", [f"a1xfm{slot}"])
                t_ev = E.i("act", lambda e, _x=xf, _bk=bank: e.activation(
                    out=_x, in_=ps[:, _bk:_bk + 512], func=AF.Identity))
                self.ps_rdone(bank, 512, t_ev)
                self.wdone([f"a1xfm{slot}"], t_ev)
                E.ws("dve", [t_ev])
                t_cb = E.i("dve", lambda e, _x=xf, _db=db, _jg=jg: e.tensor_copy(
                    out=self.a1_xnb.ap()[:, _db * T + _jg * 512:
                                         _db * T + _jg * 512 + 512], in_=_x))
                self.rdone([f"a1xfm{slot}"], t_cb)
                self.buf("xnb").w = t_cb
                t_sp = E.dma("sy", lambda e, _x=xf, _b=b, _db=db, _jg=jg:
                             e.dma_start(out=self.sc_xn[_b, _db, :,
                                                        _jg * 512:(_jg + 1) * 512],
                                         in_=_x), waits=[t_ev],
                             slot=f"a1xfmd{slot}")
                self.rdone([f"a1xfm{slot}"], t_sp)
                E.ws("pe", [t_ev, self.buf("WrT").w])
                if db == 0:
                    self.ps_wr("pe", 1024, 512)
                t_r = E.i("pe", lambda e, _x=xf, _db=db: e.matmul(
                    ps[:, 1024:1536],
                    self.WrT.ap()[:, _db * 128:(_db + 1) * 128], _x,
                    start=(_db == 0), stop=(_db == DB - 1)))
                self.rdone([f"a1xfm{slot}"], t_r)
                if db == DB - 1:
                    self.ps_wdone(1024, 512, t_r)

            self.ps_rd("act", 1024, 512)
            self.wr("act", ["a1zr"])
            t_zr = E.i("act", lambda e: e.activation(
                out=self.a1_zr.ap(), in_=ps[:, 1024:1536], func=AF.Identity))
            self.ps_rdone(1024, 512, t_zr)
            self.wdone(["a1zr"], t_zr)
            t_zq = E.i("act", lambda e: e.activation(
                out=self.a1_zsq.ap(), in_=self.a1_zr.ap(), func=AF.Square))
            E.ws("pe", [t_zq, self.buf("onesg").w])
            self.ps_wr("pe", 1536, 512)
            t_ns = E.i("pe", lambda e: e.matmul(
                ps[0:1, 1536:2048], self.onesg.ap(), self.a1_zsq.ap(),
                start=True, stop=True))
            self.ps_wdone(1536, 512, t_ns)
            self.ps_rd("act", 1536, 512)
            t_nr = E.i("act", lambda e: e.activation(
                out=self.a1_nrm.ap(), in_=ps[0:1, 1536:2048], func=AF.Ln))
            self.ps_rdone(1536, 512, t_nr)
            t_iv = E.i("act", lambda e: e.activation(
                out=self.a1_inv.ap()[0:1, :], in_=self.a1_nrm.ap(),
                func=AF.Exp, scale=-0.5))
            E.ws("pe", [t_iv, self.buf("a1inv").w])
            self.ps_wr("pe", 1536, 512)
            t_ir = E.i("pe", lambda e: e.matmul(
                ps[:, 1536:2048], self.ones1r.ap(), self.a1_inv.ap(),
                start=True, stop=True))
            self.ps_wdone(1536, 512, t_ir)
            self.ps_rd("dve", 1536, 512)
            E.ws("dve", [t_zr])
            t_zn = E.i("dve", lambda e: e.tensor_mul(
                self.a1_znorm.ap(), self.a1_zr.ap(), ps[:, 1536:2048]))
            self.ps_rdone(1536, 512, t_zn)
            E.ws("pe", [t_zn, self.buf("cnT").w])
            self.ps_wr("pe", 1024, 512)
            t_lg = E.i("pe", lambda e: e.matmul(
                ps[0:16, 1024:1536], self.cnT.ap(), self.a1_znorm.ap(),
                start=True, stop=True))
            self.ps_wdone(1024, 512, t_lg)
            self.ps_rd("act", 1024, 512)
            t_le = E.i("act", lambda e: e.activation(
                out=self.a1_lg.ap()[0:16, :], in_=ps[0:16, 1024:1536],
                func=AF.Identity))
            self.ps_rdone(1024, 512, t_le)
            E.ws("pe", [t_le, self.buf("a1lg").w])
            self.ps_wr("pe", 1024, 128)
            tl = None
            for s in range(4):
                tl = E.i("pe", lambda e, _s=s: e.transpose(
                    ps[:, 1024 + _s * 32:1024 + (_s + 1) * 32].bitcast(F32R),
                    self.a1_lg.ap()[:, _s * 128:(_s + 1) * 128],
                    self.idn_r.ap()[0:32, 0:32]))
            self.ps_wdone(1024, 128, tl)
            self.ps_rd("act", 1024, 128)
            t_lt = E.i("act", lambda e: e.activation(
                out=self.a1_lgt.ap(), in_=ps[:, 1024:1152], func=AF.Identity))
            self.ps_rdone(1024, 128, t_lt)
            for s in range(4):
                lg = self.a1_lgt.ap()[:, s * 32:s * 32 + 16]
                smc = self.a1_smc.ap()
                msk = self.a1_msk.ap()
                e2 = self.a1_e2.ap()
                wv = self.a1_wtm.ap()[:, s * 16:(s + 1) * 16]
                E.ws("dve", [t_lt])
                t_mx = E.i("dve", lambda e, _l=lg, _c=smc: e.reduce_max(
                    _c[:, 0:1], _l, AX.X))
                E.wf("dve", t_mx)
                t_bi = E.i("dve", lambda e, _c=smc: e.tensor_scalar(
                    out=_c[:, 1:2], in0=_c[:, 0:1],
                    scalar1=self.scol.ap(), scalar2=-1.0, op0=OP.mult,
                    op1=OP.mult))
                E.ws("act", [t_bi, t_lt])
                t_ex = E.i("act", lambda e, _l=lg, _c=smc: e.activation(
                    out=_l, in_=_l, func=AF.Exp, scale=self.scol.ap(),
                    bias=_c[:, 1:2]))
                E.ws("dve", [t_ex])
                t_sm = E.i("dve", lambda e, _l=lg, _c=smc: e.reduce_sum(
                    _c[:, 2:3], _l, AX.X))
                E.ws("act", [t_sm])
                t_ln = E.i("act", lambda e, _c=smc: e.activation(
                    out=_c[:, 3:4], in_=_c[:, 2:3], func=AF.Ln))
                t_rc = E.i("act", lambda e, _c=smc: e.activation(
                    out=_c[:, 3:4], in_=_c[:, 3:4], func=AF.Exp, scale=-1.0))
                t_m1 = E.i("dve", lambda e, _l=lg, _c=smc: e.reduce_max(
                    _c[:, 4:5], _l, AX.X))
                E.wf("dve", t_m1)
                t_lt2 = E.i("dve", lambda e, _l=lg, _c=smc, _m=msk:
                            e.tensor_scalar(
                    out=_m, in0=_l, scalar1=_c[:, 4:5], scalar2=None,
                    op0=OP.is_lt))
                E.wf("dve", t_lt2)
                t_p2 = E.i("dve", lambda e, _l=lg, _m=msk, _e2=e2:
                           e.tensor_mul(_e2, _l, _m))
                E.wf("dve", t_p2)
                t_m2 = E.i("dve", lambda e, _c=smc, _e2=e2: e.reduce_max(
                    _c[:, 5:6], _e2, AX.X))
                E.wf("dve", t_m2)
                t_ge = E.i("dve", lambda e, _l=lg, _c=smc, _m=msk:
                           e.tensor_scalar(
                    out=_m, in0=_l, scalar1=_c[:, 5:6], scalar2=None,
                    op0=OP.is_ge))
                E.wf("dve", t_ge)
                t_wu = E.i("dve", lambda e, _l=lg, _m=msk, _e2=e2:
                           e.tensor_mul(_e2, _l, _m))
                E.ws("dve", [t_rc])
                E.wf("dve", t_wu)
                t_w = E.i("dve", lambda e, _w=wv, _c=smc, _e2=e2:
                          e.tensor_scalar(
                    out=_w, in0=_e2, scalar1=_c[:, 3:4], scalar2=None,
                    op0=OP.mult))
                self.buf(f"a1w{s}").w = t_w
            self.ps_wr("pe", 1024, 512)
            tws = []
            for s in range(4):
                E.ws("pe", [self.buf(f"a1w{s}").w, self.buf("idn").w])
                tw = E.i("pe", lambda e, _s=s: e.transpose(
                    ps[0:16, 1024 + _s * 128:1024 + (_s + 1) * 128],
                    self.a1_wtm.ap()[:, _s * 16:(_s + 1) * 16],
                    self.idn_f.ap()))
                tws.append(tw)
            self.ps_wdone(1024, 512, tws[-1])
            self.ps_rd("act", 1024, 512)
            self.wr("act", [f"wTb{jg}"])
            t_wt = E.i("act", lambda e, _jg=jg: e.activation(
                out=self.wTb.ap()[0:16, _jg * 512:(_jg + 1) * 512],
                in_=ps[0:16, 1024:1536], func=AF.Identity))
            self.ps_rdone(1024, 512, t_wt)
            self.wdone([f"wTb{jg}"], t_wt)

        if DEBUG_TAPS:
            E.dma("sy", lambda e, _b=b: e.dma_start(
                out=self.sc_wt[_b], in_=self.wTb.ap()),
                waits=[self.buf(f"wTb{j}").w for j in range(4)])
            E.dma("sy", lambda e, _b=b: e.dma_start(
                out=self.sc_xnb[_b], in_=self.a1_xnb.ap()),
                waits=[self.buf("xnb").w])

        def wi_load(n, slot):
            return E.dma("gps", lambda e, _n=n, _s=slot: e.dma_start(
                out=self.a1_wi.ap()[:, _s * 4096:(_s + 1) * 4096].rearrange("p (k h) -> p k h", k=8),
                in_=self.IEXP[_n].rearrange("(k p) h -> p k h", p=128)),
                waits=[self.buf(f"a1wi{slot}").w] + self.buf(f"a1wi{slot}").r,
                slot=f"a1wi{slot}")

        tok = wi_load(0, 0)
        self.buf("a1wi0").w = tok
        self.buf("a1wi0").r = []
        for n in range(NH):
            slot = n % 2
            if n + 1 < NH:
                nx = wi_load(n + 1, (n + 1) % 2)
                self.buf(f"a1wi{(n + 1) % 2}").w = nx
                self.buf(f"a1wi{(n + 1) % 2}").r = []
            wi = self.a1_wi.ap()[:, slot * 4096:(slot + 1) * 4096]

            def emit_out(hb, half, tg):
                E.ws("pe", [self.buf("woutE").w, tg])
                for t2 in range(2):
                    tc = half * 2 + t2
                    if hb == 0:
                        self.ps_wr("pe", tc * 512, 512)
                    hid = self.a1_hid.ap()[:, tc * 512:(tc + 1) * 512]
                    tpo = E.i("pe", lambda e, _h=hid, _n=n, _hb=hb, _tc=tc:
                              e.matmul(
                        ps[0:64, _tc * 512:_tc * 512 + 512],
                        self.woutE.ap()[:, (_n * 4 + _hb) * 64:
                                        (_n * 4 + _hb + 1) * 64],
                        _h, start=(_hb == 0), stop=(_hb == 3)))
                    self.rdone([f"a1hid{tc}"], tpo)
                    if hb == 3:
                        self.ps_wdone(tc * 512, 512, tpo)

            prev = None
            for u in range(8):
                hb, half = u // 2, u % 2
                pbase = 2048 + half * 1024
                for t2 in range(2):
                    self.ps_wr("pe", pbase + t2 * 512, 512)
                E.ws("pe", [self.buf(f"a1wi{slot}").w, self.buf("xnb").w])
                tm = None
                for k in range(8):
                    for t2 in range(2):
                        tc = half * 2 + t2
                        tm = E.i("pe", lambda e, _k=k, _hb=hb, _tc=tc,
                                 _pb=pbase, _t2=t2, _w=wi: e.matmul(
                            ps[:, _pb + _t2 * 512:_pb + _t2 * 512 + 512],
                            _w[:, _k * 512 + _hb * 128:_k * 512 + _hb * 128 + 128],
                            self.a1_xnb.ap()[:, _k * T + _tc * 512:
                                             _k * T + _tc * 512 + 512],
                            start=(_k == 0), stop=(_k == 7)))
                self.rdone([f"a1wi{slot}"], tm)
                for t2 in range(2):
                    self.ps_wdone(pbase + t2 * 512, 512, tm)
                self.ps_rd("act", pbase, 1024)
                self.wr("act", [f"a1hid{half * 2}", f"a1hid{half * 2 + 1}"])
                tg = E.i("act", lambda e, _pb=pbase, _hf=half: e.activation(
                    out=self.a1_hid.ap()[:, _hf * 1024:(_hf + 1) * 1024],
                    in_=ps[:, _pb:_pb + 1024], func=AF.Gelu))
                self.ps_rdone(pbase, 1024, tg)
                self.wdone([f"a1hid{half * 2}", f"a1hid{half * 2 + 1}"], tg)
                if DEBUG_TAPS and b == 0 and n == 0:
                    tdh = E.dma("sy", lambda e, _hb=hb, _hf=half: e.dma_start(
                        out=self.sc_hid[_hb, :, _hf * 1024:(_hf + 1) * 1024],
                        in_=self.a1_hid.ap()[:, _hf * 1024:(_hf + 1) * 1024]),
                        waits=[tg])
                    self.rdone([f"a1hid{half * 2}"], tdh)
                    self.rdone([f"a1hid{half * 2 + 1}"], tdh)
                if prev is not None:
                    emit_out(*prev)
                prev = (hb, half, tg)
            emit_out(*prev)
            fdst = self.a1_fe if n % 2 == 0 else self.a1_fo
            fnm = "a1fe" if n % 2 == 0 else "a1fo"
            for tc in range(4):
                wbank = 2048 + (tc % 2) * 512
                self.ps_wr("pe", wbank, 512)
                E.ws("pe", [self.buf("pairs").w, self.buf(f"wTb{tc}").w])
                trw = E.i("pe", lambda e, _n=n, _tc=tc, _wb=wbank: e.matmul(
                    ps[0:64, _wb:_wb + 512],
                    self.psel_b.ap()[:, _n * 64:(_n + 1) * 64],
                    self.wTb.ap()[:, _tc * 512:(_tc + 1) * 512],
                    start=True, stop=True))
                self.ps_wdone(wbank, 512, trw)
                ws = tc % 2
                wrp = self.a1_wrep.ap()[0:64, tc * 512:(tc + 1) * 512]
                self.ps_rd("act", wbank, 512)
                if tc < 2:
                    self.wr("act", ["a1wrep"])
                tew = E.i("act", lambda e, _w=wrp, _wb=wbank: e.activation(
                    out=_w, in_=ps[0:64, _wb:_wb + 512], func=AF.Identity))
                self.ps_rdone(wbank, 512, tew)
                if tc == 3:
                    self.wdone(["a1wrep"], tew)
                obank = tc * 512
                ffn = fdst.ap()[:, tc * 512:(tc + 1) * 512]
                self.ps_rd("dve", obank, 512)
                E.ws("dve", [tew])
                if tc == 0:
                    self.wr("dve", [fnm])
                tsc = E.i("dve", lambda e, _f=ffn, _w=wrp, _ob=obank:
                          e.tensor_mul(_f, ps[0:64, _ob:_ob + 512], _w))
                self.ps_rdone(obank, 512, tsc)
                self.rdone(["a1wrep"], tsc)
                if tc == 3:
                    self.wdone([fnm], tsc)
                if DEBUG_TAPS and b == 0 and n <= 1:
                    tdf = E.dma("sy", lambda e, _f=ffn, _tc=tc, _n=n:
                                e.dma_start(
                        out=self.sc_fp.rearrange(
                            "(h p) t -> h p t", h=2)[_n % 2, :,
                                                     _tc * 512:(_tc + 1) * 512],
                        in_=_f), waits=[tsc])
                    self.rdone([fnm], tdf)

            if n % 2 == 1:
                pr = n // 2
                for tc in range(4):
                    pbank = 3072 + (tc % 2) * 512
                    self.ps_wr("pe", pbank, 512)
                    self.rd("pe", ["a1fe", "a1fo"])
                    E.ws("pe", [self.buf("idn").w])
                    tp = None
                    for j4 in range(4):
                        for hh, fsrc in ((0, self.a1_fe), (1, self.a1_fo)):
                            tp = E.i("pe", lambda e, _f=fsrc, _j=j4, _pb=pbank,
                                     _tc=tc, _hh=hh: e.transpose(
                                ps[:, _pb + _j * 128 + _hh * 64:
                                   _pb + _j * 128 + _hh * 64 + 64],
                                _f.ap()[:, _tc * 512 + _j * 128:
                                        _tc * 512 + (_j + 1) * 128],
                                self.idn_f.ap()[0:64, 0:64]))
                    self.rdone(["a1fe", "a1fo"], tp)
                    self.ps_wdone(pbank, 512, tp)
                    osl = tc % 2
                    ost = self.a1_ops.ap()[:, osl * 512:(osl + 1) * 512]
                    self.ps_rd("act", pbank, 512)
                    self.wr("act", [f"a1ops{osl}"])
                    teo = E.i("act", lambda e, _o=ost, _pb=pbank: e.activation(
                        out=_o, in_=ps[:, _pb:_pb + 512], func=AF.Identity))
                    self.ps_rdone(pbank, 512, teo)
                    self.wdone([f"a1ops{osl}"], teo)
                    tdo = E.dma("sy", lambda e, _o=ost, _b=b, _pr=pr, _tc=tc:
                                e.dma_start(
                        out=self.sc_op[_b, _tc * 512:(_tc + 1) * 512,
                                       _pr * 128:(_pr + 1) * 128]
                        .rearrange("(j p) c -> p j c", p=128),
                        in_=_o.rearrange("p (j c) -> p j c", c=128)),
                        waits=[teo], slot=f"a1ops{osl}")
                    self.rdone([f"a1ops{osl}"], tdo)

    def phase_a2a(self, b):
        E = self.E
        ps = self.ps.ap()

        t_w = E.dma("gps", lambda e: e.dma_start(
            out=self.a2_win.ap().rearrange("p (k e) -> p k e", k=8),
            in_=self.WIN.rearrange("(k p) e -> p k e", p=128)),
            waits=[self.buf("a2win").w] + self.buf("a2win").r, slot="a2win")
        self.buf("a2win").w = t_w
        self.buf("a2win").r = []
        t_wg = E.dma("gps", lambda e: e.dma_start(
            out=self.a2_wg.ap().rearrange("p (k e) -> p k e", k=8),
            in_=self.WG.rearrange("(k p) e -> p k e", p=128)),
            waits=[self.buf("a2wg").w] + self.buf("a2wg").r, slot="a2wg")
        self.buf("a2wg").w = t_wg
        self.buf("a2wg").r = []
        E.ws("dve", [self.buf("cw").w, self.buf("idnr").w])
        self.wr("dve", ["a2diag"])
        td = None
        for db in range(DB):
            for tap in range(3):
                idx = db * 3 + tap
                td = E.i("dve", lambda e, _i=idx: e.tensor_scalar(
                    out=self.a2_diag.ap()[:, _i * 128:(_i + 1) * 128],
                    in0=self.idn_r.ap(), scalar1=self.cwcol.ap()[:, _i:_i + 1],
                    scalar2=None, op0=OP.mult))
        self.wdone(["a2diag"], td)
        t_hz2 = None
        for _db in range(DB):
            for _c in (0, CA + 1):
                t_hz2 = E.i("dve", lambda e, _d=_db, _cc=_c: e.tensor_scalar(
                    out=self.a2_xpre.ap()[:, _d * (CA + 2) + _cc:
                                          _d * (CA + 2) + _cc + 1],
                    in0=self.colpad.ap()[:, 8:9], scalar1=0.0, scalar2=None,
                    op0=OP.mult))
        self.buf("a2halo").w = t_hz2

        for ch in range(NCA):
            xs = ch % 2
            xn = self.a2_xn.ap()[:, xs * DB * CA:(xs + 1) * DB * CA]
            t_x = E.dma("sy", lambda e, _x=xn, _b=b, _ch=ch: e.dma_start(
                out=_x.rearrange("p (k c) -> p k c", k=DB),
                in_=self.sc_xn[_b, :, :, _ch * CA:(_ch + 1) * CA]
                .rearrange("k p c -> p k c")),
                waits=[self.buf(f"a2xn{xs}").w] + self.buf(f"a2xn{xs}").r,
                slot=f"a2xn{xs}")
            self.buf(f"a2xn{xs}").w = t_x
            self.buf(f"a2xn{xs}").r = []

            cslot = ch % 2
            cpre = self.a2_cpre.ap()[:, cslot * DB * CA:(cslot + 1) * DB * CA]

            for e_ in range(DB):
                bank = (e_ % 4) * 512
                self.ps_wr("pe", bank, 512)
                E.ws("pe", [t_x, self.buf("a2win").w])
                tm = None
                for k in range(8):
                    tm = E.i("pe", lambda e, _k=k, _e=e_, _x=xn, _bk=bank:
                             e.matmul(
                        ps[:, _bk:_bk + 512],
                        self.a2_win.ap()[:, _k * 1024 + _e * 128:
                                         _k * 1024 + _e * 128 + 128],
                        _x[:, _k * CA:(_k + 1) * CA],
                        start=(_k == 0), stop=(_k == 7)))
                self.rdone([f"a2xn{xs}"], tm)
                self.ps_wdone(bank, 512, tm)
                xpt = self.a2_xpre.ap()[:, e_ * (CA + 2):(e_ + 1) * (CA + 2)]
                xp = xpt[:, 1:CA + 1]
                self.ps_rd("act", bank, 512)
                self.wr("act", [f"a2xp{e_}"])
                E.ws("act", [self.buf("bin").w])
                te = E.i("act", lambda e, _x=xp, _bk=bank, _e=e_: e.activation(
                    out=_x, in_=ps[:, _bk:_bk + 512], func=AF.Identity,
                    bias=self.bin_c.ap()[:, _e:_e + 1]))
                self.ps_rdone(bank, 512, te)
                self.wdone([f"a2xp{e_}"], te)

                if ch > 0:
                    pslot = (ch - 1) % 2
                    cpv = self.a2_cpre.ap()[:, pslot * DB * CA + e_ * CA:
                                            pslot * DB * CA + (e_ + 1) * CA]
                    E.ws("dve", [te, self.buf(f"a2cp{pslot}_{e_}").w,
                                 self.buf("cw").w])
                    tf = E.i("dve", lambda e, _c=cpv, _x=xpt, _e=e_:
                             e.scalar_tensor_tensor(
                        out=_c[:, CA - 1:CA], in0=_x[:, 1:2],
                        scalar=self.cwcol.ap()[:, _e * 3 + 2:_e * 3 + 3],
                        in1=_c[:, CA - 1:CA], op0=OP.mult, op1=OP.add))
                    self.rdone([f"a2xp{e_}"], tf)
                    E.ws("act", [tf])
                    ts = E.i("act", lambda e, _c=cpv: e.activation(
                        out=_c.bitcast(F32R), in_=_c, func=AF.Silu))
                    self.buf(f"a2cp{pslot}_{e_}").w = ts
                    tsp = E.dma("sy", lambda e, _c=cpv, _b=b, _e=e_, _ch=ch:
                                e.dma_start(
                        out=self.sc_x[_b, _e, :, (_ch - 1) * CA:_ch * CA],
                        in_=_c.bitcast(F32R)), waits=[ts],
                        slot=f"a2cp{pslot}")
                    self.buf(f"a2cp{pslot}_{e_}").r = [tsp]

                cbank = 2048 + (e_ % 4) * 512
                self.ps_wr("pe", cbank, 512)
                E.ws("pe", [te, self.buf("a2diag").w, self.buf("a2halo").w])
                dg = self.a2_diag.ap()
                t1 = E.i("pe", lambda e, _x=xpt, _e=e_, _bk=cbank: e.matmul(
                    ps[:, _bk:_bk + 512],
                    dg[:, (_e * 3 + 0) * 128:(_e * 3 + 1) * 128],
                    _x[:, 0:CA], start=True, stop=False))
                t2 = E.i("pe", lambda e, _x=xpt, _e=e_, _bk=cbank: e.matmul(
                    ps[:, _bk:_bk + 512],
                    dg[:, (_e * 3 + 1) * 128:(_e * 3 + 2) * 128],
                    _x[:, 1:CA + 1], start=False, stop=False))
                t4 = E.i("pe", lambda e, _x=xpt, _e=e_, _bk=cbank: e.matmul(
                    ps[:, _bk:_bk + 512],
                    dg[:, (_e * 3 + 2) * 128:(_e * 3 + 3) * 128],
                    _x[:, 2:CA + 2], start=False, stop=True))
                self.ps_wdone(cbank, 512, t4)
                self.rdone([f"a2xp{e_}"], t4)
                E.ws("act", [t4])
                th = E.i("act", lambda e, _x=xpt: e.activation(
                    out=_x[:, 0:1], in_=_x[:, CA:CA + 1], func=AF.Identity))
                if e_ == DB - 1:
                    self.buf("a2halo").w = th
                cpc = self.a2_cpre.ap()[:, cslot * DB * CA + e_ * CA:
                                        cslot * DB * CA + (e_ + 1) * CA]
                self.ps_rd("act", cbank, 512)
                self.wr("act", [f"a2cp{cslot}_{e_}"])
                E.ws("act", [self.buf("convb").w])
                tev = E.i("act", lambda e, _c=cpc, _bk=cbank, _e=e_: e.activation(
                    out=_c, in_=ps[:, _bk:_bk + 512], func=AF.Identity,
                    bias=self.convb_c.ap()[:, _e:_e + 1]))
                self.ps_rdone(cbank, 512, tev)
                self.wdone([f"a2cp{cslot}_{e_}"], tev)

                if ch == NCA - 1:
                    E.ws("act", [tev])
                    ts = E.i("act", lambda e, _c=cpc: e.activation(
                        out=_c.bitcast(F32R), in_=_c, func=AF.Silu))
                    self.buf(f"a2cp{cslot}_{e_}").w = ts
                    tsp = E.dma("sy", lambda e, _c=cpc, _b=b, _e=e_, _ch=ch:
                                e.dma_start(
                        out=self.sc_x[_b, _e, :, _ch * CA:(_ch + 1) * CA],
                        in_=_c.bitcast(F32R)), waits=[ts],
                        slot=f"a2cp{cslot}")
                    self.buf(f"a2cp{cslot}_{e_}").r = [tsp]

            zsl = 0
            zv = self.a2_z.ap()
            tz = None
            for e_ in range(DB):
                bank = (e_ % 4) * 512
                self.ps_wr("pe", bank, 512)
                E.ws("pe", [t_x, self.buf("a2wg").w])
                tm = None
                for k in range(8):
                    tm = E.i("pe", lambda e, _k=k, _e=e_, _x=xn, _bk=bank:
                             e.matmul(
                        ps[:, _bk:_bk + 512],
                        self.a2_wg.ap()[:, _k * 1024 + _e * 128:
                                        _k * 1024 + _e * 128 + 128],
                        _x[:, _k * CA:(_k + 1) * CA],
                        start=(_k == 0), stop=(_k == 7)))
                self.rdone([f"a2xn{xs}"], tm)
                self.ps_wdone(bank, 512, tm)
                self.ps_rd("act", bank, 512)
                if e_ == 0:
                    self.wr("act", [f"a2z{zsl}"])
                E.ws("act", [self.buf("bg").w])
                tz = E.i("act", lambda e, _e=e_, _z=zv, _bk=bank: e.activation(
                    out=_z[:, _e * CA:(_e + 1) * CA], in_=ps[:, _bk:_bk + 512],
                    func=AF.Silu, bias=self.bg_c.ap()[:, _e:_e + 1]))
                self.ps_rdone(bank, 512, tz)
                if e_ == DB - 1:
                    self.wdone([f"a2z{zsl}"], tz)
            tzs = E.dma("sy", lambda e, _z=zv, _b=b, _ch=ch: e.dma_start(
                out=self.sc_z[_b, :, :, _ch * CA:(_ch + 1) * CA]
                .rearrange("k p c -> p k c"),
                in_=_z.rearrange("p (k c) -> p k c", k=DB)), waits=[tz],
                slot="a2zsp")
            self.rdone([f"a2z{zsl}"], tzs)

    def phase_a2b_taps(self, b):
        E = self.E
        if not DEBUG_TAPS:
            return
        tb1 = E.dma("sy", lambda e, _b=b: e.dma_start(
            out=self.sc_bm[_b], in_=self.Bm.ap()),
            waits=[self.buf("Bm").w])
        self.rdone(["Bm"], tb1)
        tb2 = E.dma("sy", lambda e, _b=b: e.dma_start(
            out=self.sc_cm[_b], in_=self.Cm.ap()),
            waits=[self.buf("Cm").w])
        self.rdone(["Cm"], tb2)

    def phase_a2b(self, b):
        E = self.E
        ps = self.ps.ap()

        t_wd = E.dma("gps", lambda e: e.dma_start(
            out=self.b2_wdt.ap().rearrange("p (k e) -> p k e", k=8),
            in_=self.WDT.rearrange("(k p) e -> p k e", p=128)),
            waits=[self.buf("b2wdt").w] + self.buf("b2wdt").r, slot="b2wdt")
        self.buf("b2wdt").w = t_wd
        t_wb = E.dma("gps", lambda e: e.dma_start(
            out=self.b2_wb.ap().rearrange("p (k n) -> p k n", k=8),
            in_=self.WB.rearrange("(k p) n -> p k n", p=128)),
            waits=[self.buf("b2wb").w] + self.buf("b2wb").r, slot="b2wb")
        self.buf("b2wb").w = t_wb
        t_wc = E.dma("gps", lambda e: e.dma_start(
            out=self.b2_wc.ap().rearrange("p (k n) -> p k n", k=8),
            in_=self.WC.rearrange("(k p) n -> p k n", p=128)),
            waits=[self.buf("b2wc").w] + self.buf("b2wc").r, slot="b2wc")
        self.buf("b2wc").w = t_wc

        for ch in range(NCB):
            xs = ch % 2
            xv = self.b2_x.ap()[:, xs * DB * CB:(xs + 1) * DB * CB]
            t_x = E.dma("sy", lambda e, _x=xv, _b=b, _ch=ch: e.dma_start(
                out=_x.rearrange("p (k c) -> p k c", k=DB),
                in_=self.sc_x[_b, :, :, _ch * CB:(_ch + 1) * CB]
                .rearrange("k p c -> p k c")),
                waits=[self.buf(f"b2x{xs}").w] + self.buf(f"b2x{xs}").r,
                slot=f"b2x{xs}")
            self.buf(f"b2x{xs}").w = t_x
            self.buf(f"b2x{xs}").r = []

            sg = self.b2_sg.ap()[:, xs * DB * CB:(xs + 1) * DB * CB]
            for e_ in range(DB):
                bank = (e_ % 2) * 512
                self.ps_wr("pe", bank, 512)
                E.ws("pe", [t_x, self.buf("b2wdt").w])
                tm = None
                for k in range(8):
                    tm = E.i("pe", lambda e, _k=k, _e=e_, _x=xv, _bk=bank:
                             e.matmul(
                        ps[:, _bk:_bk + CB],
                        self.b2_wdt.ap()[:, _k * 1024 + _e * 128:
                                         _k * 1024 + _e * 128 + 128],
                        _x[:, _k * CB:(_k + 1) * CB],
                        start=(_k == 0), stop=(_k == 7)))
                self.rdone([f"b2x{xs}"], tm)
                self.ps_wdone(bank, 512, tm)
                self.ps_rd("act", bank, 512)
                self.wr("act", [f"b2sg{xs}"]) if e_ == 0 else None
                E.ws("act", [self.buf("nbdt").w])
                te = E.i("act", lambda e, _e=e_, _s=sg, _bk=bank: e.activation(
                    out=_s[:, _e * CB:(_e + 1) * CB], in_=ps[:, _bk:_bk + CB],
                    func=AF.Sigmoid, scale=-1.0,
                    bias=self.nbdt_c.ap()[:, _e:_e + 1]))
                self.ps_rdone(bank, 512, te)
                if e_ == DB - 1:
                    self.wdone([f"b2sg{xs}"], te)
            tl = E.i("act", lambda e, _s=sg: e.activation(
                out=_s, in_=_s, func=AF.Ln))
            self.buf(f"b2sg{xs}").w = tl
            tdt = E.dma("sy", lambda e, _s=sg, _b=b, _ch=ch: e.dma_start(
                out=self.sc_dt[_b, :, :, _ch * CB:(_ch + 1) * CB]
                .rearrange("k p c -> p k c"),
                in_=_s.rearrange("p (k c) -> p k c", k=DB)), waits=[tl],
                slot=f"b2sg{xs}d")
            self.rdone([f"b2sg{xs}"], tdt)
            dx = self.b2_dtx.ap()[:, xs * DB * CB:(xs + 1) * DB * CB]
            E.ws("dve", [tl, t_x])
            self.wr("dve", [f"b2dtx{xs}"])
            tx = E.i("dve", lambda e, _d=dx, _s=sg, _x=xv: e.tensor_mul(
                _d, _s, _x.bitcast(F32)))
            self.rdone([f"b2x{xs}"], tx)
            self.rdone([f"b2sg{xs}"], tx)
            self.wdone([f"b2dtx{xs}"], tx)
            tdx = E.dma("sy", lambda e, _d=dx, _b=b, _ch=ch: e.dma_start(
                out=self.sc_dtx[_b, :, :, _ch * CB:(_ch + 1) * CB]
                .rearrange("k p c -> p k c"),
                in_=_d.rearrange("p (k c) -> p k c", k=DB)), waits=[tx],
                slot=f"b2dtx{xs}d")
            self.rdone([f"b2dtx{xs}"], tdx)

            for (wsb, dst, biascol, scl, nm) in (
                    (self.b2_wb, self.Bm, self.nbb_c, -1.0, "Bm"),
                    (self.b2_wc, self.Cm, self.bc_c, 1.0, "Cm")):
                bank = 1024
                self.ps_wr("pe", bank, 512)
                E.ws("pe", [t_x, self.buf("b2wb").w, self.buf("b2wc").w])
                tm = None
                for k in range(8):
                    tm = E.i("pe", lambda e, _k=k, _w=wsb, _x=xv, _bk=bank:
                             e.matmul(
                        ps[0:16, _bk:_bk + CB],
                        _w.ap()[:, _k * 16:(_k + 1) * 16],
                        _x[:, _k * CB:(_k + 1) * CB],
                        start=(_k == 0), stop=(_k == 7)))
                self.rdone([f"b2x{xs}"], tm)
                self.ps_wdone(bank, 512, tm)
                self.ps_rd("act", bank, 512)
                self.wr("act", [nm])
                E.ws("act", [self.buf("nbb").w, self.buf("bc").w])
                tb = E.i("act", lambda e, _d=dst, _c=biascol, _s=scl, _b2=b,
                         _ch=ch, _bk=bank: e.activation(
                    out=_d.ap()[0:16, _ch * CB:(_ch + 1) * CB],
                    in_=ps[0:16, _bk:_bk + CB], func=AF.Identity, scale=_s,
                    bias=_c.ap()))
                self.ps_rdone(bank, 512, tb)
                self.buf(nm).w = tb


    def phase_b(self, b):
        E = self.E
        ps = self.ps.ap()

        for q in range(4):
            ns = list(range(4 * q, 4 * q + 4))
            for i, n in enumerate(ns):
                for (src, dstbuf, dst) in ((self.Bm, "brep", self.s_brep),
                                           (self.Cm, "crep", self.s_crep)):
                    for tc in range(4):
                        bank = (tc % 2) * 512
                        self.ps_wr("pe", bank, 512)
                        E.ws("pe", [self.buf("Bm").w, self.buf("Cm").w,
                                    self.buf("osel").w])
                        tr = E.i("pe", lambda e, _s=src, _n=n, _tc=tc, _bk=bank:
                                 e.matmul(
                            ps[:, _bk:_bk + 512],
                            self.osel_b.ap()[:, _n * 128:(_n + 1) * 128],
                            _s.ap()[:, _tc * 512:(_tc + 1) * 512],
                            start=True, stop=True))
                        self.ps_wdone(bank, 512, tr)
                        self.ps_rd("act", bank, 512)
                        if tc == 0:
                            self.wr("act", [f"{dstbuf}{i}"])
                        tv = E.i("act", lambda e, _d=dst, _i=i, _tc=tc, _bk=bank:
                                 e.activation(
                            out=_d.ap()[:, _i * T + _tc * 512:
                                        _i * T + (_tc + 1) * 512],
                            in_=ps[:, _bk:_bk + 512], func=AF.Identity))
                        self.ps_rdone(bank, 512, tv)
                        if tc == 3:
                            self.wdone([f"{dstbuf}{i}"], tv)
                            if DEBUG_TAPS and b == 0:
                                scd = (self.sc_brep if dstbuf == "brep"
                                       else self.sc_crep)
                                tdr = E.dma("sy", lambda e, _d=dst, _i=i, _n=n,
                                            _s=scd: e.dma_start(
                                    out=_s[_n],
                                    in_=_d.ap()[:, _i * T:(_i + 1) * T]),
                                    waits=[tv])
                                self.rdone([f"{dstbuf}{i}"], tdr)

            def load_db(db):
                sl = db % 2
                tdt = E.dma("act", lambda e, _b=b, _db=db, _sl=sl: e.dma_start(
                    out=self.s_dt.ap()[:, _sl * T:(_sl + 1) * T],
                    in_=self.sc_dt[_b, _db, :, :]),
                    waits=[self.buf(f"sdt{sl}").w] + self.buf(f"sdt{sl}").r,
                    slot=f"sdt{sl}")
                self.buf(f"sdt{sl}").w = tdt
                self.buf(f"sdt{sl}").r = []
                tdx = E.dma("act", lambda e, _b=b, _db=db, _sl=sl: e.dma_start(
                    out=self.s_dtx.ap()[:, _sl * T:(_sl + 1) * T],
                    in_=self.sc_dtx[_b, _db, :, :]),
                    waits=[self.buf(f"sdx{sl}").w] + self.buf(f"sdx{sl}").r,
                    slot=f"sdx{sl}")
                self.buf(f"sdx{sl}").w = tdx
                self.buf(f"sdx{sl}").r = []

            load_db(0)
            for db in range(DB):
                if db + 1 < DB:
                    load_db(db + 1)
                sl = db % 2
                dtv = self.s_dt.ap()[:, sl * T:(sl + 1) * T]
                dxv = self.s_dtx.ap()[:, sl * T:(sl + 1) * T]

                if q > 0:
                    E.ws("pe", [self.buf(f"ysb{db}").w, self.buf("idnr").w])
                    for tc in range(4):
                        self.ps_wr("pe", tc * 512, 512)
                    tj = None
                    for tc in range(4):
                        tj = E.i("pe", lambda e, _db=db, _tc=tc: e.matmul(
                            ps[:, _tc * 512:(_tc + 1) * 512], self.idn_r.ap(),
                            self.s_ysb.ap()[:, _db * T + _tc * 512:
                                            _db * T + (_tc + 1) * 512],
                            start=True, stop=False))
                    self.rdone([f"ysb{db}"], tj)
                    for tc in range(4):
                        self.ps_wdone(tc * 512, 512, tj)
                else:
                    for tc in range(4):
                        self.ps_wr("pe", tc * 512, 512)

                for i, n in enumerate(ns):
                    for h in (0, 1):
                        step = i * 2 + h
                        ping = 2048 + (step % 2) * 1024
                        self.ps_wr("act", ping, 1024)
                        E.ws("act", [self.buf(f"sdt{sl}").w,
                                     self.buf("negA").w])
                        ta = E.i("act", lambda e, _d=dtv, _h=h, _db=db, _n=n,
                                 _pg=ping: e.activation(
                            out=ps[:, _pg:_pg + 1024],
                            in_=_d[:, _h * TH:(_h + 1) * TH], func=AF.Exp,
                            scale=self.negA.ap()[:, _db * 16 + _n:
                                                 _db * 16 + _n + 1]))
                        self.ps_wdone(ping, 1024, ta)
                        self.rdone([f"sdt{sl}"], ta)
                        ds = step % 3
                        dbx = self.s_dbx.ap()[:, ds * TH:(ds + 1) * TH]
                        self.wr("gps", [f"dbx{ds}"])
                        E.ws("gps", [self.buf(f"sdx{sl}").w,
                                     self.buf(f"brep{i}").w])
                        tb = E.i("gps", lambda e, _x=dxv, _o=dbx, _i=i, _h=h:
                                 e.tensor_mul(
                            _o, _x[:, _h * TH:(_h + 1) * TH],
                            self.s_brep.ap()[:, _i * T + _h * TH:
                                             _i * T + (_h + 1) * TH]))
                        self.wdone([f"dbx{ds}"], tb)
                        self.rdone([f"sdx{sl}"], tb)
                        self.rdone([f"brep{i}"], tb)
                        hs = step % 3
                        hv = self.s_h.ap()[:, hs * TH:(hs + 1) * TH]
                        if h == 0:
                            init = 0.0
                        else:
                            hp = (step - 1) % 3
                            init = self.s_h.ap()[:, hp * TH + TH - 1:
                                                 hp * TH + TH]
                        self.ps_rd("dve", ping, 1024)
                        self.wr("dve", [f"sh{hs}"])
                        E.ws("dve", [ta, tb])
                        tts = E.i("dve", lambda e, _o=hv, _pg=ping, _d=dbx,
                                  _in=init: e.tensor_tensor_scan(
                            out=_o, data0=ps[:, _pg:_pg + 1024], data1=_d,
                            initial=_in, op0=OP.mult, op1=OP.add))
                        if DEBUG_TAPS and b == 0 and db == 0:
                            tdh = E.dma("sy", lambda e, _n=n, _h=h, _o=hv:
                                        e.dma_start(
                                out=self.sc_h[_n, :, _h * TH:(_h + 1) * TH],
                                in_=_o), waits=[tts])
                            self.rdone([f"sh{hs}"], tdh)
                            tdb = E.dma("sy", lambda e, _n=n, _h=h, _d=dbx:
                                        e.dma_start(
                                out=self.sc_dbx[_n, :, _h * TH:(_h + 1) * TH],
                                in_=_d), waits=[tb])
                            self.rdone([f"dbx{ds}"], tdb)
                        self.ps_rdone(ping, 1024, tts)
                        self.rdone([f"dbx{ds}"], tts)
                        self.wdone([f"sh{hs}"], tts)
                        cs = step % 3
                        hcv = self.s_hc.ap()[:, cs * TH:(cs + 1) * TH]
                        self.wr("dve", [f"shc{cs}"])
                        E.ws("dve", [self.buf(f"crep{i}").w])
                        thc = E.i("dve", lambda e, _o=hcv, _h=hv, _i=i, _hh=h:
                                  e.tensor_mul(
                            _o, _h,
                            self.s_crep.ap()[:, _i * T + _hh * TH:
                                             _i * T + (_hh + 1) * TH]))
                        self.rdone([f"sh{hs}"], thc)
                        self.rdone([f"crep{i}"], thc)
                        self.wdone([f"shc{cs}"], thc)
                        E.ws("pe", [thc, self.buf("idnb").w])
                        tp = None
                        for t2 in range(2):
                            tc = h * 2 + t2
                            tp = E.i("pe", lambda e, _hc=hcv, _t2=t2, _tc=tc,
                                     _st=(q == 0 and i == 0),
                                     _sp=(i == 3): e.matmul(
                                ps[:, _tc * 512:(_tc + 1) * 512],
                                self.idn_b.ap(),
                                _hc[:, _t2 * 512:(_t2 + 1) * 512],
                                start=_st, stop=_sp))
                        self.rdone([f"shc{cs}"], tp)
                        self.ps_wdone(h * 2 * 512, 1024, tp)

                if q < 3:
                    self.ps_rd("act", 0, 2048)
                    self.wr("act", [f"ysb{db}"])
                    ty = E.i("act", lambda e, _db=db: e.activation(
                        out=self.s_ysb.ap()[:, _db * T:(_db + 1) * T],
                        in_=ps[:, 0:2048], func=AF.Identity))
                    self.ps_rdone(0, 2048, ty)
                    self.wdone([f"ysb{db}"], ty)
                else:
                    for h in (0, 1):
                        zs = h % 2
                        zv = self.s_z.ap()[:, zs * TH:(zs + 1) * TH]
                        tz = E.dma("act", lambda e, _b=b, _db=db, _h=h, _z=zv:
                                   e.dma_start(
                            out=_z,
                            in_=self.sc_z[_b, _db, :, _h * TH:(_h + 1) * TH]),
                            waits=[self.buf(f"sz{zs}").w] + self.buf(f"sz{zs}").r,
                            slot=f"sz{zs}")
                        self.buf(f"sz{zs}").w = tz
                        self.buf(f"sz{zs}").r = []
                        ys = h % 2
                        yzv = self.s_yz.ap()[:, ys * TH:(ys + 1) * TH]
                        self.ps_rd("dve", h * 1024, 1024)
                        self.wr("dve", [f"syz{ys}"])
                        E.ws("dve", [tz])
                        tyz = E.i("dve", lambda e, _o=yzv, _z=zv, _h=h:
                                  e.tensor_mul(
                            _o, ps[:, _h * 1024:(_h + 1) * 1024], _z))
                        self.ps_rdone(h * 1024, 1024, tyz)
                        self.rdone([f"sz{zs}"], tyz)
                        self.wdone([f"syz{ys}"], tyz)
                        tsp = E.dma("sy", lambda e, _o=yzv, _b=b, _db=db, _h=h:
                                    e.dma_start(
                            out=self.sc_yz[_b, _db, :, _h * TH:(_h + 1) * TH],
                            in_=_o), waits=[tyz], slot=f"syz{ys}")
                        self.rdone([f"syz{ys}"], tsp)

    def phase_c(self, b):
        E = self.E
        ps = self.ps.ap()

        t_w = E.dma("gps", lambda e: e.dma_start(
            out=self.c_wout.ap().rearrange("p (k e) -> p k e", k=8),
            in_=self.WOUT.rearrange("(k p) e -> p k e", p=128)),
            waits=[self.buf("cwout").w] + self.buf("cwout").r, slot="cwout")
        self.buf("cwout").w = t_w
        self.buf("cwout").r = []

        def load_ch(ch):
            sl = ch % 2
            tyz = E.dma("act", lambda e, _b=b, _ch=ch, _sl=sl: e.dma_start(
                out=self.c_yz.ap()[:, _sl * DB * 512:(_sl + 1) * DB * 512]
                .rearrange("p (k c) -> p k c", k=DB),
                in_=self.sc_yz[_b, :, :, _ch * CC:(_ch + 1) * CC]
                .rearrange("k p c -> p k c")),
                waits=[self.buf(f"cyz{sl}").w] + self.buf(f"cyz{sl}").r,
                slot=f"cyz{sl}")
            self.buf(f"cyz{sl}").w = tyz
            self.buf(f"cyz{sl}").r = []
            tuo = E.dma("sy", lambda e, _b=b, _ch=ch, _sl=sl: e.dma_start(
                out=self.c_uo.ap()[:, _sl * 4096:(_sl + 1) * 4096]
                .rearrange("p (j c) -> p j c", j=4),
                in_=self.U[_b, _ch * CC:(_ch + 1) * CC, :]
                .rearrange("(j p) c -> p j c", p=128)),
                waits=[self.buf(f"cuo{sl}").w] + self.buf(f"cuo{sl}").r,
                slot=f"cuo{sl}")
            self.buf(f"cuo{sl}").w = tuo
            self.buf(f"cuo{sl}").r = []
            top = E.dma("sy", lambda e, _b=b, _ch=ch, _sl=sl: e.dma_start(
                out=self.c_op.ap()[:, _sl * 4096:(_sl + 1) * 4096]
                .rearrange("p (j c) -> p j c", j=4),
                in_=self.sc_op[_b, _ch * CC:(_ch + 1) * CC, :]
                .rearrange("(j p) c -> p j c", p=128)),
                waits=[self.buf(f"cop{sl}").w] + self.buf(f"cop{sl}").r,
                slot=f"cop{sl}")
            self.buf(f"cop{sl}").w = top
            self.buf(f"cop{sl}").r = []

        load_ch(0)
        for ch in range(NCC):
            if ch + 1 < NCC:
                load_ch(ch + 1)
            sl = ch % 2
            yz = self.c_yz.ap()[:, sl * DB * 512:(sl + 1) * DB * 512]
            uo = self.c_uo.ap()[:, sl * 4096:(sl + 1) * 4096]
            op = self.c_op.ap()[:, sl * 4096:(sl + 1) * 4096]
            E.ws("dve", [self.buf(f"cuo{sl}").w, self.buf(f"cop{sl}").w])
            tadd = E.i("dve", lambda e, _u=uo, _o=op: e.tensor_add(_u, _u, _o))
            self.rdone([f"cop{sl}"], tadd)
            self.buf(f"cuo{sl}").w = tadd
            E.wf("dve", tadd)

            for e_ in range(DB):
                bank = (e_ % 2) * 512
                self.ps_wr("pe", bank, 512)
                E.ws("pe", [self.buf(f"cyz{sl}").w, self.buf("cwout").w])
                tm = None
                for k in range(8):
                    tm = E.i("pe", lambda e, _k=k, _e=e_, _y=yz, _bk=bank:
                             e.matmul(
                        ps[:, _bk:_bk + 512],
                        self.c_wout.ap()[:, _k * 1024 + _e * 128:
                                         _k * 1024 + _e * 128 + 128],
                        _y[:, _k * 512:(_k + 1) * 512],
                        start=(_k == 0), stop=(_k == 7)))
                self.rdone([f"cyz{sl}"], tm)
                self.ps_wdone(bank, 512, tm)
                asl = e_ % 2
                att = self.c_attn.ap()[:, asl * 512:(asl + 1) * 512]
                self.ps_rd("act", bank, 512)
                self.wr("act", [f"cat{asl}"])
                E.ws("act", [self.buf("bout").w])
                te = E.i("act", lambda e, _a=att, _bk=bank, _e=e_: e.activation(
                    out=_a, in_=ps[:, _bk:_bk + 512], func=AF.Identity,
                    bias=self.bout_c.ap()[:, _e:_e + 1]))
                self.ps_rdone(bank, 512, te)
                self.wdone([f"cat{asl}"], te)
                pbank = 1024 + (e_ % 2) * 512
                self.ps_wr("pe", pbank, 512)
                E.ws("pe", [te, self.buf("idn").w])
                tp = None
                for j4 in range(4):
                    tp = E.i("pe", lambda e, _a=att, _j=j4, _pb=pbank:
                             e.transpose(
                        ps[:, _pb + _j * 128:_pb + (_j + 1) * 128],
                        _a[:, _j * 128:(_j + 1) * 128], self.idn_f.ap()))
                self.rdone([f"cat{asl}"], tp)
                self.ps_wdone(pbank, 512, tp)
                self.ps_rd("dve", pbank, 512)
                E.ws("dve", [self.buf(f"cuo{sl}").w])
                tv = E.i("dve", lambda e, _u=uo, _e=e_, _pb=pbank: e.tensor_add(
                    _u.rearrange("p (j c) -> p j c", j=4)[:, :,
                                                          _e * 128:(_e + 1) * 128],
                    _u.rearrange("p (j c) -> p j c", j=4)[:, :,
                                                          _e * 128:(_e + 1) * 128],
                    ps[:, _pb:_pb + 512].rearrange("p (j c) -> p j c", c=128)))
                self.ps_rdone(pbank, 512, tv)
                self.buf(f"cuo{sl}").w = tv
            tso = E.dma("sy", lambda e, _u=uo, _b=b, _ch=ch: e.dma_start(
                out=self.OUT[_b, _ch * CC:(_ch + 1) * CC, :]
                .rearrange("(j p) c -> p j c", p=128),
                in_=_u.rearrange("p (j c) -> p j c", j=4)),
                waits=[self.buf(f"cuo{sl}").w], slot=f"cuo{sl}")
            self.rdone([f"cuo{sl}"], tso)


def _ctx():
    import contextlib
    return contextlib.nullcontext()


_NC_CACHE = {}


def build_program():
    if "nc" not in _NC_CACHE:
        _NC_CACHE["nc"] = K().build()
    return _NC_CACHE["nc"]


def kernel(**inputs):
    u = np.ascontiguousarray(np.asarray(inputs["L0_attn_u"]))
    wrt = np.ascontiguousarray(np.asarray(inputs["L0_ffn_w_router"]).T)
    idn = np.eye(128, dtype=np.float32)
    ones = np.zeros((32, 128), dtype=np.float32)
    ones[0, :] = 1.0
    psel = np.zeros((32, 16 * 64), dtype=np.float32)
    for n in range(16):
        psel[n, n * 64:(n + 1) * 64] = 1.0
    osel = np.zeros((32, 16 * 128), dtype=np.float32)
    for n in range(16):
        osel[n, n * 128:(n + 1) * 128] = 1.0

    common = {
        "WIN": np.ascontiguousarray(inputs["L0_attn_Win"]),
        "BIN": np.ascontiguousarray(inputs["L0_attn_b_in"]),
        "WG": np.ascontiguousarray(inputs["L0_attn_Wg"]),
        "BG": np.ascontiguousarray(inputs["L0_attn_bg"]),
        "WOUT": np.ascontiguousarray(inputs["L0_attn_Wout"]),
        "BOUT": np.ascontiguousarray(inputs["L0_attn_bout"]),
        "CONVW": np.ascontiguousarray(inputs["L0_attn_convw"]),
        "CONVB": np.ascontiguousarray(inputs["L0_attn_convb"]),
        "AMAT": np.ascontiguousarray(inputs["L0_attn_A"]),
        "WB": np.ascontiguousarray(inputs["L0_attn_Wb"]),
        "BB": np.ascontiguousarray(inputs["L0_attn_bb"]),
        "WC": np.ascontiguousarray(inputs["L0_attn_Wc"]),
        "BCC": np.ascontiguousarray(inputs["L0_attn_bc"]),
        "WDT": np.ascontiguousarray(inputs["L0_attn_Wdt"]),
        "BDT": np.ascontiguousarray(inputs["L0_attn_bdt"]),
        "WRT": wrt,
        "CEN": np.ascontiguousarray(inputs["L0_ffn_head_centroids"]),
        "TEMP": np.ascontiguousarray(np.repeat(np.asarray(inputs["L0_ffn_temperature"]).reshape(1, 1), 128, axis=0)),
        "IEXP": np.ascontiguousarray(inputs["L0_ffn_input_experts"]),
        "OEXP": np.ascontiguousarray(inputs["L0_ffn_output_experts"]),
        "IDN": idn,
        "ONESR": ones,
        "PSEL": psel,
        "OSEL": osel,
    }
    in_maps = []
    for c in range(NCORE):
        m = dict(common)
        m["U"] = np.ascontiguousarray(u[2 * c:2 * c + 2])
        in_maps.append(m)

    nc = build_program()
    res = run_bass_kernel_spmd(nc, in_maps, core_ids=list(range(NCORE)))
    out = np.concatenate([r["OUT"] for r in res.results], axis=0)
    return out.astype(np.float32)


if __name__ == "__main__":
    nc = build_program()
    print("built ok")


# version 15 (final)
# speedup vs baseline: 1.1581x; 1.0180x over previous
import numpy as np

import concourse.bass as bass
import concourse.mybir as mybir
from concourse.bass_utils import run_bass_kernel_spmd

F32 = mybir.dt.float32
F32R = mybir.dt.float32r
BF16 = mybir.dt.bfloat16
AF = mybir.ActivationFunctionType
OP = mybir.AluOpType
AX = mybir.AxisListType

D = 1024
DB = 8
NST = 16
T = 2048
BC = 2
NCORE = 8
G = 128
NH = 16
HS = 512
HD = 64
JT = T // 128
NJG = T // 512
CA = 512
NCA = T // CA
CB = 512
NCB = T // CB
TH = T // 2
CC = 512
NCC = T // CC
EPS = 1e-5
DEBUG_TAPS = False
PHASE_LIMIT = 99

ENGS = ("pe", "act", "dve", "gps", "sy")
PROG = {"pe": "pe", "act": "act", "dve": "dve", "gps": "gps", "sy": "spr"}
SEMS = ["pe", "act", "dve", "gps", "spr", "qsy", "qsc", "qgp"]
ENG_OF_SEM = {"pe": "pe", "act": "act", "dve": "dve", "gps": "gps",
              "spr": "sy", "qsy": "sy", "qsc": "act", "qgp": "gps"}


class Bld:

    def __init__(self):
        self.ops = {k: [] for k in ENGS}
        self.cnt = {s: 0 for s in SEMS}
        self.sems = {}
        self.slotsems = {}
        self.sem_owner = {"qsy": "sy", "qsc": "act", "qgp": "gps"}

    def slotsem(self, slot, eng):
        if slot not in self.slotsems:
            s = f"sl{len(self.slotsems)}"
            self.slotsems[slot] = s
            self.cnt[s] = 0
            self.sem_owner[s] = eng
        return self.slotsems[slot]

    def w(self, eng, tok):
        if tok is None:
            return
        s, v = tok
        if v <= 0:
            return
        if ENG_OF_SEM.get(s) == eng and s in ("pe", "act", "dve", "gps"):
            return
        self.ops[eng].append(lambda e, _s=s, _v=v: e.wait_ge(self.sems[_s], _v))

    def ws(self, eng, toks):
        best = {}
        for t in toks:
            if t is None:
                continue
            s, v = t
            if v > best.get(s, -1):
                best[s] = v
        for s, v in best.items():
            self.w(eng, (s, v))

    def wf(self, eng, tok):
        if tok is None:
            return
        s, v = tok
        if v <= 0:
            return
        self.ops[eng].append(lambda e, _s=s, _v=v: e.wait_ge(self.sems[_s], _v))

    def i(self, eng, fn, sem=None, k=1):
        if sem is None:
            sem = PROG[eng]
        self.cnt[sem] += k
        v = self.cnt[sem]
        self.ops[eng].append(
            lambda e, _f=fn, _s=sem, _k=k: _f(e).then_inc(self.sems[_s], _k))
        return (sem, v)

    def dma(self, eng, fn, waits=(), slot=None):
        q = self.slotsem(slot, eng) if slot else             {"sy": "qsy", "act": "qsc", "gps": "qgp"}[eng]
        best = {}
        for t in waits:
            if t is None:
                continue
            s, v = t
            if v > best.get(s, -1):
                best[s] = v
        for s, v in best.items():
            self.wf(eng, (s, v))
        return self.i(eng, fn, q, 16)

    def nop_inc(self, eng):
        self.i(eng, lambda e: e.nop())

    def barrier(self):
        for eng in ENGS:
            for s, owner in self.sem_owner.items():
                if owner == eng and self.cnt.get(s, 0) > 0:
                    self.wf(eng, (s, self.cnt[s]))
            self.nop_inc(eng)
        snap = {s: self.cnt[s] for s in ("pe", "act", "dve", "gps", "spr")}
        for eng in ENGS:
            for s, v in snap.items():
                if v <= 0 or ENG_OF_SEM[s] == eng:
                    continue
                self.ops[eng].append(
                    lambda e, _s=s, _v=v: e.wait_ge(self.sems[_s], _v))


class Buf:
    __slots__ = ("w", "r")

    def __init__(self):
        self.w = None
        self.r = []


class K:
    def __init__(self):
        self.nc = bass.Bass("TRN2", target_bir_lowering=False, debug=False)
        self.E = Bld()
        self.bufs = {}
        self.psum_b = [Buf() for _ in range(8)]

    def buf(self, name):
        if name not in self.bufs:
            self.bufs[name] = Buf()
        return self.bufs[name]

    def rd(self, eng, names):
        toks = []
        for nm in names:
            toks.append(self.buf(nm).w)
        self.E.ws(eng, toks)

    def rdone(self, names, tok):
        for nm in names:
            self.buf(nm).r.append(tok)

    def wr(self, eng, names):
        toks = []
        for nm in names:
            b = self.buf(nm)
            toks.append(b.w)
            toks.extend(b.r)
        self.E.ws(eng, toks)

    def wdone(self, names, tok):
        for nm in names:
            b = self.buf(nm)
            b.w = tok
            b.r = []

    def banks_of(self, off, length):
        b0 = off // 512
        b1 = (off + length - 1) // 512
        return list(range(b0, b1 + 1))

    def ps_wr(self, eng, off, length):
        toks = []
        for b in self.banks_of(off, length):
            toks.append(self.psum_b[b].w)
            toks.extend(self.psum_b[b].r)
        self.E.ws(eng, toks)

    def ps_wdone(self, off, length, tok):
        for b in self.banks_of(off, length):
            self.psum_b[b].w = tok
            self.psum_b[b].r = []

    def ps_rd(self, eng, off, length):
        toks = [self.psum_b[b].w for b in self.banks_of(off, length)]
        self.E.ws(eng, toks)

    def ps_rdone(self, off, length, tok):
        for b in self.banks_of(off, length):
            self.psum_b[b].r.append(tok)

    def build(self):
        nc = self.nc
        E = self.E

        def din(name, shape, dt=F32):
            return nc.dram_tensor(name, shape, dt, kind="ExternalInput").ap()

        self.U = din("U", [BC, T, D])
        self.WIN = din("WIN", [D, D]); self.BIN = din("BIN", [D])
        self.WG = din("WG", [D, D]); self.BG = din("BG", [D])
        self.WOUT = din("WOUT", [D, D]); self.BOUT = din("BOUT", [D])
        self.CONVW = din("CONVW", [D, 3]); self.CONVB = din("CONVB", [D])
        self.AMAT = din("AMAT", [D, NST])
        self.WB = din("WB", [D, NST]); self.BB = din("BB", [NST])
        self.WC = din("WC", [D, NST]); self.BCC = din("BCC", [NST])
        self.WDT = din("WDT", [D, D]); self.BDT = din("BDT", [D])
        self.WRT = din("WRT", [D, G])
        self.CEN = din("CEN", [NH, G])
        self.TEMP = din("TEMP", [128, 1])
        self.IEXP = din("IEXP", [NH, D, HS])
        self.OEXP = din("OEXP", [NH, HS, HD])
        self.IDN = din("IDN", [128, 128])
        self.ONESR = din("ONESR", [32, 128])
        self.PSEL = din("PSEL", [32, 16 * 64])
        self.OSEL = din("OSEL", [32, 16 * 128])

        self.OUT = nc.dram_tensor("OUT", [BC, T, D], F32, kind="ExternalOutput").ap()

        def scr(name, shape, dt=F32):
            kind = "ExternalOutput" if DEBUG_TAPS else "Internal"
            return nc.dram_tensor(name, shape, dt, kind=kind).ap()

        self.sc_xn = scr("sc_xn", [BC, DB, 128, T], F32R)
        self.sc_x = scr("sc_x", [BC, DB, 128, T], F32R)
        self.sc_dt = scr("sc_dt", [BC, DB, 128, T], F32)
        self.sc_dtx = scr("sc_dtx", [BC, DB, 128, T], F32)
        self.sc_z = scr("sc_z", [BC, DB, 128, T], F32)
        self.sc_yz = scr("sc_yz", [BC, DB, 128, T], F32R)
        self.sc_op = scr("sc_op", [BC, T, D], F32)
        self.sc_st = scr("sc_st", [BC, 4, 128, 32], F32)
        self.sc_wt = scr("sc_wt", [BC, 32, T], BF16)
        self.sc_xnb = scr("sc_xnb", [BC, 128, DB * T], BF16)
        self.sc_h = scr("sc_h", [16, 128, T], BF16)
        self.sc_brep = scr("sc_brep", [16, 128, T], BF16)
        self.sc_crep = scr("sc_crep", [16, 128, T], BF16)
        self.sc_dbx = scr("sc_dbx", [16, 128, T], F32)
        self.sc_bm = scr("sc_bm", [BC, 32, T], BF16)
        self.sc_hid = scr("sc_hid", [4, 128, T], BF16)
        self.sc_fp = scr("sc_fp", [128, T], F32)
        self.sc_wrp = scr("sc_wrp", [128, T], F32)
        self.sc_cm = scr("sc_cm", [BC, 32, T], BF16)

        off = [16640]

        def sb(name, shape, dt, at=None):
            a = off[0] if at is None else at
            a = (a + 31) // 32 * 32
            h = nc.alloc_sbuf_tensor_at(name, shape, dt, offset=a)
            if at is None:
                off[0] = a + int(np.prod(shape[1:])) * mybir.dt.size(dt)
            return h

        self.idn_f = sb("idn_f", [128, 128], F32)
        self.idn_r = sb("idn_r", [128, 128], F32R)
        self.idn_b = sb("idn_b", [128, 128], BF16)
        self.ones1r = sb("ones1r", [32, 128], F32R)
        self.onesg = sb("onesg", [128, 1], F32R)
        self.psel_b = sb("psel_b", [32, 16 * 64], BF16)
        self.osel_b = sb("osel_b", [32, 16 * 128], BF16)
        self.negA = sb("negA", [128, 128], F32)
        self.cwcol = sb("cwcol", [128, 24], F32)
        self.bin_c = sb("bin_c", [128, 8], F32)
        self.convb_c = sb("convb_c", [128, 8], F32)
        self.bg_c = sb("bg_c", [128, 8], F32)
        self.nbdt_c = sb("nbdt_c", [128, 8], F32)
        self.bout_c = sb("bout_c", [128, 8], F32)
        self.nbb_c = sb("nbb_c", [16, 1], F32)
        self.bc_c = sb("bc_c", [16, 1], F32)
        self.scol = sb("scol", [128, 1], F32)
        self.colpad = sb("colpad", [128, 48], F32)
        self.WrT = sb("WrT", [128, DB * 128], F32R)
        self.cnT = sb("cnT", [128, 16], F32R)
        self.cen_s = sb("cen_s", [32, G], F32)
        self.zeros32 = sb("zeros32", [32, 512], BF16)
        self.cnr = sb("cnr", [32, G], F32R)
        self.wTb = sb("wTb", [32, T], BF16)
        self.Bm = sb("Bm", [32, T], BF16)
        self.Cm = sb("Cm", [32, T], BF16)
        self.woutE = sb("woutE", [128, NH * 4 * HD], BF16)
        self.tmp1 = sb("tmp1", [1, 4], F32R)
        W_END = off[0]
        assert W_END < 60000, W_END

        OV = (W_END + 255) // 256 * 256

        o = [OV]

        def ov(name, shape, dt):
            a = (o[0] + 31) // 32 * 32
            h = nc.alloc_sbuf_tensor_at(name, shape, dt, offset=a)
            o[0] = a + int(np.prod(shape[1:])) * mybir.dt.size(dt)
            return h

        def ov_reset():
            o[0] = OV

        self.a1_u = ov("a1_u", [128, 4 * 1024], F32)
        self.a1_scr = ov("a1_scr", [128, 1024], F32)
        self.a1_xtm = ov("a1_xtm", [128, 4 * 1024], F32R)
        self.a1_xfm = ov("a1_xfm", [128, 2 * 512], F32R)
        self.a1_xnb = ov("a1_xnb", [128, DB * T], BF16)
        self.a1_cols = ov("a1_cols", [128, 4 * 8], F32)
        self.a1_zr = ov("a1_zr", [128, 512], F32R)
        self.a1_zsq = ov("a1_zsq", [128, 512], F32R)
        self.a1_znorm = ov("a1_znorm", [128, 512], F32R)
        self.a1_nrm = ov("a1_nrm", [1, 512], F32)
        self.a1_inv = ov("a1_inv", [32, 512], F32R)
        self.a1_lg = ov("a1_lg", [32, 512], F32R)
        self.a1_lgt = ov("a1_lgt", [128, 128], F32)
        self.a1_wtm = ov("a1_wtm", [128, 64], F32)
        self.a1_smc = ov("a1_smc", [128, 16], F32)
        self.a1_msk = ov("a1_msk", [128, 16], F32)
        self.a1_dbg = ov("a1_dbg", [128, JT * 8], F32)
        self.a1_e2 = ov("a1_e2", [128, 16], F32)
        self.a1_wi = ov("a1_wi", [128, 2 * 8 * 512], BF16)
        self.a1_hid = ov("a1_hid", [128, 4 * 512], BF16)
        self.a1_wrep = ov("a1_wrep", [64, 2048], F32)
        self.a1_fe = ov("a1_fe", [64, 2048], F32)
        self.a1_fo = ov("a1_fo", [64, 2048], F32)
        self.a1_ops = ov("a1_ops", [128, 2 * 512], F32)
        A1_END = o[0]

        ov_reset()
        self.a2_xn = ov("a2_xn", [128, 2 * DB * CA], F32R)
        self.a2_win = ov("a2_win", [128, 8 * 1024], F32R)
        self.a2_diag = ov("a2_diag", [128, 24 * 128], F32R)
        self.a2_xpre = ov("a2_xpre", [128, DB * (CA + 2)], F32R)
        self.a2_cpre = ov("a2_cpre", [128, 2 * DB * CA], F32)
        self.a2_wg = ov("a2_wg", [128, 8 * 1024], F32R)
        self.a2_z = ov("a2_z", [128, DB * CA], F32)
        A2A_END = o[0]

        ov_reset()
        self.b2_x = ov("b2_x", [128, 2 * DB * CB], F32R)
        self.b2_sg = ov("b2_sg", [128, 2 * DB * CB], F32)
        self.b2_dtx = ov("b2_dtx", [128, 2 * DB * CB], F32)
        self.b2_z = ov("b2_z", [128, 2 * DB * CB], F32)
        self.b2_wdt = ov("b2_wdt", [128, 8 * 1024], F32R)
        self.b2_wb = ov("b2_wb", [128, 8 * 16], F32R)
        self.b2_wc = ov("b2_wc", [128, 8 * 16], F32R)
        A2B_END = o[0]

        ov_reset()
        self.s_brep = ov("s_brep", [128, 4 * T], BF16)
        self.s_crep = ov("s_crep", [128, 4 * T], BF16)
        self.s_ysb = ov("s_ysb", [128, DB * T], F32R)
        self.s_dt = ov("s_dt", [128, 2 * T], F32)
        self.s_dtx = ov("s_dtx", [128, 2 * T], F32)
        self.s_dbx = ov("s_dbx", [128, 3 * TH], F32)
        self.s_h = ov("s_h", [128, 3 * TH], BF16)
        self.s_hc = ov("s_hc", [128, 3 * TH], BF16)
        self.s_z = ov("s_z", [128, 2 * TH], F32)
        self.s_yz = ov("s_yz", [128, 2 * TH], F32R)
        B_END = o[0]

        ov_reset()
        self.c_yz = ov("c_yz", [128, 2 * DB * 512], F32R)
        self.c_wout = ov("c_wout", [128, 8 * 1024], F32R)
        self.c_attn = ov("c_attn", [128, 2 * 512], F32)
        self.c_uo = ov("c_uo", [128, 2 * 4 * 1024], F32)
        self.c_op = ov("c_op", [128, 2 * 4 * 1024], F32)
        C_END = o[0]

        assert max(A1_END, A2A_END, A2B_END, B_END, C_END) <= 229312, (
            A1_END, A2A_END, A2B_END, B_END, C_END)

        self.ps = nc.alloc_psum_tensor("ps", [128, 4096], F32)

        self.preamble()
        for b in range(BC):
            self.phase_a1(b)
            self.E.barrier()
            if PHASE_LIMIT >= 2:
                self.phase_a2a(b)
                self.E.barrier()
            if PHASE_LIMIT >= 3:
                self.phase_a2b(b)
                self.phase_a2b_taps(b)
                self.E.barrier()
            if PHASE_LIMIT >= 4:
                self.phase_b(b)
                self.E.barrier()
            if PHASE_LIMIT >= 5:
                self.phase_c(b)
                self.E.barrier()

        self.E.barrier()

        import contextlib
        stack = contextlib.ExitStack()
        for s in list(SEMS) + list(self.E.slotsems.values()):
            self.E.sems[s] = stack.enter_context(nc.semaphore(s))
        with stack:
            with nc.Block() as block:
                @block.tensor
                def _(e):
                    for f in E.ops["pe"]:
                        f(e)

                @block.scalar
                def _(e):
                    for f in E.ops["act"]:
                        f(e)

                @block.vector
                def _(e):
                    for f in E.ops["dve"]:
                        f(e)

                @block.gpsimd
                def _(e):
                    for f in E.ops["gps"]:
                        f(e)

                @block.sync
                def _(e):
                    for f in E.ops["sy"]:
                        f(e)
        return nc

    def preamble(self):
        E, nc = self.E, self.nc
        ps = self.ps.ap()

        t_idn = E.dma("sy", lambda e: e.dma_start(out=self.idn_f.ap(), in_=self.IDN), slot="c_idn")
        t_idnr = E.dma("gps", lambda e: e.dma_start(out=self.idn_r.ap(), in_=self.IDN), slot="c_idnr")
        t_idnb = E.dma("gps", lambda e: e.dma_start(out=self.idn_b.ap(), in_=self.IDN), slot="c_idnb")
        t_o1r = E.dma("gps", lambda e: e.dma_start(out=self.ones1r.ap(), in_=self.ONESR), slot="c_o1r")
        t_og = E.dma("gps", lambda e: e.dma_start(
            out=self.onesg.ap(), in_=self.ONESR[0:1, :].rearrange("o p -> p o"),
            allow_slow_non_contiguous=True), slot="c_og")
        tz0 = None
        for tns in (self.wTb, self.Bm, self.Cm):
            tz0 = E.i("dve", lambda e, _t=tns: e.memset(
                _t.ap().bitcast(F32), 0.0))
        self.buf("padzero").w = tz0
        tz1 = E.i("dve", lambda e: e.memset(self.zeros32.ap(), 0.0))
        self.buf("zeros32").w = tz1
        tz2 = E.i("dve", lambda e: e.memset(self.colpad.ap()[:, 25:26], 1e-24))
        self.buf("eps24").w = tz2
        t_ps = E.dma("gps", lambda e: e.dma_start(out=self.psel_b.ap(),
                                                  in_=self.PSEL), slot="c_ps")
        self.buf("pairs").w = t_ps
        t_os = E.dma("gps", lambda e: e.dma_start(out=self.osel_b.ap(),
                                                  in_=self.OSEL), slot="c_os")
        self.buf("osel").w = t_os

        t_a = E.dma("sy", lambda e: e.dma_start(
            out=self.a1_scr.ap()[:, 0:128].rearrange("p (db n) -> p db n", db=8),
            in_=self.AMAT.rearrange("(db p) n -> p db n", p=128)), slot="c_a")
        E.ws("dve", [t_a])
        t_na = E.i("dve", lambda e: e.tensor_scalar_mul(
            self.negA.ap(), self.a1_scr.ap()[:, 0:128], -1.0))
        self.buf("negA").w = t_na

        def loadcol(dst, src, neg=False, n16=False, nm=""):
            if n16:
                tok = E.dma("sy", lambda e, _d=dst, _s=src: e.dma_start(
                    out=_d.ap(), in_=_s.rearrange("(n o) -> n o", o=1)),
                    slot="c_" + nm)
            else:
                tok = E.dma("sy", lambda e, _d=dst, _s=src: e.dma_start(
                    out=_d.ap(), in_=_s.rearrange("(db p) -> p db", p=128),
                    allow_slow_non_contiguous=True), slot="c_" + nm)
            if neg:
                E.ws("dve", [tok])
                tok = E.i("dve", lambda e, _d=dst: e.tensor_scalar_mul(
                    _d.ap(), _d.ap(), -1.0))
            return tok

        self.buf("bin").w = loadcol(self.bin_c, self.BIN, nm="bin")
        self.buf("convb").w = loadcol(self.convb_c, self.CONVB, nm="cvb")
        self.buf("bg").w = loadcol(self.bg_c, self.BG, nm="bg")
        self.buf("nbdt").w = loadcol(self.nbdt_c, self.BDT, neg=True, nm="bdt")
        self.buf("bout").w = loadcol(self.bout_c, self.BOUT, nm="bo")
        self.buf("nbb").w = loadcol(self.nbb_c, self.BB, neg=True, n16=True, nm="bb")
        self.buf("bc").w = loadcol(self.bc_c, self.BCC, n16=True, nm="bc")

        t_cw = E.dma("sy", lambda e: e.dma_start(
            out=self.cwcol.ap()[:, 0:24].rearrange("p (db k) -> p db k", db=8),
            in_=self.CONVW.rearrange("(db p) k -> p db k", p=128)), slot="c_cw")
        self.buf("cw").w = t_cw

        t_wrt = E.dma("gps", lambda e: e.dma_start(
            out=self.WrT.ap().rearrange("p (db g) -> p db g", db=8),
            in_=self.WRT.rearrange("(db p) g -> p db g", p=128)), slot="c_wrt")
        self.buf("WrT").w = t_wrt

        t_we = E.dma("gps", lambda e: e.dma_start(
            out=self.woutE.ap().rearrange("p (n hb e) -> p n hb e", n=16, hb=4),
            in_=self.OEXP.rearrange("n (hb p) e -> p n hb e", p=128)), slot="c_we")
        self.buf("woutE").w = t_we

        E.ws("dve", [self.buf("zeros32").w])
        tcz = E.i("dve", lambda e: e.memset(self.cen_s.ap(), 0.0))
        t_ce = E.dma("sy", lambda e: e.dma_start(
            out=self.cen_s.ap()[0:16, :], in_=self.CEN), waits=[tcz],
            slot="c_cen")
        E.ws("act", [t_ce, t_na])
        t1 = E.i("act", lambda e: e.activation(
            out=self.a1_scr.ap()[0:32, 0:128], in_=self.cen_s.ap(),
            func=AF.Square))
        E.ws("dve", [t1])
        t2 = E.i("dve", lambda e: e.reduce_sum(
            self.colpad.ap()[0:32, 0:1], self.a1_scr.ap()[0:32, 0:128], AX.X))
        E.ws("act", [t2, self.buf("eps24").w])
        t3 = E.i("act", lambda e: e.activation(
            out=self.colpad.ap()[0:32, 1:2], in_=self.colpad.ap()[0:32, 0:1],
            func=AF.Ln, bias=self.colpad.ap()[0:32, 25:26]))
        t4 = E.i("act", lambda e: e.activation(
            out=self.colpad.ap()[0:32, 2:3], in_=self.colpad.ap()[0:32, 1:2],
            func=AF.Exp, scale=-0.5))
        E.ws("dve", [t4])
        E.wf("dve", t2)
        t5 = E.i("dve", lambda e: e.tensor_scalar(
            out=self.cnr.ap(),
            in0=self.cen_s.ap(), scalar1=self.colpad.ap()[0:32, 2:3],
            scalar2=None, op0=OP.mult))
        E.ws("pe", [t5, t_idnr])
        self.ps_wr("pe", 0, 32)
        tt = E.i("pe", lambda e: e.transpose(
            ps[:, 0:32].bitcast(F32R),
            self.cnr.ap(),
            self.idn_r.ap()[0:32, 0:32]))
        self.ps_wdone(0, 32, tt)
        self.ps_rd("act", 0, 32)
        tc = E.i("act", lambda e: e.activation(
            out=self.cnT.ap(), in_=ps[:, 0:16], func=AF.Identity))
        self.ps_rdone(0, 32, tc)
        self.buf("cnT").w = tc

        t_t = E.dma("sy", lambda e: e.dma_start(
            out=self.colpad.ap()[:, 8:9], in_=self.TEMP), slot="c_temp")
        E.ws("act", [t_t])
        t6 = E.i("act", lambda e: e.activation(
            out=self.colpad.ap()[:, 9:10], in_=self.colpad.ap()[:, 8:9],
            func=AF.Exp))
        E.ws("dve", [t6])
        t9 = E.i("dve", lambda e: e.tensor_scalar(
            out=self.scol.ap(), in0=self.colpad.ap()[:, 9:10],
            scalar1=float(1.0 / np.sqrt(G)), scalar2=None, op0=OP.mult))
        self.buf("scol").w = t9
        t_ep = E.i("dve", lambda e: e.memset(self.colpad.ap()[:, 24:25], EPS))
        self.buf("epscol").w = t_ep
        self.buf("idn").w = t_idn
        self.buf("idnr").w = t_idnr
        self.buf("idnb").w = t_idnb
        self.buf("onesg").w = t_og
        self.E.barrier()

    def phase_a1(self, b):
        E, nc = self.E, self.nc
        ps = self.ps.ap()
        ucols = self.a1_cols.ap()

        self.wr("dve", ["a1lg", "a1inv"])
        tzp = E.i("dve", lambda e: e.tensor_scalar(
            out=self.a1_lg.ap(), in0=self.zeros32.ap(), scalar1=1.0,
            scalar2=None, op0=OP.mult))
        tzp = E.i("dve", lambda e: e.tensor_scalar(
            out=self.a1_inv.ap(), in0=self.zeros32.ap(), scalar1=1.0,
            scalar2=None, op0=OP.mult))
        self.wdone(["a1lg", "a1inv"], tzp)

        for jg in range(NJG):
            t_sq = t_s = None
            u_toks = []
            for j4 in range(4):
                jt = jg * 4 + j4
                us = self.a1_u.ap()[:, (jt % 4) * 1024:(jt % 4) * 1024 + 1024]
                nm_u = f"a1u{jt % 4}"
                t_u = E.dma("sy", lambda e, _o=us, _b=b, _jt=jt: e.dma_start(
                    out=_o, in_=self.U[_b, _jt * 128:(_jt + 1) * 128, :]),
                    waits=[self.buf(nm_u).w] + self.buf(nm_u).r, slot=nm_u)
                self.buf(nm_u).w = t_u
                self.buf(nm_u).r = []
                u_toks.append((us, nm_u))
                E.ws("act", [t_u])
                self.wr("act", [f"a1scr"])
                t_sq = E.i("act", lambda e, _u=us, _j=j4: e.activation(
                    out=self.a1_scr.ap(), in_=_u, func=AF.Square))
                self.wdone(["a1scr"], t_sq)
                E.ws("dve", [t_u])
                t_s = E.i("dve", lambda e, _u=us, _j=j4: e.reduce_sum(
                    ucols[:, 4 + _j:5 + _j], _u, AX.X))
                self.rdone([nm_u], t_s)
                E.ws("dve", [t_sq])
                t_sq = E.i("dve", lambda e, _j=j4: e.reduce_sum(
                    ucols[:, 0 + _j:1 + _j], self.a1_scr.ap(), AX.X))
                self.rdone(["a1scr"], t_sq)
            E.wf("dve", t_s)
            t_m = E.i("dve", lambda e: e.tensor_scalar_mul(
                ucols[:, 8:12], ucols[:, 4:8], 1.0 / D))
            E.ws("dve", [t_sq])
            E.wf("dve", t_sq)
            t_m2 = E.i("dve", lambda e: e.tensor_scalar_mul(
                ucols[:, 12:16], ucols[:, 0:4], 1.0 / D))
            E.wf("dve", t_m)
            t_ms = E.i("dve", lambda e: e.tensor_mul(
                ucols[:, 16:20], ucols[:, 8:12], ucols[:, 8:12]))
            E.wf("dve", t_ms)
            t_v = E.i("dve", lambda e: e.tensor_sub(
                ucols[:, 20:24], ucols[:, 12:16], ucols[:, 16:20]))
            E.ws("act", [t_v])
            t_l = E.i("act", lambda e: e.activation(
                out=ucols[:, 24:28], in_=ucols[:, 20:24], func=AF.Ln,
                bias=self.colpad.ap()[:, 24:25]))
            t_is = E.i("act", lambda e: e.activation(
                out=ucols[:, 24:28], in_=ucols[:, 24:28], func=AF.Exp,
                scale=-0.5))
            E.ws("dve", [t_is])
            t_nm = E.i("dve", lambda e: e.tensor_mul(
                ucols[:, 28:32], ucols[:, 8:12], ucols[:, 24:28]))
            E.wf("dve", t_nm)
            t_nm2 = E.i("dve", lambda e: e.tensor_scalar_mul(
                ucols[:, 28:32], ucols[:, 28:32], -1.0))
            for j4 in range(4):
                us, nm_u = u_toks[j4]
                xs = self.a1_xtm.ap()[:, j4 * 1024:(j4 + 1) * 1024]
                E.ws("act", [t_nm2])
                self.wr("act", [f"a1xtm{j4}"])
                t_xn = E.i("act", lambda e, _u=us, _j=j4, _x=xs: e.activation(
                    out=_x, in_=_u, func=AF.Identity,
                    scale=ucols[:, 24 + _j:25 + _j],
                    bias=ucols[:, 28 + _j:29 + _j]))
                self.wdone([f"a1xtm{j4}"], t_xn)
                self.rdone([nm_u], t_xn)
            if DEBUG_TAPS:
                E.ws("dve", [t_is, t_sq])
                E.wf("dve", t_nm2)
                tdbg = E.i("dve", lambda e, _jg=jg: e.tensor_copy(
                    out=self.a1_dbg.ap()[:, _jg * 32:(_jg + 1) * 32],
                    in_=ucols[:, 0:32]))
                E.dma("sy", lambda e, _jg=jg, _b=b: e.dma_start(
                    out=self.sc_st[_b, _jg],
                    in_=self.a1_dbg.ap()[:, _jg * 32:(_jg + 1) * 32]),
                    waits=[tdbg])

            for db in range(DB):
                bank = (db % 2) * 512
                self.ps_wr("pe", bank, 512)
                E.ws("pe", [self.buf("idnr").w])
                tts = []
                for j4 in range(4):
                    self.rd("pe", [f"a1xtm{j4}"])
                    tt = E.i("pe", lambda e, _j=j4, _db=db, _bk=bank: e.transpose(
                        ps[:, _bk + _j * 128:_bk + (_j + 1) * 128].bitcast(F32R),
                        self.a1_xtm.ap()[:, _j * 1024 + _db * 128:
                                         _j * 1024 + _db * 128 + 128],
                        self.idn_r.ap()))
                    tts.append(tt)
                    self.rdone([f"a1xtm{j4}"], tt)
                self.ps_wdone(bank, 512, tts[-1])
                slot = db % 2
                xf = self.a1_xfm.ap()[:, slot * 512:slot * 512 + 512]
                self.ps_rd("act", bank, 512)
                self.wr("act", [f"a1xfm{slot}"])
                t_ev = E.i("act", lambda e, _x=xf, _bk=bank: e.activation(
                    out=_x, in_=ps[:, _bk:_bk + 512], func=AF.Identity))
                self.ps_rdone(bank, 512, t_ev)
                self.wdone([f"a1xfm{slot}"], t_ev)
                E.ws("dve", [t_ev])
                t_cb = E.i("dve", lambda e, _x=xf, _db=db, _jg=jg: e.tensor_copy(
                    out=self.a1_xnb.ap()[:, _db * T + _jg * 512:
                                         _db * T + _jg * 512 + 512], in_=_x))
                self.rdone([f"a1xfm{slot}"], t_cb)
                self.buf("xnb").w = t_cb
                t_sp = E.dma("sy", lambda e, _x=xf, _b=b, _db=db, _jg=jg:
                             e.dma_start(out=self.sc_xn[_b, _db, :,
                                                        _jg * 512:(_jg + 1) * 512],
                                         in_=_x), waits=[t_ev],
                             slot=f"a1xfmd{slot}")
                self.rdone([f"a1xfm{slot}"], t_sp)
                E.ws("pe", [t_ev, self.buf("WrT").w])
                if db == 0:
                    self.ps_wr("pe", 1024, 512)
                t_r = E.i("pe", lambda e, _x=xf, _db=db: e.matmul(
                    ps[:, 1024:1536],
                    self.WrT.ap()[:, _db * 128:(_db + 1) * 128], _x,
                    start=(_db == 0), stop=(_db == DB - 1)))
                self.rdone([f"a1xfm{slot}"], t_r)
                if db == DB - 1:
                    self.ps_wdone(1024, 512, t_r)

            self.ps_rd("act", 1024, 512)
            self.wr("act", ["a1zr"])
            t_zr = E.i("act", lambda e: e.activation(
                out=self.a1_zr.ap(), in_=ps[:, 1024:1536], func=AF.Identity))
            self.ps_rdone(1024, 512, t_zr)
            self.wdone(["a1zr"], t_zr)
            t_zq = E.i("act", lambda e: e.activation(
                out=self.a1_zsq.ap(), in_=self.a1_zr.ap(), func=AF.Square))
            E.ws("pe", [t_zq, self.buf("onesg").w])
            self.ps_wr("pe", 1536, 512)
            t_ns = E.i("pe", lambda e: e.matmul(
                ps[0:1, 1536:2048], self.onesg.ap(), self.a1_zsq.ap(),
                start=True, stop=True))
            self.ps_wdone(1536, 512, t_ns)
            self.ps_rd("act", 1536, 512)
            t_nr = E.i("act", lambda e: e.activation(
                out=self.a1_nrm.ap(), in_=ps[0:1, 1536:2048], func=AF.Ln))
            self.ps_rdone(1536, 512, t_nr)
            t_iv = E.i("act", lambda e: e.activation(
                out=self.a1_inv.ap()[0:1, :], in_=self.a1_nrm.ap(),
                func=AF.Exp, scale=-0.5))
            E.ws("pe", [t_iv, self.buf("a1inv").w])
            self.ps_wr("pe", 1536, 512)
            t_ir = E.i("pe", lambda e: e.matmul(
                ps[:, 1536:2048], self.ones1r.ap(), self.a1_inv.ap(),
                start=True, stop=True))
            self.ps_wdone(1536, 512, t_ir)
            self.ps_rd("dve", 1536, 512)
            E.ws("dve", [t_zr])
            t_zn = E.i("dve", lambda e: e.tensor_mul(
                self.a1_znorm.ap(), self.a1_zr.ap(), ps[:, 1536:2048]))
            self.ps_rdone(1536, 512, t_zn)
            E.ws("pe", [t_zn, self.buf("cnT").w])
            self.ps_wr("pe", 1024, 512)
            t_lg = E.i("pe", lambda e: e.matmul(
                ps[0:16, 1024:1536], self.cnT.ap(), self.a1_znorm.ap(),
                start=True, stop=True))
            self.ps_wdone(1024, 512, t_lg)
            self.ps_rd("act", 1024, 512)
            t_le = E.i("act", lambda e: e.activation(
                out=self.a1_lg.ap()[0:16, :], in_=ps[0:16, 1024:1536],
                func=AF.Identity))
            self.ps_rdone(1024, 512, t_le)
            E.ws("pe", [t_le, self.buf("a1lg").w])
            self.ps_wr("pe", 1024, 128)
            tl = None
            for s in range(4):
                tl = E.i("pe", lambda e, _s=s: e.transpose(
                    ps[:, 1024 + _s * 32:1024 + (_s + 1) * 32].bitcast(F32R),
                    self.a1_lg.ap()[:, _s * 128:(_s + 1) * 128],
                    self.idn_r.ap()[0:32, 0:32]))
            self.ps_wdone(1024, 128, tl)
            self.ps_rd("act", 1024, 128)
            t_lt = E.i("act", lambda e: e.activation(
                out=self.a1_lgt.ap(), in_=ps[:, 1024:1152], func=AF.Identity))
            self.ps_rdone(1024, 128, t_lt)
            for s in range(4):
                lg = self.a1_lgt.ap()[:, s * 32:s * 32 + 16]
                smc = self.a1_smc.ap()
                msk = self.a1_msk.ap()
                e2 = self.a1_e2.ap()
                wv = self.a1_wtm.ap()[:, s * 16:(s + 1) * 16]
                E.ws("dve", [t_lt])
                t_mx = E.i("dve", lambda e, _l=lg, _c=smc: e.reduce_max(
                    _c[:, 0:1], _l, AX.X))
                E.wf("dve", t_mx)
                t_bi = E.i("dve", lambda e, _c=smc: e.tensor_scalar(
                    out=_c[:, 1:2], in0=_c[:, 0:1],
                    scalar1=self.scol.ap(), scalar2=-1.0, op0=OP.mult,
                    op1=OP.mult))
                E.ws("act", [t_bi, t_lt])
                t_ex = E.i("act", lambda e, _l=lg, _c=smc: e.activation(
                    out=_l, in_=_l, func=AF.Exp, scale=self.scol.ap(),
                    bias=_c[:, 1:2]))
                E.ws("dve", [t_ex])
                t_sm = E.i("dve", lambda e, _l=lg, _c=smc: e.reduce_sum(
                    _c[:, 2:3], _l, AX.X))
                E.ws("act", [t_sm])
                t_ln = E.i("act", lambda e, _c=smc: e.activation(
                    out=_c[:, 3:4], in_=_c[:, 2:3], func=AF.Ln))
                t_rc = E.i("act", lambda e, _c=smc: e.activation(
                    out=_c[:, 3:4], in_=_c[:, 3:4], func=AF.Exp, scale=-1.0))
                t_m1 = E.i("dve", lambda e, _l=lg, _c=smc: e.reduce_max(
                    _c[:, 4:5], _l, AX.X))
                E.wf("dve", t_m1)
                t_lt2 = E.i("dve", lambda e, _l=lg, _c=smc, _m=msk:
                            e.tensor_scalar(
                    out=_m, in0=_l, scalar1=_c[:, 4:5], scalar2=None,
                    op0=OP.is_lt))
                E.wf("dve", t_lt2)
                t_p2 = E.i("dve", lambda e, _l=lg, _m=msk, _e2=e2:
                           e.tensor_mul(_e2, _l, _m))
                E.wf("dve", t_p2)
                t_m2 = E.i("dve", lambda e, _c=smc, _e2=e2: e.reduce_max(
                    _c[:, 5:6], _e2, AX.X))
                E.wf("dve", t_m2)
                t_ge = E.i("dve", lambda e, _l=lg, _c=smc, _m=msk:
                           e.tensor_scalar(
                    out=_m, in0=_l, scalar1=_c[:, 5:6], scalar2=None,
                    op0=OP.is_ge))
                E.wf("dve", t_ge)
                t_wu = E.i("dve", lambda e, _l=lg, _m=msk, _e2=e2:
                           e.tensor_mul(_e2, _l, _m))
                E.ws("dve", [t_rc])
                E.wf("dve", t_wu)
                t_w = E.i("dve", lambda e, _w=wv, _c=smc, _e2=e2:
                          e.tensor_scalar(
                    out=_w, in0=_e2, scalar1=_c[:, 3:4], scalar2=None,
                    op0=OP.mult))
                self.buf(f"a1w{s}").w = t_w
            self.ps_wr("pe", 1024, 512)
            tws = []
            for s in range(4):
                E.ws("pe", [self.buf(f"a1w{s}").w, self.buf("idn").w])
                tw = E.i("pe", lambda e, _s=s: e.transpose(
                    ps[0:16, 1024 + _s * 128:1024 + (_s + 1) * 128],
                    self.a1_wtm.ap()[:, _s * 16:(_s + 1) * 16],
                    self.idn_f.ap()))
                tws.append(tw)
            self.ps_wdone(1024, 512, tws[-1])
            self.ps_rd("act", 1024, 512)
            self.wr("act", [f"wTb{jg}"])
            t_wt = E.i("act", lambda e, _jg=jg: e.activation(
                out=self.wTb.ap()[0:16, _jg * 512:(_jg + 1) * 512],
                in_=ps[0:16, 1024:1536], func=AF.Identity))
            self.ps_rdone(1024, 512, t_wt)
            self.wdone([f"wTb{jg}"], t_wt)

        if DEBUG_TAPS:
            E.dma("sy", lambda e, _b=b: e.dma_start(
                out=self.sc_wt[_b], in_=self.wTb.ap()),
                waits=[self.buf(f"wTb{j}").w for j in range(4)])
            E.dma("sy", lambda e, _b=b: e.dma_start(
                out=self.sc_xnb[_b], in_=self.a1_xnb.ap()),
                waits=[self.buf("xnb").w])

        def wi_load(n, slot):
            return E.dma("gps", lambda e, _n=n, _s=slot: e.dma_start(
                out=self.a1_wi.ap()[:, _s * 4096:(_s + 1) * 4096].rearrange("p (k h) -> p k h", k=8),
                in_=self.IEXP[_n].rearrange("(k p) h -> p k h", p=128)),
                waits=[self.buf(f"a1wi{slot}").w] + self.buf(f"a1wi{slot}").r,
                slot=f"a1wi{slot}")

        tok = wi_load(0, 0)
        self.buf("a1wi0").w = tok
        self.buf("a1wi0").r = []
        for n in range(NH):
            slot = n % 2
            if n + 1 < NH:
                nx = wi_load(n + 1, (n + 1) % 2)
                self.buf(f"a1wi{(n + 1) % 2}").w = nx
                self.buf(f"a1wi{(n + 1) % 2}").r = []
            wi = self.a1_wi.ap()[:, slot * 4096:(slot + 1) * 4096]

            def emit_out(hb, half, tg):
                E.ws("pe", [self.buf("woutE").w, tg])
                for t2 in range(2):
                    tc = half * 2 + t2
                    if hb == 0:
                        self.ps_wr("pe", tc * 512, 512)
                    hid = self.a1_hid.ap()[:, tc * 512:(tc + 1) * 512]
                    tpo = E.i("pe", lambda e, _h=hid, _n=n, _hb=hb, _tc=tc:
                              e.matmul(
                        ps[0:64, _tc * 512:_tc * 512 + 512],
                        self.woutE.ap()[:, (_n * 4 + _hb) * 64:
                                        (_n * 4 + _hb + 1) * 64],
                        _h, start=(_hb == 0), stop=(_hb == 3)))
                    self.rdone([f"a1hid{tc}"], tpo)
                    if hb == 3:
                        self.ps_wdone(tc * 512, 512, tpo)

            prev = None
            for u in range(8):
                hb, half = u // 2, u % 2
                pbase = 2048 + half * 1024
                for t2 in range(2):
                    self.ps_wr("pe", pbase + t2 * 512, 512)
                E.ws("pe", [self.buf(f"a1wi{slot}").w, self.buf("xnb").w])
                tm = None
                for k in range(8):
                    for t2 in range(2):
                        tc = half * 2 + t2
                        tm = E.i("pe", lambda e, _k=k, _hb=hb, _tc=tc,
                                 _pb=pbase, _t2=t2, _w=wi: e.matmul(
                            ps[:, _pb + _t2 * 512:_pb + _t2 * 512 + 512],
                            _w[:, _k * 512 + _hb * 128:_k * 512 + _hb * 128 + 128],
                            self.a1_xnb.ap()[:, _k * T + _tc * 512:
                                             _k * T + _tc * 512 + 512],
                            start=(_k == 0), stop=(_k == 7)))
                self.rdone([f"a1wi{slot}"], tm)
                for t2 in range(2):
                    self.ps_wdone(pbase + t2 * 512, 512, tm)
                self.ps_rd("act", pbase, 1024)
                self.wr("act", [f"a1hid{half * 2}", f"a1hid{half * 2 + 1}"])
                tg = E.i("act", lambda e, _pb=pbase, _hf=half: e.activation(
                    out=self.a1_hid.ap()[:, _hf * 1024:(_hf + 1) * 1024],
                    in_=ps[:, _pb:_pb + 1024], func=AF.Gelu))
                self.ps_rdone(pbase, 1024, tg)
                self.wdone([f"a1hid{half * 2}", f"a1hid{half * 2 + 1}"], tg)
                if DEBUG_TAPS and b == 0 and n == 0:
                    tdh = E.dma("sy", lambda e, _hb=hb, _hf=half: e.dma_start(
                        out=self.sc_hid[_hb, :, _hf * 1024:(_hf + 1) * 1024],
                        in_=self.a1_hid.ap()[:, _hf * 1024:(_hf + 1) * 1024]),
                        waits=[tg])
                    self.rdone([f"a1hid{half * 2}"], tdh)
                    self.rdone([f"a1hid{half * 2 + 1}"], tdh)
                if prev is not None:
                    emit_out(*prev)
                prev = (hb, half, tg)
            emit_out(*prev)
            fdst = self.a1_fe if n % 2 == 0 else self.a1_fo
            fnm = "a1fe" if n % 2 == 0 else "a1fo"
            for tc in range(4):
                wbank = 2048 + (tc % 2) * 512
                self.ps_wr("pe", wbank, 512)
                E.ws("pe", [self.buf("pairs").w, self.buf(f"wTb{tc}").w])
                trw = E.i("pe", lambda e, _n=n, _tc=tc, _wb=wbank: e.matmul(
                    ps[0:64, _wb:_wb + 512],
                    self.psel_b.ap()[:, _n * 64:(_n + 1) * 64],
                    self.wTb.ap()[:, _tc * 512:(_tc + 1) * 512],
                    start=True, stop=True))
                self.ps_wdone(wbank, 512, trw)
                ws = tc % 2
                wrp = self.a1_wrep.ap()[0:64, tc * 512:(tc + 1) * 512]
                self.ps_rd("act", wbank, 512)
                if tc < 2:
                    self.wr("act", ["a1wrep"])
                tew = E.i("act", lambda e, _w=wrp, _wb=wbank: e.activation(
                    out=_w, in_=ps[0:64, _wb:_wb + 512], func=AF.Identity))
                self.ps_rdone(wbank, 512, tew)
                if tc == 3:
                    self.wdone(["a1wrep"], tew)
                obank = tc * 512
                ffn = fdst.ap()[:, tc * 512:(tc + 1) * 512]
                self.ps_rd("dve", obank, 512)
                E.ws("dve", [tew])
                if tc == 0:
                    self.wr("dve", [fnm])
                tsc = E.i("dve", lambda e, _f=ffn, _w=wrp, _ob=obank:
                          e.tensor_mul(_f, ps[0:64, _ob:_ob + 512], _w))
                self.ps_rdone(obank, 512, tsc)
                self.rdone(["a1wrep"], tsc)
                if tc == 3:
                    self.wdone([fnm], tsc)
                if DEBUG_TAPS and b == 0 and n <= 1:
                    tdf = E.dma("sy", lambda e, _f=ffn, _tc=tc, _n=n:
                                e.dma_start(
                        out=self.sc_fp.rearrange(
                            "(h p) t -> h p t", h=2)[_n % 2, :,
                                                     _tc * 512:(_tc + 1) * 512],
                        in_=_f), waits=[tsc])
                    self.rdone([fnm], tdf)

            if n % 2 == 1:
                pr = n // 2
                for tc in range(4):
                    pbank = 3072 + (tc % 2) * 512
                    self.ps_wr("pe", pbank, 512)
                    self.rd("pe", ["a1fe", "a1fo"])
                    E.ws("pe", [self.buf("idn").w])
                    tp = None
                    for j4 in range(4):
                        for hh, fsrc in ((0, self.a1_fe), (1, self.a1_fo)):
                            tp = E.i("pe", lambda e, _f=fsrc, _j=j4, _pb=pbank,
                                     _tc=tc, _hh=hh: e.transpose(
                                ps[:, _pb + _j * 128 + _hh * 64:
                                   _pb + _j * 128 + _hh * 64 + 64],
                                _f.ap()[:, _tc * 512 + _j * 128:
                                        _tc * 512 + (_j + 1) * 128],
                                self.idn_f.ap()[0:64, 0:64]))
                    self.rdone(["a1fe", "a1fo"], tp)
                    self.ps_wdone(pbank, 512, tp)
                    osl = tc % 2
                    ost = self.a1_ops.ap()[:, osl * 512:(osl + 1) * 512]
                    self.ps_rd("act", pbank, 512)
                    self.wr("act", [f"a1ops{osl}"])
                    teo = E.i("act", lambda e, _o=ost, _pb=pbank: e.activation(
                        out=_o, in_=ps[:, _pb:_pb + 512], func=AF.Identity))
                    self.ps_rdone(pbank, 512, teo)
                    self.wdone([f"a1ops{osl}"], teo)
                    tdo = E.dma("sy", lambda e, _o=ost, _b=b, _pr=pr, _tc=tc:
                                e.dma_start(
                        out=self.sc_op[_b, _tc * 512:(_tc + 1) * 512,
                                       _pr * 128:(_pr + 1) * 128]
                        .rearrange("(j p) c -> p j c", p=128),
                        in_=_o.rearrange("p (j c) -> p j c", c=128)),
                        waits=[teo], slot=f"a1ops{osl}")
                    self.rdone([f"a1ops{osl}"], tdo)

    def phase_a2a(self, b):
        E = self.E
        ps = self.ps.ap()

        t_w = E.dma("gps", lambda e: e.dma_start(
            out=self.a2_win.ap().rearrange("p (k e) -> p k e", k=8),
            in_=self.WIN.rearrange("(k p) e -> p k e", p=128)),
            waits=[self.buf("a2win").w] + self.buf("a2win").r, slot="a2win")
        self.buf("a2win").w = t_w
        self.buf("a2win").r = []
        t_wg = E.dma("gps", lambda e: e.dma_start(
            out=self.a2_wg.ap().rearrange("p (k e) -> p k e", k=8),
            in_=self.WG.rearrange("(k p) e -> p k e", p=128)),
            waits=[self.buf("a2wg").w] + self.buf("a2wg").r, slot="a2wg")
        self.buf("a2wg").w = t_wg
        self.buf("a2wg").r = []
        E.ws("dve", [self.buf("cw").w, self.buf("idnr").w])
        self.wr("dve", ["a2diag"])
        td = None
        for db in range(DB):
            for tap in range(3):
                idx = db * 3 + tap
                td = E.i("dve", lambda e, _i=idx: e.tensor_scalar(
                    out=self.a2_diag.ap()[:, _i * 128:(_i + 1) * 128],
                    in0=self.idn_r.ap(), scalar1=self.cwcol.ap()[:, _i:_i + 1],
                    scalar2=None, op0=OP.mult))
        self.wdone(["a2diag"], td)
        t_hz2 = None
        for _db in range(DB):
            for _c in (0, CA + 1):
                t_hz2 = E.i("dve", lambda e, _d=_db, _cc=_c: e.tensor_scalar(
                    out=self.a2_xpre.ap()[:, _d * (CA + 2) + _cc:
                                          _d * (CA + 2) + _cc + 1],
                    in0=self.colpad.ap()[:, 8:9], scalar1=0.0, scalar2=None,
                    op0=OP.mult))
        self.buf("a2halo").w = t_hz2

        for ch in range(NCA):
            xs = ch % 2
            xn = self.a2_xn.ap()[:, xs * DB * CA:(xs + 1) * DB * CA]
            t_x = E.dma("sy", lambda e, _x=xn, _b=b, _ch=ch: e.dma_start(
                out=_x.rearrange("p (k c) -> p k c", k=DB),
                in_=self.sc_xn[_b, :, :, _ch * CA:(_ch + 1) * CA]
                .rearrange("k p c -> p k c")),
                waits=[self.buf(f"a2xn{xs}").w] + self.buf(f"a2xn{xs}").r,
                slot=f"a2xn{xs}")
            self.buf(f"a2xn{xs}").w = t_x
            self.buf(f"a2xn{xs}").r = []

            cslot = ch % 2
            cpre = self.a2_cpre.ap()[:, cslot * DB * CA:(cslot + 1) * DB * CA]

            for e_ in range(DB):
                bank = (e_ % 4) * 512
                self.ps_wr("pe", bank, 512)
                E.ws("pe", [t_x, self.buf("a2win").w])
                tm = None
                for k in range(8):
                    tm = E.i("pe", lambda e, _k=k, _e=e_, _x=xn, _bk=bank:
                             e.matmul(
                        ps[:, _bk:_bk + 512],
                        self.a2_win.ap()[:, _k * 1024 + _e * 128:
                                         _k * 1024 + _e * 128 + 128],
                        _x[:, _k * CA:(_k + 1) * CA],
                        start=(_k == 0), stop=(_k == 7)))
                self.rdone([f"a2xn{xs}"], tm)
                self.ps_wdone(bank, 512, tm)
                xpt = self.a2_xpre.ap()[:, e_ * (CA + 2):(e_ + 1) * (CA + 2)]
                xp = xpt[:, 1:CA + 1]
                self.ps_rd("act", bank, 512)
                self.wr("act", [f"a2xp{e_}"])
                E.ws("act", [self.buf("bin").w])
                te = E.i("act", lambda e, _x=xp, _bk=bank, _e=e_: e.activation(
                    out=_x, in_=ps[:, _bk:_bk + 512], func=AF.Identity,
                    bias=self.bin_c.ap()[:, _e:_e + 1]))
                self.ps_rdone(bank, 512, te)
                self.wdone([f"a2xp{e_}"], te)

                if ch > 0:
                    pslot = (ch - 1) % 2
                    cpv = self.a2_cpre.ap()[:, pslot * DB * CA + e_ * CA:
                                            pslot * DB * CA + (e_ + 1) * CA]
                    E.ws("dve", [te, self.buf(f"a2cp{pslot}_{e_}").w,
                                 self.buf("cw").w])
                    tf = E.i("dve", lambda e, _c=cpv, _x=xpt, _e=e_:
                             e.scalar_tensor_tensor(
                        out=_c[:, CA - 1:CA], in0=_x[:, 1:2],
                        scalar=self.cwcol.ap()[:, _e * 3 + 2:_e * 3 + 3],
                        in1=_c[:, CA - 1:CA], op0=OP.mult, op1=OP.add))
                    self.rdone([f"a2xp{e_}"], tf)
                    E.ws("act", [tf])
                    ts = E.i("act", lambda e, _c=cpv: e.activation(
                        out=_c.bitcast(F32R), in_=_c, func=AF.Silu))
                    self.buf(f"a2cp{pslot}_{e_}").w = ts
                    tsp = E.dma("sy", lambda e, _c=cpv, _b=b, _e=e_, _ch=ch:
                                e.dma_start(
                        out=self.sc_x[_b, _e, :, (_ch - 1) * CA:_ch * CA],
                        in_=_c.bitcast(F32R)), waits=[ts],
                        slot=f"a2cp{pslot}")
                    self.buf(f"a2cp{pslot}_{e_}").r = [tsp]

                cbank = 2048 + (e_ % 4) * 512
                self.ps_wr("pe", cbank, 512)
                E.ws("pe", [te, self.buf("a2diag").w, self.buf("a2halo").w])
                dg = self.a2_diag.ap()
                t1 = E.i("pe", lambda e, _x=xpt, _e=e_, _bk=cbank: e.matmul(
                    ps[:, _bk:_bk + 512],
                    dg[:, (_e * 3 + 0) * 128:(_e * 3 + 1) * 128],
                    _x[:, 0:CA], start=True, stop=False))
                t2 = E.i("pe", lambda e, _x=xpt, _e=e_, _bk=cbank: e.matmul(
                    ps[:, _bk:_bk + 512],
                    dg[:, (_e * 3 + 1) * 128:(_e * 3 + 2) * 128],
                    _x[:, 1:CA + 1], start=False, stop=False))
                t4 = E.i("pe", lambda e, _x=xpt, _e=e_, _bk=cbank: e.matmul(
                    ps[:, _bk:_bk + 512],
                    dg[:, (_e * 3 + 2) * 128:(_e * 3 + 3) * 128],
                    _x[:, 2:CA + 2], start=False, stop=True))
                self.ps_wdone(cbank, 512, t4)
                self.rdone([f"a2xp{e_}"], t4)
                E.ws("act", [t4])
                th = E.i("act", lambda e, _x=xpt: e.activation(
                    out=_x[:, 0:1], in_=_x[:, CA:CA + 1], func=AF.Identity))
                if e_ == DB - 1:
                    self.buf("a2halo").w = th
                cpc = self.a2_cpre.ap()[:, cslot * DB * CA + e_ * CA:
                                        cslot * DB * CA + (e_ + 1) * CA]
                self.ps_rd("act", cbank, 512)
                self.wr("act", [f"a2cp{cslot}_{e_}"])
                E.ws("act", [self.buf("convb").w])
                tev = E.i("act", lambda e, _c=cpc, _bk=cbank, _e=e_: e.activation(
                    out=_c, in_=ps[:, _bk:_bk + 512], func=AF.Identity,
                    bias=self.convb_c.ap()[:, _e:_e + 1]))
                self.ps_rdone(cbank, 512, tev)
                self.wdone([f"a2cp{cslot}_{e_}"], tev)

                if ch == NCA - 1:
                    E.ws("act", [tev])
                    ts = E.i("act", lambda e, _c=cpc: e.activation(
                        out=_c.bitcast(F32R), in_=_c, func=AF.Silu))
                    self.buf(f"a2cp{cslot}_{e_}").w = ts
                    tsp = E.dma("sy", lambda e, _c=cpc, _b=b, _e=e_, _ch=ch:
                                e.dma_start(
                        out=self.sc_x[_b, _e, :, _ch * CA:(_ch + 1) * CA],
                        in_=_c.bitcast(F32R)), waits=[ts],
                        slot=f"a2cp{cslot}")
                    self.buf(f"a2cp{cslot}_{e_}").r = [tsp]

            zsl = 0
            zv = self.a2_z.ap()
            tz = None
            for e_ in range(DB):
                bank = (e_ % 4) * 512
                self.ps_wr("pe", bank, 512)
                E.ws("pe", [t_x, self.buf("a2wg").w])
                tm = None
                for k in range(8):
                    tm = E.i("pe", lambda e, _k=k, _e=e_, _x=xn, _bk=bank:
                             e.matmul(
                        ps[:, _bk:_bk + 512],
                        self.a2_wg.ap()[:, _k * 1024 + _e * 128:
                                        _k * 1024 + _e * 128 + 128],
                        _x[:, _k * CA:(_k + 1) * CA],
                        start=(_k == 0), stop=(_k == 7)))
                self.rdone([f"a2xn{xs}"], tm)
                self.ps_wdone(bank, 512, tm)
                self.ps_rd("act", bank, 512)
                if e_ == 0:
                    self.wr("act", [f"a2z{zsl}"])
                E.ws("act", [self.buf("bg").w])
                tz = E.i("act", lambda e, _e=e_, _z=zv, _bk=bank: e.activation(
                    out=_z[:, _e * CA:(_e + 1) * CA], in_=ps[:, _bk:_bk + 512],
                    func=AF.Silu, bias=self.bg_c.ap()[:, _e:_e + 1]))
                self.ps_rdone(bank, 512, tz)
                if e_ == DB - 1:
                    self.wdone([f"a2z{zsl}"], tz)
            tzs = E.dma("sy", lambda e, _z=zv, _b=b, _ch=ch: e.dma_start(
                out=self.sc_z[_b, :, :, _ch * CA:(_ch + 1) * CA]
                .rearrange("k p c -> p k c"),
                in_=_z.rearrange("p (k c) -> p k c", k=DB)), waits=[tz],
                slot="a2zsp")
            self.rdone([f"a2z{zsl}"], tzs)

    def phase_a2b_taps(self, b):
        E = self.E
        if not DEBUG_TAPS:
            return
        tb1 = E.dma("sy", lambda e, _b=b: e.dma_start(
            out=self.sc_bm[_b], in_=self.Bm.ap()),
            waits=[self.buf("Bm").w])
        self.rdone(["Bm"], tb1)
        tb2 = E.dma("sy", lambda e, _b=b: e.dma_start(
            out=self.sc_cm[_b], in_=self.Cm.ap()),
            waits=[self.buf("Cm").w])
        self.rdone(["Cm"], tb2)

    def phase_a2b(self, b):
        E = self.E
        ps = self.ps.ap()

        t_wd = E.dma("gps", lambda e: e.dma_start(
            out=self.b2_wdt.ap().rearrange("p (k e) -> p k e", k=8),
            in_=self.WDT.rearrange("(k p) e -> p k e", p=128)),
            waits=[self.buf("b2wdt").w] + self.buf("b2wdt").r, slot="b2wdt")
        self.buf("b2wdt").w = t_wd
        t_wb = E.dma("gps", lambda e: e.dma_start(
            out=self.b2_wb.ap().rearrange("p (k n) -> p k n", k=8),
            in_=self.WB.rearrange("(k p) n -> p k n", p=128)),
            waits=[self.buf("b2wb").w] + self.buf("b2wb").r, slot="b2wb")
        self.buf("b2wb").w = t_wb
        t_wc = E.dma("gps", lambda e: e.dma_start(
            out=self.b2_wc.ap().rearrange("p (k n) -> p k n", k=8),
            in_=self.WC.rearrange("(k p) n -> p k n", p=128)),
            waits=[self.buf("b2wc").w] + self.buf("b2wc").r, slot="b2wc")
        self.buf("b2wc").w = t_wc

        for ch in range(NCB):
            xs = ch % 2
            xv = self.b2_x.ap()[:, xs * DB * CB:(xs + 1) * DB * CB]
            t_x = E.dma("sy", lambda e, _x=xv, _b=b, _ch=ch: e.dma_start(
                out=_x.rearrange("p (k c) -> p k c", k=DB),
                in_=self.sc_x[_b, :, :, _ch * CB:(_ch + 1) * CB]
                .rearrange("k p c -> p k c")),
                waits=[self.buf(f"b2x{xs}").w] + self.buf(f"b2x{xs}").r,
                slot=f"b2x{xs}")
            self.buf(f"b2x{xs}").w = t_x
            self.buf(f"b2x{xs}").r = []

            sg = self.b2_sg.ap()[:, xs * DB * CB:(xs + 1) * DB * CB]
            for e_ in range(DB):
                bank = (e_ % 2) * 512
                self.ps_wr("pe", bank, 512)
                E.ws("pe", [t_x, self.buf("b2wdt").w])
                tm = None
                for k in range(8):
                    tm = E.i("pe", lambda e, _k=k, _e=e_, _x=xv, _bk=bank:
                             e.matmul(
                        ps[:, _bk:_bk + CB],
                        self.b2_wdt.ap()[:, _k * 1024 + _e * 128:
                                         _k * 1024 + _e * 128 + 128],
                        _x[:, _k * CB:(_k + 1) * CB],
                        start=(_k == 0), stop=(_k == 7)))
                self.rdone([f"b2x{xs}"], tm)
                self.ps_wdone(bank, 512, tm)
                self.ps_rd("act", bank, 512)
                self.wr("act", [f"b2sg{xs}"]) if e_ == 0 else None
                E.ws("act", [self.buf("nbdt").w])
                te = E.i("act", lambda e, _e=e_, _s=sg, _bk=bank: e.activation(
                    out=_s[:, _e * CB:(_e + 1) * CB], in_=ps[:, _bk:_bk + CB],
                    func=AF.Sigmoid, scale=-1.0,
                    bias=self.nbdt_c.ap()[:, _e:_e + 1]))
                self.ps_rdone(bank, 512, te)
                if e_ == DB - 1:
                    self.wdone([f"b2sg{xs}"], te)
            tl = E.i("act", lambda e, _s=sg: e.activation(
                out=_s, in_=_s, func=AF.Ln))
            self.buf(f"b2sg{xs}").w = tl
            tdt = E.dma("sy", lambda e, _s=sg, _b=b, _ch=ch: e.dma_start(
                out=self.sc_dt[_b, :, :, _ch * CB:(_ch + 1) * CB]
                .rearrange("k p c -> p k c"),
                in_=_s.rearrange("p (k c) -> p k c", k=DB)), waits=[tl],
                slot=f"b2sg{xs}d")
            self.rdone([f"b2sg{xs}"], tdt)
            dx = self.b2_dtx.ap()[:, xs * DB * CB:(xs + 1) * DB * CB]
            E.ws("dve", [tl, t_x])
            self.wr("dve", [f"b2dtx{xs}"])
            tx = E.i("dve", lambda e, _d=dx, _s=sg, _x=xv: e.tensor_mul(
                _d, _s, _x.bitcast(F32)))
            self.rdone([f"b2x{xs}"], tx)
            self.rdone([f"b2sg{xs}"], tx)
            self.wdone([f"b2dtx{xs}"], tx)
            tdx = E.dma("sy", lambda e, _d=dx, _b=b, _ch=ch: e.dma_start(
                out=self.sc_dtx[_b, :, :, _ch * CB:(_ch + 1) * CB]
                .rearrange("k p c -> p k c"),
                in_=_d.rearrange("p (k c) -> p k c", k=DB)), waits=[tx],
                slot=f"b2dtx{xs}d")
            self.rdone([f"b2dtx{xs}"], tdx)

            for (wsb, dst, biascol, scl, nm) in (
                    (self.b2_wb, self.Bm, self.nbb_c, -1.0, "Bm"),
                    (self.b2_wc, self.Cm, self.bc_c, 1.0, "Cm")):
                bank = 1024
                self.ps_wr("pe", bank, 512)
                E.ws("pe", [t_x, self.buf("b2wb").w, self.buf("b2wc").w])
                tm = None
                for k in range(8):
                    tm = E.i("pe", lambda e, _k=k, _w=wsb, _x=xv, _bk=bank:
                             e.matmul(
                        ps[0:16, _bk:_bk + CB],
                        _w.ap()[:, _k * 16:(_k + 1) * 16],
                        _x[:, _k * CB:(_k + 1) * CB],
                        start=(_k == 0), stop=(_k == 7)))
                self.rdone([f"b2x{xs}"], tm)
                self.ps_wdone(bank, 512, tm)
                self.ps_rd("act", bank, 512)
                self.wr("act", [nm])
                E.ws("act", [self.buf("nbb").w, self.buf("bc").w])
                tb = E.i("act", lambda e, _d=dst, _c=biascol, _s=scl, _b2=b,
                         _ch=ch, _bk=bank: e.activation(
                    out=_d.ap()[0:16, _ch * CB:(_ch + 1) * CB],
                    in_=ps[0:16, _bk:_bk + CB], func=AF.Identity, scale=_s,
                    bias=_c.ap()))
                self.ps_rdone(bank, 512, tb)
                self.buf(nm).w = tb


    def phase_b(self, b):
        E = self.E
        ps = self.ps.ap()

        for q in range(4):
            ns = list(range(4 * q, 4 * q + 4))
            for i, n in enumerate(ns):
                for (src, dstbuf, dst) in ((self.Bm, "brep", self.s_brep),
                                           (self.Cm, "crep", self.s_crep)):
                    for tc in range(4):
                        bank = (tc % 2) * 512
                        self.ps_wr("pe", bank, 512)
                        E.ws("pe", [self.buf("Bm").w, self.buf("Cm").w,
                                    self.buf("osel").w])
                        tr = E.i("pe", lambda e, _s=src, _n=n, _tc=tc, _bk=bank:
                                 e.matmul(
                            ps[:, _bk:_bk + 512],
                            self.osel_b.ap()[:, _n * 128:(_n + 1) * 128],
                            _s.ap()[:, _tc * 512:(_tc + 1) * 512],
                            start=True, stop=True))
                        self.ps_wdone(bank, 512, tr)
                        self.ps_rd("act", bank, 512)
                        if tc == 0:
                            self.wr("act", [f"{dstbuf}{i}"])
                        tv = E.i("act", lambda e, _d=dst, _i=i, _tc=tc, _bk=bank:
                                 e.activation(
                            out=_d.ap()[:, _i * T + _tc * 512:
                                        _i * T + (_tc + 1) * 512],
                            in_=ps[:, _bk:_bk + 512], func=AF.Identity))
                        self.ps_rdone(bank, 512, tv)
                        if tc == 3:
                            self.wdone([f"{dstbuf}{i}"], tv)
                            if DEBUG_TAPS and b == 0:
                                scd = (self.sc_brep if dstbuf == "brep"
                                       else self.sc_crep)
                                tdr = E.dma("sy", lambda e, _d=dst, _i=i, _n=n,
                                            _s=scd: e.dma_start(
                                    out=_s[_n],
                                    in_=_d.ap()[:, _i * T:(_i + 1) * T]),
                                    waits=[tv])
                                self.rdone([f"{dstbuf}{i}"], tdr)

            def load_db(db):
                sl = db % 2
                tdt = E.dma("act", lambda e, _b=b, _db=db, _sl=sl: e.dma_start(
                    out=self.s_dt.ap()[:, _sl * T:(_sl + 1) * T],
                    in_=self.sc_dt[_b, _db, :, :]),
                    waits=[self.buf(f"sdt{sl}").w] + self.buf(f"sdt{sl}").r,
                    slot=f"sdt{sl}")
                self.buf(f"sdt{sl}").w = tdt
                self.buf(f"sdt{sl}").r = []
                tdx = E.dma("act", lambda e, _b=b, _db=db, _sl=sl: e.dma_start(
                    out=self.s_dtx.ap()[:, _sl * T:(_sl + 1) * T],
                    in_=self.sc_dtx[_b, _db, :, :]),
                    waits=[self.buf(f"sdx{sl}").w] + self.buf(f"sdx{sl}").r,
                    slot=f"sdx{sl}")
                self.buf(f"sdx{sl}").w = tdx
                self.buf(f"sdx{sl}").r = []

            load_db(0)
            for db in range(DB):
                if db + 1 < DB:
                    load_db(db + 1)
                sl = db % 2
                dtv = self.s_dt.ap()[:, sl * T:(sl + 1) * T]
                dxv = self.s_dtx.ap()[:, sl * T:(sl + 1) * T]

                if q > 0:
                    E.ws("pe", [self.buf(f"ysb{db}").w, self.buf("idnr").w])
                    for tc in range(4):
                        self.ps_wr("pe", tc * 512, 512)
                    tj = None
                    for tc in range(4):
                        tj = E.i("pe", lambda e, _db=db, _tc=tc: e.matmul(
                            ps[:, _tc * 512:(_tc + 1) * 512], self.idn_r.ap(),
                            self.s_ysb.ap()[:, _db * T + _tc * 512:
                                            _db * T + (_tc + 1) * 512],
                            start=True, stop=False))
                    self.rdone([f"ysb{db}"], tj)
                    for tc in range(4):
                        self.ps_wdone(tc * 512, 512, tj)
                else:
                    for tc in range(4):
                        self.ps_wr("pe", tc * 512, 512)

                for i, n in enumerate(ns):
                    for h in (0, 1):
                        step = i * 2 + h
                        ping = 2048 + (step % 2) * 1024
                        self.ps_wr("act", ping, 1024)
                        E.ws("act", [self.buf(f"sdt{sl}").w,
                                     self.buf("negA").w])
                        ta = E.i("act", lambda e, _d=dtv, _h=h, _db=db, _n=n,
                                 _pg=ping: e.activation(
                            out=ps[:, _pg:_pg + 1024],
                            in_=_d[:, _h * TH:(_h + 1) * TH], func=AF.Exp,
                            scale=self.negA.ap()[:, _db * 16 + _n:
                                                 _db * 16 + _n + 1]))
                        self.ps_wdone(ping, 1024, ta)
                        self.rdone([f"sdt{sl}"], ta)
                        ds = step % 3
                        dbx = self.s_dbx.ap()[:, ds * TH:(ds + 1) * TH]
                        self.wr("gps", [f"dbx{ds}"])
                        E.ws("gps", [self.buf(f"sdx{sl}").w,
                                     self.buf(f"brep{i}").w])
                        tb = E.i("gps", lambda e, _x=dxv, _o=dbx, _i=i, _h=h:
                                 e.tensor_mul(
                            _o, _x[:, _h * TH:(_h + 1) * TH],
                            self.s_brep.ap()[:, _i * T + _h * TH:
                                             _i * T + (_h + 1) * TH]))
                        self.wdone([f"dbx{ds}"], tb)
                        self.rdone([f"sdx{sl}"], tb)
                        self.rdone([f"brep{i}"], tb)
                        hs = step % 3
                        hv = self.s_h.ap()[:, hs * TH:(hs + 1) * TH]
                        if h == 0:
                            init = 0.0
                        else:
                            hp = (step - 1) % 3
                            init = self.s_h.ap()[:, hp * TH + TH - 1:
                                                 hp * TH + TH]
                        self.ps_rd("dve", ping, 1024)
                        self.wr("dve", [f"sh{hs}"])
                        E.ws("dve", [ta, tb])
                        tts = E.i("dve", lambda e, _o=hv, _pg=ping, _d=dbx,
                                  _in=init: e.tensor_tensor_scan(
                            out=_o, data0=ps[:, _pg:_pg + 1024], data1=_d,
                            initial=_in, op0=OP.mult, op1=OP.add))
                        if DEBUG_TAPS and b == 0 and db == 0:
                            tdh = E.dma("sy", lambda e, _n=n, _h=h, _o=hv:
                                        e.dma_start(
                                out=self.sc_h[_n, :, _h * TH:(_h + 1) * TH],
                                in_=_o), waits=[tts])
                            self.rdone([f"sh{hs}"], tdh)
                            tdb = E.dma("sy", lambda e, _n=n, _h=h, _d=dbx:
                                        e.dma_start(
                                out=self.sc_dbx[_n, :, _h * TH:(_h + 1) * TH],
                                in_=_d), waits=[tb])
                            self.rdone([f"dbx{ds}"], tdb)
                        self.ps_rdone(ping, 1024, tts)
                        self.rdone([f"dbx{ds}"], tts)
                        self.wdone([f"sh{hs}"], tts)
                        cs = step % 3
                        hcv = self.s_hc.ap()[:, cs * TH:(cs + 1) * TH]
                        self.wr("dve", [f"shc{cs}"])
                        E.ws("dve", [self.buf(f"crep{i}").w])
                        thc = E.i("dve", lambda e, _o=hcv, _h=hv, _i=i, _hh=h:
                                  e.tensor_mul(
                            _o, _h,
                            self.s_crep.ap()[:, _i * T + _hh * TH:
                                             _i * T + (_hh + 1) * TH]))
                        self.rdone([f"sh{hs}"], thc)
                        self.rdone([f"crep{i}"], thc)
                        self.wdone([f"shc{cs}"], thc)
                        E.ws("pe", [thc, self.buf("idnb").w])
                        tp = None
                        for t2 in range(2):
                            tc = h * 2 + t2
                            tp = E.i("pe", lambda e, _hc=hcv, _t2=t2, _tc=tc,
                                     _st=(q == 0 and i == 0),
                                     _sp=(i == 3): e.matmul(
                                ps[:, _tc * 512:(_tc + 1) * 512],
                                self.idn_b.ap(),
                                _hc[:, _t2 * 512:(_t2 + 1) * 512],
                                start=_st, stop=_sp))
                        self.rdone([f"shc{cs}"], tp)
                        self.ps_wdone(h * 2 * 512, 1024, tp)

                if q < 3:
                    self.ps_rd("act", 0, 2048)
                    self.wr("act", [f"ysb{db}"])
                    ty = E.i("act", lambda e, _db=db: e.activation(
                        out=self.s_ysb.ap()[:, _db * T:(_db + 1) * T],
                        in_=ps[:, 0:2048], func=AF.Identity))
                    self.ps_rdone(0, 2048, ty)
                    self.wdone([f"ysb{db}"], ty)
                else:
                    for h in (0, 1):
                        zs = h % 2
                        zv = self.s_z.ap()[:, zs * TH:(zs + 1) * TH]
                        tz = E.dma("act", lambda e, _b=b, _db=db, _h=h, _z=zv:
                                   e.dma_start(
                            out=_z,
                            in_=self.sc_z[_b, _db, :, _h * TH:(_h + 1) * TH]),
                            waits=[self.buf(f"sz{zs}").w] + self.buf(f"sz{zs}").r,
                            slot=f"sz{zs}")
                        self.buf(f"sz{zs}").w = tz
                        self.buf(f"sz{zs}").r = []
                        ys = h % 2
                        yzv = self.s_yz.ap()[:, ys * TH:(ys + 1) * TH]
                        self.ps_rd("dve", h * 1024, 1024)
                        self.wr("dve", [f"syz{ys}"])
                        E.ws("dve", [tz])
                        tyz = E.i("dve", lambda e, _o=yzv, _z=zv, _h=h:
                                  e.tensor_mul(
                            _o, ps[:, _h * 1024:(_h + 1) * 1024], _z))
                        self.ps_rdone(h * 1024, 1024, tyz)
                        self.rdone([f"sz{zs}"], tyz)
                        self.wdone([f"syz{ys}"], tyz)
                        tsp = E.dma("sy", lambda e, _o=yzv, _b=b, _db=db, _h=h:
                                    e.dma_start(
                            out=self.sc_yz[_b, _db, :, _h * TH:(_h + 1) * TH],
                            in_=_o), waits=[tyz], slot=f"syz{ys}")
                        self.rdone([f"syz{ys}"], tsp)

    def phase_c(self, b):
        E = self.E
        ps = self.ps.ap()

        t_w = E.dma("gps", lambda e: e.dma_start(
            out=self.c_wout.ap().rearrange("p (k e) -> p k e", k=8),
            in_=self.WOUT.rearrange("(k p) e -> p k e", p=128)),
            waits=[self.buf("cwout").w] + self.buf("cwout").r, slot="cwout")
        self.buf("cwout").w = t_w
        self.buf("cwout").r = []

        def load_ch(ch):
            sl = ch % 2
            tyz = E.dma("act", lambda e, _b=b, _ch=ch, _sl=sl: e.dma_start(
                out=self.c_yz.ap()[:, _sl * DB * 512:(_sl + 1) * DB * 512]
                .rearrange("p (k c) -> p k c", k=DB),
                in_=self.sc_yz[_b, :, :, _ch * CC:(_ch + 1) * CC]
                .rearrange("k p c -> p k c")),
                waits=[self.buf(f"cyz{sl}").w] + self.buf(f"cyz{sl}").r,
                slot=f"cyz{sl}")
            self.buf(f"cyz{sl}").w = tyz
            self.buf(f"cyz{sl}").r = []
            tuo = E.dma("sy", lambda e, _b=b, _ch=ch, _sl=sl: e.dma_start(
                out=self.c_uo.ap()[:, _sl * 4096:(_sl + 1) * 4096]
                .rearrange("p (j c) -> p j c", j=4),
                in_=self.U[_b, _ch * CC:(_ch + 1) * CC, :]
                .rearrange("(j p) c -> p j c", p=128)),
                waits=[self.buf(f"cuo{sl}").w] + self.buf(f"cuo{sl}").r,
                slot=f"cuo{sl}")
            self.buf(f"cuo{sl}").w = tuo
            self.buf(f"cuo{sl}").r = []
            top = E.dma("sy", lambda e, _b=b, _ch=ch, _sl=sl: e.dma_start(
                out=self.c_op.ap()[:, _sl * 4096:(_sl + 1) * 4096]
                .rearrange("p (j c) -> p j c", j=4),
                in_=self.sc_op[_b, _ch * CC:(_ch + 1) * CC, :]
                .rearrange("(j p) c -> p j c", p=128)),
                waits=[self.buf(f"cop{sl}").w] + self.buf(f"cop{sl}").r,
                slot=f"cop{sl}")
            self.buf(f"cop{sl}").w = top
            self.buf(f"cop{sl}").r = []

        load_ch(0)
        for ch in range(NCC):
            if ch + 1 < NCC:
                load_ch(ch + 1)
            sl = ch % 2
            yz = self.c_yz.ap()[:, sl * DB * 512:(sl + 1) * DB * 512]
            uo = self.c_uo.ap()[:, sl * 4096:(sl + 1) * 4096]
            op = self.c_op.ap()[:, sl * 4096:(sl + 1) * 4096]
            E.ws("dve", [self.buf(f"cuo{sl}").w, self.buf(f"cop{sl}").w])
            tadd = E.i("dve", lambda e, _u=uo, _o=op: e.tensor_add(_u, _u, _o))
            self.rdone([f"cop{sl}"], tadd)
            self.buf(f"cuo{sl}").w = tadd
            E.wf("dve", tadd)

            for e_ in range(DB):
                bank = (e_ % 2) * 512
                self.ps_wr("pe", bank, 512)
                E.ws("pe", [self.buf(f"cyz{sl}").w, self.buf("cwout").w])
                tm = None
                for k in range(8):
                    tm = E.i("pe", lambda e, _k=k, _e=e_, _y=yz, _bk=bank:
                             e.matmul(
                        ps[:, _bk:_bk + 512],
                        self.c_wout.ap()[:, _k * 1024 + _e * 128:
                                         _k * 1024 + _e * 128 + 128],
                        _y[:, _k * 512:(_k + 1) * 512],
                        start=(_k == 0), stop=(_k == 7)))
                self.rdone([f"cyz{sl}"], tm)
                self.ps_wdone(bank, 512, tm)
                asl = e_ % 2
                att = self.c_attn.ap()[:, asl * 512:(asl + 1) * 512]
                self.ps_rd("act", bank, 512)
                self.wr("act", [f"cat{asl}"])
                E.ws("act", [self.buf("bout").w])
                te = E.i("act", lambda e, _a=att, _bk=bank, _e=e_: e.activation(
                    out=_a, in_=ps[:, _bk:_bk + 512], func=AF.Identity,
                    bias=self.bout_c.ap()[:, _e:_e + 1]))
                self.ps_rdone(bank, 512, te)
                self.wdone([f"cat{asl}"], te)
                pbank = 1024 + (e_ % 2) * 512
                self.ps_wr("pe", pbank, 512)
                E.ws("pe", [te, self.buf("idn").w])
                tp = None
                for j4 in range(4):
                    tp = E.i("pe", lambda e, _a=att, _j=j4, _pb=pbank:
                             e.transpose(
                        ps[:, _pb + _j * 128:_pb + (_j + 1) * 128],
                        _a[:, _j * 128:(_j + 1) * 128], self.idn_f.ap()))
                self.rdone([f"cat{asl}"], tp)
                self.ps_wdone(pbank, 512, tp)
                self.ps_rd("dve", pbank, 512)
                E.ws("dve", [self.buf(f"cuo{sl}").w])
                tv = E.i("dve", lambda e, _u=uo, _e=e_, _pb=pbank: e.tensor_add(
                    _u.rearrange("p (j c) -> p j c", j=4)[:, :,
                                                          _e * 128:(_e + 1) * 128],
                    _u.rearrange("p (j c) -> p j c", j=4)[:, :,
                                                          _e * 128:(_e + 1) * 128],
                    ps[:, _pb:_pb + 512].rearrange("p (j c) -> p j c", c=128)))
                self.ps_rdone(pbank, 512, tv)
                self.buf(f"cuo{sl}").w = tv
            tso = E.dma("sy", lambda e, _u=uo, _b=b, _ch=ch: e.dma_start(
                out=self.OUT[_b, _ch * CC:(_ch + 1) * CC, :]
                .rearrange("(j p) c -> p j c", p=128),
                in_=_u.rearrange("p (j c) -> p j c", j=4)),
                waits=[self.buf(f"cuo{sl}").w], slot=f"cuo{sl}")
            self.rdone([f"cuo{sl}"], tso)


def _ctx():
    import contextlib
    return contextlib.nullcontext()


_NC_CACHE = {}


def build_program():
    if "nc" not in _NC_CACHE:
        _NC_CACHE["nc"] = K().build()
    return _NC_CACHE["nc"]


def kernel(**inputs):
    u = np.ascontiguousarray(np.asarray(inputs["L0_attn_u"]))
    wrt = np.ascontiguousarray(np.asarray(inputs["L0_ffn_w_router"]).T)
    idn = np.eye(128, dtype=np.float32)
    ones = np.zeros((32, 128), dtype=np.float32)
    ones[0, :] = 1.0
    psel = np.zeros((32, 16 * 64), dtype=np.float32)
    for n in range(16):
        psel[n, n * 64:(n + 1) * 64] = 1.0
    osel = np.zeros((32, 16 * 128), dtype=np.float32)
    for n in range(16):
        osel[n, n * 128:(n + 1) * 128] = 1.0

    common = {
        "WIN": np.ascontiguousarray(inputs["L0_attn_Win"]),
        "BIN": np.ascontiguousarray(inputs["L0_attn_b_in"]),
        "WG": np.ascontiguousarray(inputs["L0_attn_Wg"]),
        "BG": np.ascontiguousarray(inputs["L0_attn_bg"]),
        "WOUT": np.ascontiguousarray(inputs["L0_attn_Wout"]),
        "BOUT": np.ascontiguousarray(inputs["L0_attn_bout"]),
        "CONVW": np.ascontiguousarray(inputs["L0_attn_convw"]),
        "CONVB": np.ascontiguousarray(inputs["L0_attn_convb"]),
        "AMAT": np.ascontiguousarray(inputs["L0_attn_A"]),
        "WB": np.ascontiguousarray(inputs["L0_attn_Wb"]),
        "BB": np.ascontiguousarray(inputs["L0_attn_bb"]),
        "WC": np.ascontiguousarray(inputs["L0_attn_Wc"]),
        "BCC": np.ascontiguousarray(inputs["L0_attn_bc"]),
        "WDT": np.ascontiguousarray(inputs["L0_attn_Wdt"]),
        "BDT": np.ascontiguousarray(inputs["L0_attn_bdt"]),
        "WRT": wrt,
        "CEN": np.ascontiguousarray(inputs["L0_ffn_head_centroids"]),
        "TEMP": np.ascontiguousarray(np.repeat(np.asarray(inputs["L0_ffn_temperature"]).reshape(1, 1), 128, axis=0)),
        "IEXP": np.ascontiguousarray(inputs["L0_ffn_input_experts"]),
        "OEXP": np.ascontiguousarray(inputs["L0_ffn_output_experts"]),
        "IDN": idn,
        "ONESR": ones,
        "PSEL": psel,
        "OSEL": osel,
    }
    in_maps = []
    for c in range(NCORE):
        m = dict(common)
        m["U"] = np.ascontiguousarray(u[2 * c:2 * c + 2])
        in_maps.append(m)

    nc = build_program()
    res = run_bass_kernel_spmd(nc, in_maps, core_ids=list(range(NCORE)))
    out = np.concatenate([r["OUT"] for r in res.results], axis=0)
    return out.astype(np.float32)


if __name__ == "__main__":
    nc = build_program()
    print("built ok")
